# version 03
# baseline (speedup 1.0000x reference)
import numpy as np
import ml_dtypes
from contextlib import ExitStack
import concourse.bass as bass
import concourse.mybir as mybir
from concourse.bass_utils import run_bass_kernel_spmd



"""Launch builders for mk_seq_preln_d2 (dev copy; later inlined into kernel.py)."""

F32 = mybir.dt.float32
BF16 = mybir.dt.bfloat16
AF = mybir.ActivationFunctionType
ALU = mybir.AluOpType
AX = mybir.AxisListType

P = 128
D = 2048
T = 8192
NCORES = 8
TPC = T // NCORES
DCH = D // P
TS = TPC // P
NH = 16
HD = 128
HPC = NH // NCORES
LN_EPS = 1e-5
RMS_EPS = float(np.finfo(np.float32).eps)
SCALE0 = 1.0 / float(np.sqrt(128.0))
SCALE1 = 1.0 / float(np.sqrt(192.0))
NEG = -1e30
ROLES = ("gpsimd", "sync", "tensor", "vector", "scalar")


class Sem:
    def __init__(self, h):
        self.h = h
        self.n = 0
        self.base = 0


class SemView:
    def __init__(self, phys):
        self.phys = phys
        self.base = phys.n
        self.h = phys.h

    @property
    def n(self):
        return self.phys.n - self.base

    @n.setter
    def n(self, val):
        self.phys.n = self.base + val


class Em:
    def __init__(self, role, eng, sems):
        self.role = role
        self.eng = eng
        self.sems = sems
        self.pre = {}

    def set_pre(self, waits):
        for e in ROLES:
            self.pre[e] = list(waits)

    def do(self, engine, fn, waits=(), inc=None, k=1):
        pw = self.pre.pop(engine, ())
        if engine == self.role:
            for (s, v) in list(pw) + list(waits):
                if v > 0:
                    self.eng.wait_ge(s.h, getattr(s, "base", 0) + v)
            ins = fn(self.eng)
            if inc is not None:
                ins.then_inc(inc.h, k)
        if inc is not None:
            inc.n += k

    def wait(self, engine, waits):
        if engine == self.role:
            for (s, v) in waits:
                if v > 0:
                    self.eng.wait_ge(s.h, getattr(s, "base", 0) + v)


def run_walk(nc, block, sems, walk):
    import functools
    for role in ROLES:
        for s in sems.values():
            s.n = 0
        def body(eng, role=role):
            walk(Em(role, eng, sems))
        getattr(block, role)(body)


L0PRE_SEMS = ["ln_sqd", "ln_mmst", "ln_rowd", "ln_rstd", "ln_bcd", "ln_nrm",
              "ld", "wql0", "wql1", "mmq", "roped", "qout0", "qout1",
              "mmkv", "kvd", "kvout0", "kvout1", "fin", "nout",
              "trq", "tqcp", "trk", "tkcp"]


def sec_l0pre(nc, E, dram, shared=None, fused=False):
    xTd = dram["xT"]
    wqd = dram["wq"]
    wkvd = dram["wkv"]
    cosd = dram["cos4"]
    sind = dram["sin4"]
    qroped = dram["qrope"]
    ktd = dram["kt"]
    vtd = dram["vt"]
    qn2d = dram["qn2"]
    kn2d = dram["kn2"]
    if True:
        if shared is None:
            xT = E(nc.sbuf_tensor("xTs", [P, DCH, TPC], F32))
            xnT = E(nc.sbuf_tensor("xnT", [P, DCH, TPC], BF16))
        else:
            xT = shared["xT"]
            xnT = shared["xnT"]
        wqb = E(nc.sbuf_tensor("wqb", [P, 2, DCH, 512], BF16))
        wkvb = E(nc.sbuf_tensor("wkvb", [P, DCH, 256], BF16))
        cosb = E(nc.sbuf_tensor("cos", [P, TS, 512], F32))
        sinb = E(nc.sbuf_tensor("sin", [P, TS, 512], F32))
        shuf = E(nc.sbuf_tensor("shuf", [P, 512], F32))
        tmpA = E(nc.sbuf_tensor("tmpA", [P, 512], F32))
        tmpB = E(nc.sbuf_tensor("tmpB", [P, 512], F32))
        qtm = E(nc.sbuf_tensor("qtm", [P, 2, 512], BF16))
        sqscr = E(nc.sbuf_tensor("sqscr", [P, 512], F32))
        qn2b = E(nc.sbuf_tensor("qn2s", [P, TS, NH], F32))
        kn2b = E(nc.sbuf_tensor("kn2s", [P, TS, 1], F32))
        ktm = E(nc.sbuf_tensor("ktm", [P, 2, HD], BF16))
        vtm = E(nc.sbuf_tensor("vtm", [P, 2, HD], BF16))
        qpsA = E(nc.psum_tensor("qpsA", [P, 512], F32))
        qpsB = E(nc.psum_tensor("qpsB", [P, 512], F32))
        kvps = E(nc.psum_tensor("kvps", [P, 256], F32))
        lnaux = mk_ln_aux(nc, E)
        if fused:
            identq = E(nc.sbuf_tensor("identq", [P, P], BF16))
            qfm = E(nc.sbuf_tensor("qfm", [P, 2, P], BF16))
            kfm = E(nc.sbuf_tensor("kfm", [P, 2, P], BF16))
            tpsq = E(nc.psum_tensor("tpsq", [P, P], BF16))

        def walk(em, S):
            qps = (qpsA, qpsB)
            em.do("gpsimd", lambda g: g.dma_start(
                out=xT[:], in_=xTd.rearrange("(c p) n -> p c n", p=P)), inc=S["ld"], k=16)
            em.do("gpsimd", lambda g: g.dma_start(
                out=wkvb[:], in_=wkvd.rearrange("(c p) n -> p c n", p=P)), inc=S["ld"], k=16)
            em.do("gpsimd", lambda g: g.dma_start(
                out=cosb[:], in_=cosd.rearrange("(s p) n -> p s n", p=P)), inc=S["ld"], k=16)
            em.do("gpsimd", lambda g: g.dma_start(
                out=sinb[:], in_=sind.rearrange("(s p) n -> p s n", p=P)), inc=S["ld"], k=16)
            if fused:
                em.do("gpsimd", lambda g: g.dma_start(
                    out=identq[:], in_=dram["ident"]), inc=S["ld"], k=16)
            LDA = S["ld"].n
            em.do("vector", lambda v: v.memset(lnaux["ones_col"][:], 1.0))
            em.do("vector", lambda v: v.memset(lnaux["ones_row"][:], 1.0))
            em.do("vector", lambda v: v.memset(lnaux["eps_b"][:], LN_EPS), inc=S["fin"])

            emit_ln(em, S, xT, xnT, lnaux,
                    pre_waits=[(S["ld"], LDA), (S["fin"], 1)])

            nchunks = 0
            for fc in range(4):
                em.do("gpsimd", lambda g, fc=fc: g.dma_start(
                    out=wqb[:, fc % 2, :, :],
                    in_=wqd.rearrange("(c p) f -> p c f", p=P)[:, :, fc * 512:(fc + 1) * 512]),
                    waits=[(S["mmq"], max(0, 8 * (fc - 1)))],
                    inc=S["wql%d" % (fc % 2)], k=16)
                for ts in range(TS):
                    idx = fc * TS + ts
                    tsl = slice(ts * P, (ts + 1) * P)
                    ps = qps[idx % 2]
                    def mmq(t, fc=fc, ts=ts, tsl=tsl, ps=ps):
                        last = None
                        for c in range(DCH):
                            last = t.matmul(ps[:], xnT[:, c, tsl], wqb[:, fc % 2, c, :],
                                            start=(c == 0), stop=(c == DCH - 1),
                                            skip_group_check=True)
                        return last
                    em.do("tensor", mmq,
                          waits=[(S["wql%d" % (fc % 2)], 16 * (fc // 2 + 1)), (S["ln_nrm"], 32),
                                 (S["roped"], max(0, idx - 1))],
                          inc=S["mmq"])
                    em.do("vector", lambda v, ps=ps: v.tensor_copy(
                        shuf[:].rearrange("p (h a b) -> p h a b", a=2, b=64)[:, :, 0, :],
                        ps[:].rearrange("p (h a b) -> p h a b", a=2, b=64)[:, :, 1, :]),
                        waits=[(S["mmq"], idx + 1)] +
                              ([(S["trq"], max(0, 4 * (idx - 1)))] if fused else
                               [(S["qout%d" % (idx % 2)], 16 * (idx // 2))]))
                    em.do("vector", lambda v, ps=ps: v.tensor_copy(
                        shuf[:].rearrange("p (h a b) -> p h a b", a=2, b=64)[:, :, 1, :],
                        ps[:].rearrange("p (h a b) -> p h a b", a=2, b=64)[:, :, 0, :]))
                    em.do("vector", lambda v, ps=ps, ts=ts: v.tensor_mul(
                        tmpA[:], ps[:], cosb[:, ts, :]))
                    em.do("vector", lambda v, ts=ts: v.tensor_mul(
                        tmpB[:], shuf[:], sinb[:, ts, :]))
                    em.do("vector", lambda v, idx=idx: v.tensor_add(
                        qtm[:, idx % 2, :], tmpA[:], tmpB[:]))
                    em.do("vector", lambda v, idx=idx: v.tensor_mul(
                        sqscr[:], qtm[:, idx % 2, :], qtm[:, idx % 2, :]))
                    em.do("vector", lambda v, ts=ts, fc=fc: v.reduce_sum(
                        qn2b[:, ts, fc * 4:(fc + 1) * 4],
                        sqscr[:].rearrange("p (h f) -> p h f", f=P), axis=AX.X),
                        inc=S["roped"])
                    if not fused:
                        em.do("gpsimd", lambda g, idx=idx, ts=ts, fc=fc: g.dma_start(
                            out=qroped[ts * P:(ts + 1) * P, fc * 512:(fc + 1) * 512],
                            in_=qtm[:, idx % 2, :]),
                            waits=[(S["roped"], idx + 1)], inc=S["qout%d" % (idx % 2)], k=16)
                    else:
                        for j in range(4):
                            qi = idx * 4 + j
                            em.do("tensor", lambda t, idx=idx, j=j: t.transpose(
                                tpsq[:], qtm[:, idx % 2, j * P:(j + 1) * P], identq[:]),
                                waits=[(S["roped"], idx + 1),
                                       (S["tqcp"], max(0, qi))],
                                inc=S["trq"])
                            em.do("vector", lambda v, qi=qi: v.tensor_copy(
                                qfm[:, qi % 2, :], tpsq[:]),
                                waits=[(S["trq"], S["trq"].n),
                                       (S["qout%d" % (qi % 2)], 16 * (qi // 2))],
                                inc=S["tqcp"])
                            em.do("gpsimd", lambda g, fc=fc, j=j, ts=ts, qi=qi: g.dma_start(
                                out=qroped[fc * 512 + j * P:fc * 512 + (j + 1) * P,
                                           ts * P:(ts + 1) * P],
                                in_=qfm[:, qi % 2, :]),
                                waits=[(S["tqcp"], S["tqcp"].n)],
                                inc=S["qout%d" % (qi % 2)], k=16)
                    nchunks += 1

            for ts in range(TS):
                tsl = slice(ts * P, (ts + 1) * P)
                def mmkv(t, tsl=tsl):
                    last = None
                    for c in range(DCH):
                        last = t.matmul(kvps[:], xnT[:, c, tsl], wkvb[:, c, :],
                                        start=(c == 0), stop=(c == DCH - 1),
                                        skip_group_check=True)
                    return last
                em.do("tensor", mmkv,
                      waits=[(S["ld"], LDA), (S["ln_nrm"], 32),
                             (S["kvd"], max(0, ts))],
                      inc=S["mmkv"])
                em.do("vector", lambda v: v.tensor_copy(
                    shuf[:, 0:64], kvps[:, 64:128]),
                    waits=[(S["mmkv"], ts + 1),
                           (S["kvout%d" % (ts % 2)], 32 * (ts // 2))] +
                          ([(S["trk"], max(0, ts - 1))] if fused else []))
                em.do("vector", lambda v: v.tensor_copy(
                    shuf[:, 64:128], kvps[:, 0:64]))
                em.do("vector", lambda v, ts=ts: v.tensor_mul(
                    tmpA[:, 0:128], kvps[:, 0:128], cosb[:, ts, 0:128]))
                em.do("vector", lambda v, ts=ts: v.tensor_mul(
                    tmpB[:, 0:128], shuf[:, 0:128], sinb[:, ts, 0:128]))
                em.do("vector", lambda v, ts=ts: v.tensor_add(
                    ktm[:, ts % 2, :], tmpA[:, 0:128], tmpB[:, 0:128]))
                em.do("vector", lambda v, ts=ts: v.tensor_copy(
                    vtm[:, ts % 2, :], kvps[:, 128:256]))
                em.do("vector", lambda v, ts=ts: v.tensor_mul(
                    sqscr[:, 0:128], ktm[:, ts % 2, :], ktm[:, ts % 2, :]))
                em.do("vector", lambda v, ts=ts: v.reduce_sum(
                    kn2b[:, ts, :], sqscr[:, 0:128], axis=AX.X), inc=S["kvd"])
                if not fused:
                    em.do("gpsimd", lambda g, ts=ts: g.dma_start(
                        out=ktd[ts * P:(ts + 1) * P, :], in_=ktm[:, ts % 2, :]),
                        waits=[(S["kvd"], ts + 1)], inc=S["kvout%d" % (ts % 2)], k=16)
                else:
                    em.do("tensor", lambda t, ts=ts: t.transpose(
                        tpsq[:], ktm[:, ts % 2, :], identq[:]),
                        waits=[(S["kvd"], ts + 1), (S["tkcp"], max(0, ts))] +
                              ([(S["tqcp"], S["tqcp"].n)] if ts == 0 else []),
                        inc=S["trk"])
                    em.do("vector", lambda v, ts=ts: v.tensor_copy(
                        kfm[:, ts % 2, :], tpsq[:]),
                        waits=[(S["trk"], S["trk"].n),
                               (S["kvout%d" % (ts % 2)], 32 * (ts // 2))],
                        inc=S["tkcp"])
                    em.do("gpsimd", lambda g, ts=ts: g.dma_start(
                        out=ktd[0:HD, ts * P:(ts + 1) * P], in_=kfm[:, ts % 2, :]),
                        waits=[(S["tkcp"], S["tkcp"].n)],
                        inc=S["kvout%d" % (ts % 2)], k=16)
                em.do("gpsimd", lambda g, ts=ts: g.dma_start(
                    out=vtd[ts * P:(ts + 1) * P, :], in_=vtm[:, ts % 2, :]),
                    inc=S["kvout%d" % (ts % 2)], k=16)

            if not fused:
                em.do("gpsimd", lambda g: g.dma_start(
                    out=qn2d, in_=qn2b[:]),
                    waits=[(S["roped"], 32)], inc=S["nout"], k=16)
            else:
                for s_ in range(TS):
                    em.do("gpsimd", lambda g, s_=s_: g.dma_start(
                        out=qn2d.rearrange("h (s p) -> s p h", p=P)[s_],
                        in_=qn2b[:, s_, :], allow_slow_non_contiguous=True),
                        waits=[(S["roped"], 32)], inc=S["nout"], k=16)
            em.do("gpsimd", lambda g: g.dma_start(
                out=(kn2d.rearrange("(s p) o -> p s o", p=P) if fused else kn2d),
                in_=(kn2b[:] if fused else kn2b[:, :, 0]),
                allow_slow_non_contiguous=fused),
                waits=[(S["kvd"], 8)], inc=S["nout"], k=16)
        def done(S):
            return [(S["qout0"], S["qout0"].n), (S["qout1"], S["qout1"].n),
                    (S["kvout0"], S["kvout0"].n), (S["kvout1"], S["kvout1"].n),
                    (S["nout"], S["nout"].n)]
        return walk, done


def build_l0pre():
    nc = bass.Bass("TRN2", target_bir_lowering=False, debug=True, detect_race_conditions=False)
    dram = {
        "xT": nc.dram_tensor("xT", [D, TPC], F32, kind="ExternalInput").ap(),
        "wq": nc.dram_tensor("wq", [D, D], F32, kind="ExternalInput").ap(),
        "wkv": nc.dram_tensor("wkv", [D, 256], F32, kind="ExternalInput").ap(),
        "cos4": nc.dram_tensor("cos4", [TPC, 512], F32, kind="ExternalInput").ap(),
        "sin4": nc.dram_tensor("sin4", [TPC, 512], F32, kind="ExternalInput").ap(),
        "qrope": nc.dram_tensor("qrope", [TPC, D], BF16, kind="ExternalOutput").ap(),
        "kt": nc.dram_tensor("kt", [TPC, HD], BF16, kind="ExternalOutput").ap(),
        "vt": nc.dram_tensor("vt", [TPC, HD], BF16, kind="ExternalOutput").ap(),
        "qn2": nc.dram_tensor("qn2", [P, TS, NH], F32, kind="ExternalOutput").ap(),
        "kn2": nc.dram_tensor("kn2", [P, TS], F32, kind="ExternalOutput").ap(),
    }
    with ExitStack() as _ctx:
        walk, done = sec_l0pre(nc, _ctx.enter_context, dram)
        block = _ctx.enter_context(nc.Block())
        sems = {n: Sem(nc.alloc_semaphore(n)) for n in L0PRE_SEMS}

        def top(em):
            walk(em, sems)
            em.wait("gpsimd", done(sems))

        run_walk(nc, block, sems, top)
    return nc


def host_rope_tables(n, dim, rep):
    inv = 1.0 / (10000.0 ** (np.arange(0, dim, 2, dtype=np.float32) / np.float32(dim)))
    t = np.arange(n, dtype=np.float32)
    f = np.outer(t, inv).astype(np.float32)
    pos = np.concatenate([f, f], axis=1)
    cos = np.cos(pos).astype(np.float32)
    sin = np.sin(pos).astype(np.float32)
    half = dim // 2
    sin_signed = sin.copy()
    sin_signed[:, :half] = -sin[:, half:]
    sin_signed = sin.copy()
    sin_signed[:, :half] *= -1.0
    cosr = np.tile(cos, (1, rep))
    sinr = np.tile(sin_signed, (1, rep))
    return np.ascontiguousarray(cosr), np.ascontiguousarray(sinr)



def emit_ln(em, S, xT, xnT, aux, pre_waits=(), tag=""):
    a = aux
    first = {"scalar": True, "tensor": True, "vector": True}
    def pw(eng):
        if first[eng]:
            first[eng] = False
            return list(pre_waits)
        return []
    for th in range(2):
        cols = slice(th * 512, (th + 1) * 512)
        for c in range(DCH):
            idx = th * DCH + c
            em.do("scalar", lambda s, c=c, cols=cols, idx=idx: s.activation(
                a["sq"][:, idx % 2, :], xT[:, c, cols], AF.Square),
                waits=pw("scalar") + [(S["ln_mmst"], max(0, 2 * (idx - 1)))],
                inc=S["ln_sqd"])
            em.do("tensor", lambda t, c=c, cols=cols: t.matmul(
                a["ssum"][:], a["ones_col"][:], xT[:, c, cols],
                start=(c == 0), stop=(c == DCH - 1), skip_group_check=True),
                waits=pw("tensor") + [(S["ln_rowd"], 2 * th)],
                inc=S["ln_mmst"])
            em.do("tensor", lambda t, c=c, idx=idx: t.matmul(
                a["ssq"][:], a["ones_col"][:], a["sq"][:, idx % 2, :],
                start=(c == 0), stop=(c == DCH - 1), skip_group_check=True),
                waits=[(S["ln_sqd"], idx + 1)], inc=S["ln_mmst"])
        em.do("vector", lambda v: v.tensor_single_scalar(
            a["mrow"][:], a["ssum"][:], 1.0 / D, ALU.mult),
            waits=pw("vector") + [(S["ln_mmst"], 32 * (th + 1))], inc=S["ln_rowd"])
        em.do("vector", lambda v: v.tensor_mul(a["m2row"][:], a["mrow"][:], a["mrow"][:]))
        em.do("vector", lambda v: v.tensor_single_scalar(
            a["vrow"][:], a["ssq"][:], 1.0 / D, ALU.mult))
        em.do("vector", lambda v: v.tensor_sub(a["vrow"][:], a["vrow"][:], a["m2row"][:]),
              inc=S["ln_rowd"])
        em.do("scalar", lambda s: s.activation(
            a["vrow"][:], a["vrow"][:], AF.Ln, bias=a["eps_b"][:]),
            waits=[(S["ln_rowd"], 2 * (th + 1))])
        em.do("scalar", lambda s: s.activation(
            a["rrow"][:], a["vrow"][:], AF.Exp, scale=-0.5), inc=S["ln_rstd"])
        em.do("tensor", lambda t: t.matmul(
            a["bcm"][:], a["ones_row"][:], a["mrow"][:], start=True, stop=True,
            skip_group_check=True),
            waits=[(S["ln_rstd"], th + 1), (S["ln_nrm"], 16 * th)], inc=S["ln_bcd"])
        em.do("tensor", lambda t: t.matmul(
            a["bcr"][:], a["ones_row"][:], a["rrow"][:], start=True, stop=True,
            skip_group_check=True), inc=S["ln_bcd"])
        for c in range(DCH):
            em.do("vector", lambda v, c=c, cols=cols: v.tensor_sub(
                a["tmpA"][:], xT[:, c, cols], a["bcm"][:]),
                waits=[(S["ln_bcd"], 2 * (th + 1))])
            em.do("vector", lambda v, c=c, cols=cols: v.tensor_mul(
                xnT[:, c, cols], a["tmpA"][:], a["bcr"][:]), inc=S["ln_nrm"])


LN_SEMS = ["ln_sqd", "ln_mmst", "ln_rowd", "ln_rstd", "ln_bcd", "ln_nrm"]


def mk_ln_aux(nc, E):
    d = _mk_ln_aux_inner(nc, E)
    d["m2row"] = d["rrow"]
    return d


def _mk_ln_aux_inner(nc, E):
    return {
        "ones_col": E(nc.sbuf_tensor("lnx_ones_col", [P, 1], F32)),
        "ones_row": E(nc.sbuf_tensor("lnx_ones_row", [1, P], F32)),
        "eps_b": E(nc.sbuf_tensor("lnx_eps", [1, 1], F32)),
        "sq": E(nc.sbuf_tensor("lnx_sq", [P, 2, 512], F32)),
        "mrow": E(nc.sbuf_tensor("lnx_mrow", [1, 512], F32)),
        "vrow": E(nc.sbuf_tensor("lnx_vrow", [1, 512], F32)),
        "rrow": E(nc.sbuf_tensor("lnx_rrow", [1, 512], F32)),
        "tmpA": E(nc.sbuf_tensor("lnx_tmpA", [P, 512], F32)),
        "ssum": E(nc.psum_tensor("lnx_ssum", [1, 512], F32)),
        "ssq": E(nc.psum_tensor("lnx_ssq", [1, 512], F32)),
        "bcm": E(nc.psum_tensor("lnx_bcm", [P, 512], F32)),
        "bcr": E(nc.psum_tensor("lnx_bcr", [P, 512], F32)),
    }


def emit_ln_consts(em, S, aux):
    em.do("vector", lambda v: v.memset(aux["ones_col"][:], 1.0))
    em.do("vector", lambda v: v.memset(aux["ones_row"][:], 1.0))
    em.do("vector", lambda v: v.memset(aux["eps_b"][:], LN_EPS), inc=S["consts"])



ATTN_SEMS = ["at_pre", "at_qmr", "at_expd", "at_lavd", "at_biasd", "at_maskd",
             "at_lpostd", "at_bcd", "at_lbcd", "at_evacd", "at_smm"]


def emit_attn_core(em, S, tl, L1, kv_ready_waits=()):
    scale = SCALE1 if L1 else SCALE0
    NQT = T // 512
    sps = tl["sps"]
    avps = tl["avps"]
    lps, bcps, smallps = tl["lps"], tl["bcps"], tl["smallps"]
    qTh, kT, vt, qn2 = tl["qTh"], tl["kT"], tl["vt"], tl["qn2"]
    masks, eye32 = tl["masks"], tl["eye32"]
    pT, biascol, selrow = tl["pT"], tl["biascol"], tl["selrow"]
    linv, lbc = tl["linv"], tl["lbc"]
    qm, kbc, mneg = tl["qm"], tl["kbc"], tl["mneg"]
    ones_colb, ones_row, ones_r32 = tl["ones_colb"], tl["ones_row"], tl["ones_r32"]

    em.do("vector", lambda v: v.reduce_max(qm[:], qn2[:], axis=AX.X),
          waits=list(kv_ready_waits), inc=S["at_qmr"])
    em.do("scalar", lambda s: s.activation(tl["maxk2"][:], tl["maxk2"][:], AF.Sqrt),
          waits=list(kv_ready_waits), inc=S["at_pre"])
    em.do("scalar", lambda s: s.activation(qm[:], qm[:], AF.Sqrt),
          waits=[(S["at_qmr"], 1)], inc=S["at_pre"])
    em.do("tensor", lambda t: t.matmul(
        smallps[0:32, 0:1], ones_r32[:], tl["maxk2"][:],
        start=True, stop=True, skip_group_check=True),
        waits=[(S["at_pre"], S["at_pre"].n)], inc=S["at_smm"])
    em.do("vector", lambda v: v.tensor_copy(kbc[:], smallps[0:32, 0:1]),
          waits=[(S["at_smm"], S["at_smm"].n)])
    em.do("vector", lambda v: v.tensor_mul(mneg[:], qm[:], kbc[:]))
    em.do("vector", lambda v: v.tensor_single_scalar(
        mneg[:], mneg[:], -scale, ALU.mult), inc=S["at_pre"])
    PREALL = S["at_pre"].n

    gidx = 0
    hq = 0
    glast = {}
    for h in range(HPC):
        for qt in range(NQT):
            nb = qt * 4 + 4
            em.do("tensor", lambda t, h=h, qt=qt, hq=hq: t.matmul(
                smallps[0:1, 1:2], eye32[:, h * NQT + qt:h * NQT + qt + 1],
                mneg[:], start=True, stop=True, skip_group_check=True),
                waits=[(S["at_pre"], PREALL),
                       (S["at_biasd"], max(0, 2 * (hq - 1)))],
                inc=S["at_smm"])
            em.do("vector", lambda v, hq=hq: v.tensor_copy(
                selrow[:, hq % 2:hq % 2 + 1], smallps[0:1, 1:2]),
                waits=[(S["at_smm"], S["at_smm"].n),
                       (S["at_expd"], glast.get(hq - 2, 0))],
                inc=S["at_biasd"])
            em.do("tensor", lambda t, hq=hq: t.matmul(
                smallps[:, 2 + hq % 2:3 + hq % 2], ones_row[:],
                selrow[:, hq % 2:hq % 2 + 1],
                start=True, stop=True, skip_group_check=True),
                waits=[(S["at_biasd"], S["at_biasd"].n)], inc=S["at_smm"])
            em.do("vector", lambda v, hq=hq: v.tensor_copy(
                biascol[:, hq % 2:hq % 2 + 1], smallps[:, 2 + hq % 2:3 + hq % 2]),
                waits=[(S["at_smm"], S["at_smm"].n)], inc=S["at_biasd"])
            BIASRDY = S["at_biasd"].n
            for b in range(nb):
                par = gidx % 2
                qsl = slice(qt * 512, (qt + 1) * 512)
                ksl = slice(b * P, (b + 1) * P)
                def smm(t, h=h, qsl=qsl, ksl=ksl, par=par):
                    if L1:
                        t.matmul(sps[par][:], kT[:, h, ksl], qTh[:, h, qsl],
                                 start=True, stop=False, skip_group_check=True)
                        return t.matmul(sps[par][:], tl["krT"][0:64, ksl],
                                        tl["qrT"][0:64, qsl],
                                        start=False, stop=True, skip_group_check=True)
                    return t.matmul(sps[par][:], kT[:, 0, ksl], qTh[:, h, qsl],
                                    start=True, stop=True, skip_group_check=True)
                em.do("tensor", smm,
                      waits=[(S["at_expd"], max(0, gidx - 1))] +
                            (list(kv_ready_waits) if gidx == 0 else []),
                      inc=S["at_smm"])
                SMMN = S["at_smm"].n
                is_diag = b >= qt * 4
                if is_diag:
                    em.do("vector", lambda v, b=b, qt=qt, par=par: v.tensor_add(
                        sps[par][:], sps[par][:], masks[:, b - qt * 4, :]),
                        waits=[(S["at_smm"], SMMN)], inc=S["at_maskd"])
                MASKN = S["at_maskd"].n
                em.do("scalar", lambda s, par=par, hq=hq: s.activation(
                    pT[:, par, :], sps[par][:], AF.Exp,
                    bias=biascol[:, hq % 2:hq % 2 + 1], scale=scale),
                    waits=[(S["at_smm"], SMMN), (S["at_maskd"], MASKN),
                           (S["at_biasd"], BIASRDY),
                           (S["at_lavd"], max(0, 2 * (gidx - 1)))],
                    inc=S["at_expd"])
                em.do("tensor", lambda t, par=par, hq=hq, b=b, nb=nb: t.matmul(
                    lps[0:1, (hq % 2) * 512:(hq % 2 + 1) * 512],
                    ones_colb[:], pT[:, par, :],
                    start=(b == 0), stop=(b == nb - 1), skip_group_check=True),
                    waits=[(S["at_expd"], gidx + 1)] +
                          ([(S["at_lpostd"], max(0, hq - 1)),
                            (S["at_evacd"], max(0, hq - 1))] if b == 0 else []),
                    inc=S["at_lavd"])
                def avmm(t, par=par, hq=hq, b=b, nb=nb, h=h):
                    vsl = vt[:, b, h * HD:(h + 1) * HD] if L1 else vt[:, b, :]
                    return t.matmul(avps[hq % 2][:], vsl, pT[:, par, :],
                                    start=(b == 0), stop=(b == nb - 1),
                                    skip_group_check=True)
                em.do("tensor", avmm, inc=S["at_lavd"])
                gidx += 1
            glast[hq] = gidx
            em.do("vector", lambda v, hq=hq: v.reciprocal(
                linv[:, hq % 2, :], lps[0:1, (hq % 2) * 512:(hq % 2 + 1) * 512]),
                waits=[(S["at_lavd"], 2 * gidx)], inc=S["at_lpostd"])
            em.do("tensor", lambda t, hq=hq: t.matmul(
                bcps[:], ones_row[:], linv[:, hq % 2, :],
                start=True, stop=True, skip_group_check=True),
                waits=[(S["at_lpostd"], hq + 1), (S["at_lbcd"], max(0, hq))],
                inc=S["at_smm"])
            BCN = S["at_smm"].n
            em.do("vector", lambda v, hq=hq: v.tensor_copy(
                lbc[:, hq % 2, :], bcps[:]),
                waits=[(S["at_smm"], BCN)], inc=S["at_lbcd"])
            em.do("vector", lambda v, hq=hq, h=h, qt=qt: v.tensor_mul(
                tl["attnout"][:, h, qt * 512:(qt + 1) * 512], avps[hq % 2][:],
                lbc[:, hq % 2, :]),
                waits=[(S["at_lbcd"], hq + 1)], inc=S["at_evacd"])
            hq += 1


def mk_attn_tiles(nc, E, L1):
    HK = HPC if L1 else 1
    NKB = T // P
    tl = {}
    tl["qTh"] = E(nc.sbuf_tensor("at_qTh", [P, HPC, T], BF16))
    tl["kT"] = E(nc.sbuf_tensor("at_kT", [P, HK, T], BF16))
    tl["vt"] = E(nc.sbuf_tensor("at_vt", [P, NKB, HK * HD], BF16))
    tl["qn2"] = E(nc.sbuf_tensor("at_qn2", [32, 512], F32))
    tl["masks"] = E(nc.sbuf_tensor("at_masks", [P, 4, 512], F32))
    tl["eye32"] = E(nc.sbuf_tensor("at_eye32", [32, 32], F32))
    tl["pT"] = E(nc.sbuf_tensor("at_pT", [P, 2, 512], BF16))
    tl["ones_colb"] = E(nc.sbuf_tensor("at_onescb", [P, 1], BF16))
    tl["ones_row"] = E(nc.sbuf_tensor("at_onesrf", [1, P], F32))
    tl["ones_r32"] = E(nc.sbuf_tensor("at_ones32", [1, 32], F32))
    tl["maxk2"] = E(nc.sbuf_tensor("at_maxk2", [1, 1], F32))
    tl["qm"] = E(nc.sbuf_tensor("at_qm", [32, 1], F32))
    tl["kbc"] = E(nc.sbuf_tensor("at_kbc", [32, 1], F32))
    tl["mneg"] = E(nc.sbuf_tensor("at_mneg", [32, 1], F32))
    tl["selrow"] = E(nc.sbuf_tensor("at_selrow", [1, 2], F32))
    tl["biascol"] = E(nc.sbuf_tensor("at_biascol", [P, 2], F32))
    tl["linv"] = E(nc.sbuf_tensor("at_linv", [1, 2, 512], F32))
    tl["lbc"] = E(nc.sbuf_tensor("at_lbc", [P, 2, 512], F32))
    tl["sps"] = (E(nc.psum_tensor("at_sps0", [P, 512], F32)),
                 E(nc.psum_tensor("at_sps1", [P, 512], F32)))
    tl["avps"] = (E(nc.psum_tensor("at_avps0", [P, 512], F32)),
                  E(nc.psum_tensor("at_avps1", [P, 512], F32)))
    tl["lps"] = E(nc.psum_tensor("at_lps", [1, 1024], F32))
    tl["bcps"] = E(nc.psum_tensor("at_bcps", [P, 512], F32))
    tl["smallps"] = E(nc.psum_tensor("at_smallps", [P, 4], F32))
    return tl


ATTN0_SEMS = ATTN_SEMS + ["ld", "sout"]


def sec_attn0(nc, E, dram, qn2_rows=None):
    tl = mk_attn_tiles(nc, E, L1=False)
    kn2 = E(nc.sbuf_tensor("at_kn2", [1, T], F32))
    attnout = E(nc.sbuf_tensor("at_out", [P, HPC, T], BF16))
    tl["attnout"] = attnout

    def walk(em, S):
        def dma_in(out, in_):
            em.do("gpsimd", lambda g, out=out, in_=in_: g.dma_start(out=out, in_=in_),
                  inc=S["ld"], k=16)
        if isinstance(dram["qTh_src"], list):
            for s, ap in enumerate(dram["qTh_src"]):
                dma_in(tl["qTh"][:, :, s * TPC:(s + 1) * TPC], ap)
        else:
            dma_in(tl["qTh"][:], dram["qTh_src"])
        if isinstance(dram["kT_src"], list):
            for s, ap in enumerate(dram["kT_src"]):
                dma_in(tl["kT"][:, 0, s * TPC:(s + 1) * TPC], ap)
        else:
            dma_in(tl["kT"][:, 0, :], dram["kT_src"])
        dma_in(tl["vt"][:], dram["vt_src"])
        if qn2_rows is None:
            dma_in(tl["qn2"][:], dram["qn2_src"])
        else:
            for h in range(HPC):
                for qt in range(16):
                    r = h * 16 + qt
                    dma_in(tl["qn2"][r:r + 1, :], qn2_rows(h, qt))
        if isinstance(dram["kn2_src"], list):
            for s, ap in enumerate(dram["kn2_src"]):
                dma_in(kn2[0:1, s * TPC:(s + 1) * TPC], ap)
        else:
            dma_in(kn2[:], dram["kn2_src"])
        dma_in(tl["masks"][:], dram["masks"])
        dma_in(tl["eye32"][:], dram["eye32"])
        LDALL = S["ld"].n
        em.do("vector", lambda v: v.memset(tl["ones_colb"][:], 1.0))
        em.do("vector", lambda v: v.memset(tl["ones_row"][:], 1.0))
        em.do("vector", lambda v: v.memset(tl["ones_r32"][:], 1.0))
        em.do("vector", lambda v: v.reduce_max(tl["maxk2"][:], kn2[:], axis=AX.X),
              waits=[(S["ld"], LDALL)], inc=S["at_pre"])
        emit_attn_core(em, S, tl, L1=False,
                       kv_ready_waits=[(S["at_pre"], 1), (S["ld"], LDALL)])
        if isinstance(dram["attnout_dst"], list):
            for s, ap in enumerate(dram["attnout_dst"]):
                em.do("gpsimd", lambda g, s=s, ap=ap: g.dma_start(
                    out=ap, in_=attnout[:, :, s * TPC:(s + 1) * TPC]),
                    waits=[(S["at_evacd"], HPC * 16)], inc=S["sout"], k=16)
        else:
            em.do("gpsimd", lambda g: g.dma_start(
                out=dram["attnout_dst"], in_=attnout[:]),
                waits=[(S["at_evacd"], HPC * 16)], inc=S["sout"], k=16)

    def done(S):
        return [(S["sout"], S["sout"].n)]
    return walk, done


def build_attn(layer):
    assert layer == 0
    nc = bass.Bass("TRN2", target_bir_lowering=False, debug=True,
                   detect_race_conditions=False)
    qThd = nc.dram_tensor("qTh", [HPC * HD, T], BF16, kind="ExternalInput").ap()
    kTd = nc.dram_tensor("kT", [HD, T], BF16, kind="ExternalInput").ap()
    vtd = nc.dram_tensor("vt", [T, HD], BF16, kind="ExternalInput").ap()
    qn2d = nc.dram_tensor("qn2", [32, 512], F32, kind="ExternalInput").ap()
    kn2d = nc.dram_tensor("kn2", [1, T], F32, kind="ExternalInput").ap()
    maskd = nc.dram_tensor("masks", [P, 4, 512], F32, kind="ExternalInput").ap()
    eyed = nc.dram_tensor("eye32", [32, 32], F32, kind="ExternalInput").ap()
    outd = nc.dram_tensor("attnout", [HPC * HD, T], BF16, kind="ExternalOutput").ap()
    dram = {
        "qTh_src": qThd.rearrange("(h p) t -> p h t", p=P),
        "kT_src": kTd,
        "vt_src": vtd.rearrange("(b p) f -> p b f", p=P),
        "qn2_src": qn2d,
        "kn2_src": kn2d,
        "masks": maskd,
        "eye32": eyed,
        "attnout_dst": outd.rearrange("(h p) t -> p h t", p=P),
    }
    with ExitStack() as _ctx:
        walk, done = sec_attn0(nc, _ctx.enter_context, dram)
        block = _ctx.enter_context(nc.Block())
        sems = {n: Sem(nc.alloc_semaphore(n)) for n in ATTN0_SEMS}

        def top(em):
            walk(em, sems)
            em.wait("gpsimd", done(sems))

        run_walk(nc, block, sems, top)
    return nc




POST_SEMS = LN_SEMS + ["ld", "wl0", "wl1", "ymm", "resd", "xout", "nout"]


def sec_post(nc, E, dram, shared=None, load_x=True):
    aoTd = dram["aoT"]
    woutd = dram["wout"]
    if True:
        if shared is None:
            xT = E(nc.sbuf_tensor("xTs", [P, DCH, TPC], F32))
            xnT = E(nc.sbuf_tensor("xnTs", [P, DCH, TPC], BF16))
        else:
            xT = shared["xT"]
            xnT = shared["xnT"]
        aoT = E(nc.sbuf_tensor("aoTs", [P, DCH, TPC], BF16))
        wblk = E(nc.sbuf_tensor("wblks", [P, 2, DCH, P], BF16))
        yps0 = E(nc.psum_tensor("yps0", [P, 512], F32))
        yps1 = E(nc.psum_tensor("yps1", [P, 512], F32))
        lnaux = mk_ln_aux(nc, E)

        def walk(em, S):
            yps = (yps0, yps1)
            if load_x:
                em.do("gpsimd", lambda g: g.dma_start(
                    out=xT[:], in_=dram["xT"].rearrange("(c p) n -> p c n", p=P)),
                    inc=S["ld"], k=16)
            em.do("gpsimd", lambda g: g.dma_start(
                out=aoT[:], in_=aoTd.rearrange("(c p) n -> p c n", p=P)), inc=S["ld"], k=16)
            LD = S["ld"].n
            idx = 0
            for os_ in range(DCH):
                em.do("gpsimd", lambda g, os_=os_: g.dma_start(
                    out=wblk[:, os_ % 2, :, :],
                    in_=woutd.rearrange("(c p) f -> p c f", p=P)[:, :, os_ * P:(os_ + 1) * P]),
                    waits=[(S["ymm"], max(0, 2 * (os_ - 1)))],
                    inc=S["wl%d" % (os_ % 2)], k=16)
                for th in range(2):
                    cols = slice(th * 512, (th + 1) * 512)
                    ps = yps[idx % 2]
                    def ymm(t, os_=os_, cols=cols, ps=ps):
                        last = None
                        for c in range(DCH):
                            last = t.matmul(ps[:], wblk[:, os_ % 2, c, :], aoT[:, c, cols],
                                            start=(c == 0), stop=(c == DCH - 1),
                                            skip_group_check=True)
                        return last
                    em.do("tensor", ymm,
                          waits=[(S["wl%d" % (os_ % 2)], 16 * (os_ // 2 + 1)),
                                 (S["ld"], LD), (S["resd"], max(0, idx - 1))],
                          inc=S["ymm"])
                    em.do("vector", lambda v, os_=os_, cols=cols, ps=ps: v.tensor_add(
                        xT[:, os_, cols], xT[:, os_, cols], ps[:]),
                        waits=[(S["ymm"], idx + 1)], inc=S["resd"])
                    idx += 1
            if "x1T" in dram:
                em.do("gpsimd", lambda g: g.dma_start(
                    out=dram["x1T"].rearrange("(c p) n -> p c n", p=P), in_=xT[:]),
                    waits=[(S["resd"], 32)], inc=S["xout"], k=16)
            em.do("vector", lambda v: v.memset(lnaux["ones_col"][:], 1.0))
            em.do("vector", lambda v: v.memset(lnaux["ones_row"][:], 1.0))
            em.do("vector", lambda v: v.memset(lnaux["eps_b"][:], 1e-5), inc=S["resd"])
            emit_ln(em, S, xT, xnT, lnaux,
                    pre_waits=[(S["resd"], 33)])
            if "xnT" in dram:
                em.do("gpsimd", lambda g: g.dma_start(
                    out=dram["xnT"].rearrange("(c p) n -> p c n", p=P), in_=xnT[:]),
                    waits=[(S["ln_nrm"], 32)], inc=S["nout"], k=16)

        def done(S):
            return [(S["xout"], S["xout"].n), (S["nout"], S["nout"].n),
                    (S["ln_nrm"], 32), (S["resd"], S["resd"].n)]
        return walk, done


def build_post():
    nc = bass.Bass("TRN2", target_bir_lowering=False, debug=True,
                   detect_race_conditions=False)
    dram = {
        "xT": nc.dram_tensor("xT", [D, TPC], F32, kind="ExternalInput").ap(),
        "aoT": nc.dram_tensor("aoT", [D, TPC], BF16, kind="ExternalInput").ap(),
        "wout": nc.dram_tensor("wout", [D, D], F32, kind="ExternalInput").ap(),
        "x1T": nc.dram_tensor("x1T", [D, TPC], F32, kind="ExternalOutput").ap(),
        "xnT": nc.dram_tensor("xnT", [D, TPC], BF16, kind="ExternalOutput").ap(),
    }
    with ExitStack() as _ctx:
        walk, done = sec_post(nc, _ctx.enter_context, dram)
        block = _ctx.enter_context(nc.Block())
        sems = {n: Sem(nc.alloc_semaphore(n)) for n in POST_SEMS}

        def top(em):
            walk(em, sems)
            em.wait("gpsimd", done(sems))

        run_walk(nc, block, sems, top)
    return nc


QUANT_SEMS = ["csel", "cl0", "cl1", "cst0", "cst1", "tl0", "tl1", "absd",
              "summ", "srd", "prepd", "bcmm", "dved", "qout0", "qout1"]


def sec_quant(nc, E, dram, shared=None):
    wgud = dram["wgu"]
    wprd = dram["wpr"]
    bcselTd = dram["bcselT"]
    bcselfd = dram["bcself"]
    wguTd = dram["wguT"]
    wprTd = dram["wprT"]
    wgub = nc.dram_tensor("wgub", [D, D], BF16).ap()
    wprb = nc.dram_tensor("wprb", [256, 4 * D], BF16).ap()
    if True:
        ldbuf = E(nc.sbuf_tensor("ldbuf", [P, 2, D], BF16))
        ldbufp = E(nc.sbuf_tensor("ldbufp", [P, 2, 4 * D], BF16))
        wt = E(nc.sbuf_tensor("wt", [P, 2, D], BF16))
        aw = E(nc.sbuf_tensor("aw", [P, D], BF16))
        tq = E(nc.sbuf_tensor("tq", [P, D], BF16))
        qa = E(nc.sbuf_tensor("qa", [P, D], BF16))
        qb = E(nc.sbuf_tensor("qb", [P, D], BF16))
        wq = E(nc.sbuf_tensor("wq2", [P, 2, D], BF16))
        srow = E(nc.sbuf_tensor("srow", [2, 512], BF16))
        sf32 = E(nc.sbuf_tensor("sf32", [2, 512], F32))
        r32 = E(nc.sbuf_tensor("r32", [2, 512], F32))
        bcselT = E(nc.sbuf_tensor("bcselTs", [P, 2], BF16))
        bcself = E(nc.sbuf_tensor("bcselfs", [2, P], F32))
        sumps = E(nc.psum_tensor("sumps", [2, 512], F32))
        sbps = E(nc.psum_tensor("sbps", [P, 512], F32))
        rbps = E(nc.psum_tensor("rbps", [P, 512], F32))

        def walk(em, S):
            em.do("gpsimd", lambda g: g.dma_start(out=bcselT[:], in_=bcselTd),
                  inc=S["csel"], k=16)
            em.do("gpsimd", lambda g: g.dma_start(out=bcself[:], in_=bcselfd),
                  inc=S["csel"], k=16)
            for i in range(DCH):
                em.do("gpsimd", lambda g, i=i: g.dma_start(
                    out=ldbuf[:, i % 2, :], in_=wgud[i * P:(i + 1) * P, :]),
                    waits=[(S["cst%d" % (i % 2)], 16 * (i // 2))],
                    inc=S["cl%d" % (i % 2)], k=16)
                em.do("sync", lambda g, i=i: g.dma_start(
                    out=wgub[i * P:(i + 1) * P, :], in_=ldbuf[:, i % 2, :]),
                    waits=[(S["cl%d" % (i % 2)], 16 * (i // 2 + 1))],
                    inc=S["cst%d" % (i % 2)], k=16)
            for i in range(2):
                em.do("gpsimd", lambda g, i=i: g.dma_start(
                    out=ldbufp[:, i, :], in_=wprd[i * P:(i + 1) * P, :]),
                    inc=S["cl%d" % (i % 2)], k=16)
                em.do("sync", lambda g, i=i: g.dma_start(
                    out=wprb[i * P:(i + 1) * P, :], in_=ldbufp[:, i, :]),
                    waits=[[(S["cl0"], S["cl0"].n), (S["cl1"], S["cl1"].n)][i]],
                    inc=S["cst%d" % (i % 2)], k=16)
            CST0, CST1 = S["cst0"].n, S["cst1"].n

            jobs = [("gu", c) for c in range(DCH)] + [("pr", c) for c in range(64)]
            dved_after = {}
            npar = {0: 0, 1: 0}
            for j, (kind, c) in enumerate(jobs):
                par = j % 2
                F = D if kind == "gu" else 256
                src = (wgub[:, c * P:(c + 1) * P] if kind == "gu"
                       else wprb[:, c * P:(c + 1) * P])
                em.do("sync", lambda g, src=src, par=par, F=F: g.dma_start(
                    out=wt[:, par, 0:F], in_=src, transpose=True),
                    waits=[(S["cst0"], CST0), (S["cst1"], CST1),
                           (S["dved"], dved_after.get(j - 2, 0))],
                    inc=S["tl%d" % par], k=16)
                em.do("scalar", lambda s, par=par, F=F: s.activation(
                    aw[:, 0:F], wt[:, par, 0:F], AF.Abs),
                    waits=[(S["tl%d" % par], 16 * (j // 2 + 1)),
                           (S["summ"], S["summ"].n)],
                    inc=S["absd"])
                nstr = max(1, F // 512)
                W = min(F, 512)
                for st in range(nstr):
                    sl = slice(st * W, (st + 1) * W)
                    em.do("tensor", lambda t, sl=sl, W=W: t.matmul(
                        sumps[:, 0:W], bcselT[:], aw[:, sl],
                        start=True, stop=True, skip_group_check=True),
                        waits=[(S["absd"], S["absd"].n), (S["csel"], 32),
                               (S["srd"], S["srd"].n)],
                        inc=S["summ"])
                    em.do("vector", lambda v, W=W: v.tensor_single_scalar(
                        srow[:, 0:W], sumps[:, 0:W], 1.0 / 64, ALU.mult),
                        waits=[(S["summ"], S["summ"].n),
                               (S["bcmm"], S["bcmm"].n)],
                        inc=S["srd"])
                    em.do("vector", lambda v, W=W: v.tensor_single_scalar(
                        srow[:, 0:W], srow[:, 0:W], 1e-8, ALU.max))
                    em.do("vector", lambda v, W=W: v.tensor_copy(
                        sf32[:, 0:W], srow[:, 0:W]))
                    em.do("vector", lambda v, W=W: v.reciprocal(
                        r32[:, 0:W], sf32[:, 0:W]), inc=S["prepd"])
                    em.do("tensor", lambda t, W=W: t.matmul(
                        sbps[:, 0:W], bcself[:], sf32[:, 0:W],
                        start=True, stop=True, skip_group_check=True),
                        waits=[(S["prepd"], S["prepd"].n)])
                    em.do("tensor", lambda t, W=W: t.matmul(
                        rbps[:, 0:W], bcself[:], r32[:, 0:W],
                        start=True, stop=True, skip_group_check=True),
                        inc=S["bcmm"])
                    BC = S["bcmm"].n
                    em.do("vector", lambda v, par=par, sl=sl, W=W: v.tensor_mul(
                        tq[:, sl], wt[:, par, sl], rbps[:, 0:W]),
                        waits=[(S["bcmm"], BC)], inc=S["dved"])
                    em.do("vector", lambda v, sl=sl: v.tensor_single_scalar(
                        qa[:, sl], tq[:, sl], 0.5, ALU.is_gt), inc=S["dved"])
                    em.do("vector", lambda v, sl=sl: v.tensor_single_scalar(
                        qb[:, sl], tq[:, sl], -0.5, ALU.is_lt), inc=S["dved"])
                    em.do("vector", lambda v, sl=sl: v.tensor_sub(
                        qa[:, sl], qa[:, sl], qb[:, sl]), inc=S["dved"])
                    nq = npar[par]
                    em.do("vector", lambda v, par=par, sl=sl, W=W: v.tensor_mul(
                        wq[:, par, sl], qa[:, sl], sbps[:, 0:W]),
                        waits=[(S["qout%d" % par], 16 * nq)], inc=S["dved"])
                dved_after[j] = S["dved"].n
                def qst(g, kind=kind, c=c, par=par, F=F):
                    dst = (wguTd[c * P:(c + 1) * P, :] if kind == "gu"
                           else wprTd[c * P:(c + 1) * P, :])
                    return g.dma_start(out=dst, in_=wq[:, par, 0:F])
                em.do("gpsimd", qst, waits=[(S["dved"], S["dved"].n)],
                      inc=S["qout%d" % par], k=16)
                npar[par] += 1

        def done(S):
            return [(S["qout0"], S["qout0"].n), (S["qout1"], S["qout1"].n)]
        return walk, done


def build_quant():
    nc = bass.Bass("TRN2", target_bir_lowering=False, debug=True,
                   detect_race_conditions=False)
    dram = {
        "wgu": nc.dram_tensor("wgu", [D, D], F32, kind="ExternalInput").ap(),
        "wpr": nc.dram_tensor("wpr", [256, 4 * D], F32, kind="ExternalInput").ap(),
        "bcselT": nc.dram_tensor("bcselT", [P, 2], BF16, kind="ExternalInput").ap(),
        "bcself": nc.dram_tensor("bcself", [2, P], F32, kind="ExternalInput").ap(),
        "wguT": nc.dram_tensor("wguT", [D, D], BF16, kind="ExternalOutput").ap(),
        "wprT": nc.dram_tensor("wprT", [4 * D, 256], BF16, kind="ExternalOutput").ap(),
    }
    with ExitStack() as _ctx:
        walk, done = sec_quant(nc, _ctx.enter_context, dram)
        block = _ctx.enter_context(nc.Block())
        sems = {n: Sem(nc.alloc_semaphore(n)) for n in QUANT_SEMS}

        def top(em):
            walk(em, sems)
            em.wait("gpsimd", done(sems))

        run_walk(nc, block, sems, top)
    return nc



H = 4 * D
NH64 = H // P


FFN0_SEMS = ["ld", "consts", "wl0", "wl1", "gmm", "sild", "hmd", "hcp",
             "sqd", "sqmm", "rstd", "rbcmm", "rbd", "hnd", "wpl",
             "ymm", "resd", "xout"]


def sec_l0ffn(nc, E, dram, shared=None, gu_src=None, pr_src=None):
    wguTd = dram.get("wguT")
    wprTd = dram.get("wprT")
    if gu_src is None:
        def gu_src(i, up):
            base = H + i * P if up else i * P
            return wguTd.rearrange("(c p) h -> p c h", p=P)[:, :, base:base + P]
    if pr_src is None:
        def pr_src(os_):
            return wprTd.rearrange("(c p) o -> p c o", p=P)[:, :, os_ * P:(os_ + 1) * P]
    if True:
        if shared is None:
            x1T = E(nc.sbuf_tensor("x1Ts", [P, DCH, TPC], F32))
            xnT = E(nc.sbuf_tensor("xnTs", [P, DCH, TPC], BF16))
        else:
            x1T = shared["xT"]
            xnT = shared["xnT"]
        hbuf = E(nc.sbuf_tensor("hbufs", [P, NH64, 512], BF16))
        wg = E(nc.sbuf_tensor("wgs", [P, 2, DCH, P], BF16))
        wu = E(nc.sbuf_tensor("wus", [P, 2, DCH, P], BF16))
        wp = E(nc.sbuf_tensor("wps", [P, NH64, P], BF16))
        sg = E(nc.sbuf_tensor("sgs", [P, 512], F32))
        ht = E(nc.sbuf_tensor("hts", [P, 512], F32))
        sq = E(nc.sbuf_tensor("sqs", [P, 512], F32))
        rb = E(nc.sbuf_tensor("rbs", [P, 512], F32))
        rrow = E(nc.sbuf_tensor("rrows", [1, 512], F32))
        epsr = E(nc.sbuf_tensor("epsrs", [1, 1], F32))
        ones_col = E(nc.sbuf_tensor("ones_cols", [P, 1], F32))
        ones_row = E(nc.sbuf_tensor("ones_rows", [1, P], F32))
        gps = E(nc.psum_tensor("gps", [P, 512], F32))
        ups = E(nc.psum_tensor("ups", [P, 512], F32))
        yps0 = E(nc.psum_tensor("yps0", [P, 512], F32))
        yps1 = E(nc.psum_tensor("yps1", [P, 512], F32))
        ssq = E(nc.psum_tensor("ssq", [1, 512], F32))
        rbps = E(nc.psum_tensor("rbps", [P, 512], F32))

        def walk(em, S):
            yps = (yps0, yps1)
            if shared is None:
                em.do("gpsimd", lambda g: g.dma_start(
                    out=x1T[:], in_=dram["x1T"].rearrange("(c p) n -> p c n", p=P)),
                    inc=S["ld"], k=16)
                em.do("gpsimd", lambda g: g.dma_start(
                    out=xnT[:], in_=dram["xnT"].rearrange("(c p) n -> p c n", p=P)),
                    inc=S["ld"], k=16)
            LD = S["ld"].n
            em.do("vector", lambda v: v.memset(ones_col[:], 1.0))
            em.do("vector", lambda v: v.memset(ones_row[:], 1.0))
            em.do("vector", lambda v: v.memset(epsr[:], RMS_EPS), inc=S["consts"])

            for th in range(2):
                cols = slice(th * 512, (th + 1) * 512)
                for i in range(NH64):
                    gi = th * NH64 + i
                    par = gi % 2
                    em.do("gpsimd", lambda g, i=i, par=par: g.dma_start(
                        out=wg[:, par, :, :], in_=gu_src(i, False)),
                        waits=[(S["gmm"], max(0, S["gmm"].n - 1))],
                        inc=S["wl%d" % par], k=16)
                    em.do("gpsimd", lambda g, i=i, par=par: g.dma_start(
                        out=wu[:, par, :, :], in_=gu_src(i, True)),
                        inc=S["wl%d" % par], k=16)
                    def gmm(t, par=par, cols=cols):
                        last = None
                        for c in range(DCH):
                            last = t.matmul(gps[:], wg[:, par, c, :], xnT[:, c, cols],
                                            start=(c == 0), stop=(c == DCH - 1),
                                            skip_group_check=True)
                        return last
                    em.do("tensor", gmm,
                          waits=[(S["wl%d" % par], 32 * (gi // 2 + 1)), (S["ld"], LD),
                                 (S["sild"], max(0, gi))],
                          inc=S["gmm"])
                    def umm(t, par=par, cols=cols):
                        last = None
                        for c in range(DCH):
                            last = t.matmul(ups[:], wu[:, par, c, :], xnT[:, c, cols],
                                            start=(c == 0), stop=(c == DCH - 1),
                                            skip_group_check=True)
                        return last
                    em.do("tensor", umm,
                          waits=[(S["hmd"], max(0, gi))],
                          inc=S["gmm"])
                    em.do("scalar", lambda s: s.activation(sg[:], gps[:], AF.Silu),
                          waits=[(S["gmm"], S["gmm"].n - 1), (S["hmd"], max(0, gi))],
                          inc=S["sild"])
                    em.do("vector", lambda v: v.tensor_mul(ht[:], sg[:], ups[:]),
                          waits=[(S["sild"], S["sild"].n), (S["gmm"], S["gmm"].n),
                                 (S["sqd"], max(0, gi))],
                          inc=S["hmd"])
                    em.do("scalar", lambda s: s.activation(sq[:], ht[:], AF.Square),
                          waits=[(S["hmd"], S["hmd"].n),
                                 (S["sqmm"], S["sqmm"].n)],
                          inc=S["sqd"])
                    em.do("tensor", lambda t, i=i, th=th: t.matmul(
                        ssq[:], ones_col[:], sq[:],
                        start=(i == 0), stop=(i == NH64 - 1), skip_group_check=True),
                        waits=[(S["sqd"], S["sqd"].n), (S["consts"], 1)] +
                              ([(S["rstd"], th)] if i == 0 else []),
                        inc=S["sqmm"])
                    em.do("vector", lambda v, i=i, th=th: v.tensor_copy(hbuf[:, i, :], ht[:]),
                          waits=[(S["ymm"], 16 * th)] if i == 0 else [],
                          inc=S["hcp"])
                em.do("scalar", lambda s: s.activation(
                    rrow[:], ssq[:], AF.Ln, bias=epsr[:], scale=1.0 / H),
                    waits=[(S["sqmm"], S["sqmm"].n)])
                em.do("scalar", lambda s: s.activation(
                    rrow[:], rrow[:], AF.Exp, scale=-0.5), inc=S["rstd"])
                em.do("tensor", lambda t: t.matmul(
                    rbps[:], ones_row[:], rrow[:], start=True, stop=True,
                    skip_group_check=True),
                    waits=[(S["rstd"], th + 1), (S["rbd"], th)], inc=S["rbcmm"])
                em.do("vector", lambda v: v.tensor_copy(rb[:], rbps[:]),
                      waits=[(S["rbcmm"], th + 1), (S["hnd"], 64 * th)], inc=S["rbd"])
                for i in range(NH64):
                    em.do("vector", lambda v, i=i: v.tensor_mul(
                        hbuf[:, i, :], hbuf[:, i, :], rb[:]),
                        waits=[(S["rbd"], th + 1)], inc=S["hnd"])
                for os_ in range(DCH):
                    oi = th * DCH + os_
                    em.do("gpsimd", lambda g, os_=os_: g.dma_start(
                        out=wp[:], in_=pr_src(os_)),
                        waits=[(S["ymm"], max(0, S["ymm"].n))],
                        inc=S["wpl"], k=16)
                    ps = yps[oi % 2]
                    def ymm(t, ps=ps, os_=os_):
                        last = None
                        for c in range(NH64):
                            last = t.matmul(ps[:], wp[:, c, :], hbuf[:, c, :],
                                            start=(c == 0), stop=(c == NH64 - 1),
                                            skip_group_check=True)
                        return last
                    em.do("tensor", ymm,
                          waits=[(S["wpl"], 16 * (oi + 1)), (S["hnd"], 64 * (th + 1)),
                                 (S["resd"], max(0, oi - 1))],
                          inc=S["ymm"])
                    em.do("vector", lambda v, os_=os_, cols=cols, ps=ps: v.tensor_add(
                        x1T[:, os_, cols], x1T[:, os_, cols], ps[:]),
                        waits=[(S["ymm"], S["ymm"].n)], inc=S["resd"])
            if "x2T" in dram:
                em.do("gpsimd", lambda g: g.dma_start(
                    out=dram["x2T"].rearrange("(c p) n -> p c n", p=P), in_=x1T[:]),
                    waits=[(S["resd"], 32)], inc=S["xout"], k=16)

        def done(S):
            return [(S["xout"], S["xout"].n), (S["resd"], 32),
                    (S["hcp"], S["hcp"].n), (S["hnd"], S["hnd"].n)]
        return walk, done


def build_l0ffn():
    nc = bass.Bass("TRN2", target_bir_lowering=False, debug=True,
                   detect_race_conditions=False)
    dram = {
        "x1T": nc.dram_tensor("x1T", [D, TPC], F32, kind="ExternalInput").ap(),
        "xnT": nc.dram_tensor("xnT", [D, TPC], BF16, kind="ExternalInput").ap(),
        "wguT": nc.dram_tensor("wguT", [D, 2 * H], BF16, kind="ExternalInput").ap(),
        "wprT": nc.dram_tensor("wprT", [H, D], BF16, kind="ExternalInput").ap(),
        "x2T": nc.dram_tensor("x2T", [D, TPC], F32, kind="ExternalOutput").ap(),
    }
    with ExitStack() as _ctx:
        walk, done = sec_l0ffn(nc, _ctx.enter_context, dram)
        block = _ctx.enter_context(nc.Block())
        sems = {n: Sem(nc.alloc_semaphore(n)) for n in FFN0_SEMS}

        def top(em):
            walk(em, sems)
            em.wait("gpsimd", done(sems))

        run_walk(nc, block, sems, top)
    return nc



DC_KV = 4
DR = 64


L1PRE_SEMS = LN_SEMS + ["ld", "consts", "wl0", "wl1", "pmm", "evac", "sqd",
                        "nmm", "ncp", "rmm", "roped", "trd", "tcp", "nadd", "out"]


def sec_l1pre(nc, E, dram, shared=None):
    upto = "F"
    wqd = dram["wq"]
    wkvd = dram["wkv"]
    wqrd = dram["wqr"]
    wkrd = dram["wkr"]
    cosd = dram["cos64"]
    sind = dram["sin64"]
    identd = dram["ident"]
    qTd = dram["qT"]
    ckvTd = dram["ckvT"]
    qrTd = dram["qrT"]
    krTd = dram["krT"]
    qn2d = dram["qn2"]
    if True:
        if shared is None:
            xT = E(nc.sbuf_tensor("xTs", [P, DCH, TPC], F32))
            xnT = E(nc.sbuf_tensor("xnTs", [P, DCH, TPC], BF16))
        else:
            xT = shared["xT"]
            xnT = shared["xnT"]
        qT = E(nc.sbuf_tensor("qTs", [P, DCH, TPC], BF16))
        ckvT = E(nc.sbuf_tensor("ckvTs", [P, DC_KV, TPC], BF16))
        qrT = E(nc.sbuf_tensor("qrTs", [DR, TPC], BF16))
        krT = E(nc.sbuf_tensor("krTs", [DR, TPC], BF16))
        wblk = E(nc.sbuf_tensor("wblks", [P, 2, DCH, P], BF16))
        wqr = E(nc.sbuf_tensor("wqrs", [P, DCH, DR], BF16))
        wkr = E(nc.sbuf_tensor("wkrs", [P, DC_KV, DR], BF16))
        cosb = E(nc.sbuf_tensor("cosbs", [P, TS, DR], F32))
        sinb = E(nc.sbuf_tensor("sinbs", [P, TS, DR], F32))
        ident = E(nc.sbuf_tensor("idents", [P, P], BF16))
        sqs = E(nc.sbuf_tensor("sqss", [P, 512], F32))
        shuf = E(nc.sbuf_tensor("shufs", [P, DR], F32))
        tmpA = E(nc.sbuf_tensor("tmpAs", [P, DR], F32))
        tmpB = E(nc.sbuf_tensor("tmpBs", [P, DR], F32))
        rtm = E(nc.sbuf_tensor("rtms", [P, 2, P], BF16))
        qn2row = E(nc.sbuf_tensor("qn2rows", [1, 2, 512], F32))
        qrn2 = E(nc.sbuf_tensor("qrn2s", [1, 2, 512], F32))
        lnaux = mk_ln_aux(nc, E)
        pps0 = E(nc.psum_tensor("pps0", [P, 512], F32))
        pps1 = E(nc.psum_tensor("pps1", [P, 512], F32))
        tps = E(nc.psum_tensor("tps", [P, P], BF16))
        xps = E(nc.psum_tensor("xps", [P, 512], F32))

        def walk(em, S):
            sub = upto
            lvl = "A12BCDEF".index(upto) - 2 if upto in "A12BCDEF" else 5
            pps = (pps0, pps1)
            nps = xps
            rps = xps
            if shared is None:
                em.do("gpsimd", lambda g: g.dma_start(
                    out=xT[:], in_=dram["x2T"].rearrange("(c p) n -> p c n", p=P)),
                    inc=S["ld"], k=16)
            if sub not in "46":
                em.do("gpsimd", lambda g: g.dma_start(
                    out=wqr[:], in_=wqrd.rearrange("(c p) n -> p c n", p=P)), inc=S["ld"], k=16)
                em.do("gpsimd", lambda g: g.dma_start(
                    out=wkr[:], in_=wkrd.rearrange("(c p) n -> p c n", p=P)), inc=S["ld"], k=16)
                em.do("gpsimd", lambda g: g.dma_start(
                    out=cosb[:], in_=cosd.rearrange("(s p) n -> p s n", p=P)), inc=S["ld"], k=16)
                em.do("gpsimd", lambda g: g.dma_start(
                    out=sinb[:], in_=sind.rearrange("(s p) n -> p s n", p=P)), inc=S["ld"], k=16)
                em.do("gpsimd", lambda g: g.dma_start(out=ident[:], in_=identd), inc=S["ld"], k=16)
            LDA = S["ld"].n
            em.do("vector", lambda v: v.memset(lnaux["ones_col"][:], 1.0))
            em.do("vector", lambda v: v.memset(lnaux["ones_row"][:], 1.0))
            em.do("vector", lambda v: v.memset(rtm[:], 0.0))
            em.do("vector", lambda v: v.memset(lnaux["eps_b"][:], 1e-5), inc=S["consts"])
            emit_ln(em, S, xT, xnT, lnaux,
                    pre_waits=[(S["ld"], LDA), (S["consts"], 1)])

            jobs = [("kv", i) for i in range(DC_KV)] if (lvl >= 1 or sub in "12") else []
            for j, (kind, i) in enumerate(jobs):
                par = j % 2
                wsrc = wqd if kind == "q" else wkvd
                em.do("gpsimd", lambda g, i=i, par=par, wsrc=wsrc: g.dma_start(
                    out=wblk[:, par, :, :],
                    in_=wsrc.rearrange("(c p) f -> p c f", p=P)[:, :, i * P:(i + 1) * P]),
                    waits=[(S["pmm"], max(0, 2 * (j - 1)))],
                    inc=S["wl%d" % par], k=16)
                for th in range(2):
                    if sub == "1":
                        continue
                    cols = slice(th * 512, (th + 1) * 512)
                    jj = 2 * j + th
                    ps = pps[jj % 2]
                    def pmm(t, par=par, cols=cols, ps=ps):
                        last = None
                        for c in range(DCH):
                            last = t.matmul(ps[:], wblk[:, par, c, :], xnT[:, c, cols],
                                            start=(c == 0), stop=(c == DCH - 1),
                                            skip_group_check=True)
                        return last
                    em.do("tensor", pmm,
                          waits=[(S["wl%d" % par], 16 * (j // 2 + 1)), (S["ln_nrm"], 32),
                                 (S["evac"], max(0, jj - 1))],
                          inc=S["pmm"])
                    dst = qT if kind == "q" else ckvT
                    if sub != "2":
                        em.do("vector", lambda v, dst=dst, i=i, cols=cols, ps=ps: v.tensor_copy(
                            dst[:, i, cols], ps[:]),
                            waits=[(S["pmm"], S["pmm"].n)], inc=S["evac"])
                    if kind == "q":
                        em.do("scalar", lambda s, ps=ps: s.activation(sqs[:], ps[:], AF.Square),
                              waits=[(S["pmm"], S["pmm"].n),
                                     (S["nmm"], S["nmm"].n)],
                              inc=S["sqd"])
                        if sub != "9":
                            em.do("tensor", lambda t: t.matmul(
                                nps[0:1, :], lnaux["ones_col"][:], sqs[:],
                                start=True, stop=True, skip_group_check=True),
                                waits=[(S["sqd"], S["sqd"].n), (S["ncp"], S["ncp"].n)],
                                inc=S["nmm"])
                        em.do("vector", lambda v, i=i, cols=cols: v.tensor_copy(
                            qn2r[:, i, cols], nps[:]),
                            waits=[(S["nmm"], S["nmm"].n)], inc=S["ncp"])
            for kind in (("qr", "kr") if lvl >= 2 else ()):
                for ts in range(TS):
                    tsl = slice(ts * P, (ts + 1) * P)
                    def rmm(t, kind=kind, tsl=tsl):
                        last = None
                        if kind == "qr":
                            for c in range(DCH):
                                last = t.matmul(rps[:, 0:DR], xnT[:, c, tsl], wqr[:, c, :],
                                                start=(c == 0), stop=(c == DCH - 1),
                                                skip_group_check=True)
                        else:
                            for c in range(DC_KV):
                                last = t.matmul(rps[:, 0:DR], ckvT[:, c, tsl], wkr[:, c, :],
                                                start=(c == 0), stop=(c == DC_KV - 1),
                                                skip_group_check=True)
                        return last
                    ri = (0 if kind == "qr" else TS) + ts
                    em.do("tensor", rmm,
                          waits=[(S["ld"], LDA), (S["ln_nrm"], 32),
                                 (S["evac"], 8 if kind == "kr" else 0),
                                 (S["roped"], max(0, ri))],
                          inc=S["rmm"])
                    em.do("vector", lambda v: v.tensor_copy(
                        shuf[:, 0:32], rps[:, 32:64]),
                        waits=[(S["rmm"], S["rmm"].n), (S["trd"], max(0, ri - 1))])
                    em.do("vector", lambda v: v.tensor_copy(
                        shuf[:, 32:64], rps[:, 0:32]))
                    em.do("vector", lambda v, ts=ts: v.tensor_mul(
                        tmpA[:, 0:DR], rps[:, 0:DR], cosb[:, ts, :]))
                    em.do("vector", lambda v, ts=ts: v.tensor_mul(
                        tmpB[:, 0:DR], shuf[:, 0:DR], sinb[:, ts, :]))
                    em.do("vector", lambda v, ri=ri: v.tensor_add(
                        rtm[:, ri % 2, 0:DR], tmpA[:, 0:DR], tmpB[:, 0:DR]),
                        inc=S["roped"])
                    if lvl >= 3:
                        em.do("tensor", lambda t, ri=ri: t.transpose(
                            tps[:], rtm[:, ri % 2, :], ident[:]),
                            waits=[(S["roped"], S["roped"].n), (S["tcp"], max(0, ri))],
                            inc=S["trd"])
                        dstT = qrT if kind == "qr" else krT
                        em.do("vector", lambda v, dstT=dstT, tsl=tsl: v.tensor_copy(
                            dstT[0:DR, tsl], tps[0:DR, :]),
                            waits=[(S["trd"], S["trd"].n)], inc=S["tcp"])
                    else:
                        em.do("vector", lambda v, ri=ri: v.tensor_copy(
                            rtm[:, ri % 2, 0:1], rtm[:, ri % 2, 0:1]), inc=S["tcp"])
            for th in (range(2) if lvl >= 4 else ()):
                cols = slice(th * 512, (th + 1) * 512)
                em.do("scalar", lambda s, th=th, cols=cols: s.activation(
                    sqs[0:DR, 0:512], qrT[0:DR, cols], AF.Square),
                    waits=[(S["tcp"], 8), (S["nmm"], S["nmm"].n)], inc=S["sqd"])
                em.do("tensor", lambda t: t.matmul(
                    nps[0:1, :], lnaux["ones_col"][0:DR, :], sqs[0:DR, 0:512],
                    start=True, stop=True, skip_group_check=True),
                    waits=[(S["sqd"], S["sqd"].n), (S["ncp"], S["ncp"].n),
                           (S["roped"], 16)],
                    inc=S["nmm"])
                em.do("vector", lambda v, th=th: v.tensor_copy(
                    qrn2[:, th, :], nps[0:1, :]),
                    waits=[(S["nmm"], S["nmm"].n)], inc=S["ncp"])
            for i in (range(DCH) if lvl >= 5 else []):
                j = DC_KV + 8 + i
                par = i % 2
                em.do("gpsimd", lambda g, i=i, par=par: g.dma_start(
                    out=wblk[:, par, :, :],
                    in_=wqd.rearrange("(c p) f -> p c f", p=P)[:, :, i * P:(i + 1) * P]),
                    waits=[(S["pmm"], max(0, 2 * (DC_KV + i - 1)))],
                    inc=S["wl%d" % par], k=16)
                for th in range(2):
                    cols = slice(th * 512, (th + 1) * 512)
                    jj = 2 * (DC_KV + i) + th
                    ps = pps[jj % 2]
                    def pmm(t, par=par, cols=cols, ps=ps):
                        last = None
                        for c in range(DCH):
                            last = t.matmul(ps[:], wblk[:, par, c, :], xnT[:, c, cols],
                                            start=(c == 0), stop=(c == DCH - 1),
                                            skip_group_check=True)
                        return last
                    em.do("tensor", pmm,
                          waits=[(S["wl%d" % par], 16 * ((DC_KV + i) // 2 + 1)),
                                 (S["ln_nrm"], 32),
                                 (S["evac"], max(0, jj - 1))] +
                                ([(S["sqd"], max(0, S["sqd"].n - 1))] if sub != "8" else []),
                          inc=S["pmm"])
                    em.do("vector", lambda v, i=i, cols=cols, ps=ps: v.tensor_copy(
                        qT[:, i, cols], ps[:]),
                        waits=[(S["pmm"], S["pmm"].n)], inc=S["evac"])
                    if sub not in "8":
                        em.do("scalar", lambda s, i=i, cols=cols: s.activation(
                            sqs[:], qT[:, i, cols], AF.Square),
                              waits=[(S["evac"], S["evac"].n), (S["nmm"], S["nmm"].n)],
                              inc=S["sqd"])
                        if sub != "9":
                            em.do("tensor", lambda t: t.matmul(
                                nps[0:1, :], lnaux["ones_col"][:], sqs[:],
                                start=True, stop=True, skip_group_check=True),
                                waits=[(S["sqd"], S["sqd"].n), (S["ncp"], S["ncp"].n)],
                                inc=S["nmm"])
                        if sub != "9":
                            em.do("vector", lambda v, th=th: v.tensor_copy(
                                qn2row[:, th, :], nps[0:1, :]),
                                waits=[(S["nmm"], S["nmm"].n), (S["out"], S["out"].n)])
                            em.do("vector", lambda v, th=th: v.tensor_add(
                                qn2row[:, th, :], qn2row[:, th, :], qrn2[:, th, :]),
                                inc=S["ncp"])
                    if sub not in "789":
                        em.do("gpsimd", lambda g, i=i, th=th, cols=cols: g.dma_start(
                            out=qn2d[i:i + 1, cols], in_=qn2row[:, th, :]),
                            waits=[(S["ncp"], S["ncp"].n)], inc=S["out"], k=16)
            em.do("gpsimd", lambda g: g.dma_start(
                out=qTd.rearrange("(c p) n -> p c n", p=P), in_=qT[:]),
                waits=[(S["evac"], S["evac"].n)], inc=S["out"], k=16)
            em.do("gpsimd", lambda g: g.dma_start(
                out=ckvTd.rearrange("(c p) n -> p c n", p=P), in_=ckvT[:]),
                waits=[(S["evac"], S["evac"].n)], inc=S["out"], k=16)
            if sub not in "45":
                em.do("gpsimd", lambda g: g.dma_start(out=qrTd, in_=qrT[0:DR, :]),
                      waits=[(S["tcp"], S["tcp"].n)], inc=S["out"], k=16)
                em.do("gpsimd", lambda g: g.dma_start(out=krTd, in_=krT[0:DR, :]),
                      waits=[(S["tcp"], S["tcp"].n)], inc=S["out"], k=16)

        def done(S):
            return [(S["out"], S["out"].n)]
        return walk, done


def build_l1pre(upto="F"):
    nc = bass.Bass("TRN2", target_bir_lowering=False, debug=True,
                   detect_race_conditions=False)
    dram = {
        "x2T": nc.dram_tensor("x2T", [D, TPC], F32, kind="ExternalInput").ap(),
        "wq": nc.dram_tensor("wq", [D, D], F32, kind="ExternalInput").ap(),
        "wkv": nc.dram_tensor("wkv", [D, 512], F32, kind="ExternalInput").ap(),
        "wqr": nc.dram_tensor("wqr", [D, DR], F32, kind="ExternalInput").ap(),
        "wkr": nc.dram_tensor("wkr", [512, DR], F32, kind="ExternalInput").ap(),
        "cos64": nc.dram_tensor("cos64", [TPC, DR], F32, kind="ExternalInput").ap(),
        "sin64": nc.dram_tensor("sin64", [TPC, DR], F32, kind="ExternalInput").ap(),
        "ident": nc.dram_tensor("ident", [P, P], BF16, kind="ExternalInput").ap(),
        "qT": nc.dram_tensor("qT", [D, TPC], BF16, kind="ExternalOutput").ap(),
        "ckvT": nc.dram_tensor("ckvT", [512, TPC], BF16, kind="ExternalOutput").ap(),
        "qrT": nc.dram_tensor("qrT", [DR, TPC], BF16, kind="ExternalOutput").ap(),
        "krT": nc.dram_tensor("krT", [DR, TPC], BF16, kind="ExternalOutput").ap(),
        "qn2": nc.dram_tensor("qn2", [NH, TPC], F32, kind="ExternalOutput").ap(),
    }
    with ExitStack() as _ctx:
        walk, done = sec_l1pre(nc, _ctx.enter_context, dram)
        block = _ctx.enter_context(nc.Block())
        sems = {n: Sem(nc.alloc_semaphore(n)) for n in L1PRE_SEMS}

        def top(em):
            walk(em, sems)
            em.wait("gpsimd", done(sems))

        run_walk(nc, block, sems, top)
    return nc




ATTN1_SEMS = ATTN_SEMS + ["ld", "cks0", "cks1", "sout", "kvmm", "kvcp", "vcp",
                          "nsq", "nmm", "nred", "prek"]


def sec_l1attn(nc, E, dram, qn2_rows=None, ckv_src=None):
    if ckv_src is None:
        def ckv_src(st):
            tsl = slice(st * 512, (st + 1) * 512)
            return dram["ckvT"].rearrange("(c p) t -> p c t", p=P)[:, :, tsl]
    if True:
        tl = mk_attn_tiles(nc, E, L1=True)
        attnout = E(nc.sbuf_tensor("l1_attnout", [P, HPC, T], BF16))
        tl["attnout"] = attnout
        tl["qrT"] = E(nc.sbuf_tensor("l1_qrT", [DR, T], BF16))
        tl["krT"] = E(nc.sbuf_tensor("l1_krT", [DR, T], BF16))
        ckvs = E(nc.sbuf_tensor("l1_ckvs", [P, 2, DC_KV, 512], BF16))
        wkup = E(nc.sbuf_tensor("l1_wkup", [P, DC_KV, HPC * HD], BF16))
        wvup = E(nc.sbuf_tensor("l1_wvup", [P, DC_KV, HPC * HD], BF16))
        kmax = E(nc.sbuf_tensor("l1_kmax", [1, 1], F32))
        kmaxkr = E(nc.sbuf_tensor("l1_kmaxkr", [1, 1], F32))
        ones_colf = E(nc.sbuf_tensor("l1_onescf", [P, 1], F32))

        def walk(em, S):
            sps = tl["sps"]
            avps = tl["avps"]
            nrps = tl["lps"]
            sqn = tl["lbc"]
            def dma_in(out, in_):
                em.do("gpsimd", lambda g, out=out, in_=in_: g.dma_start(out=out, in_=in_),
                      inc=S["ld"], k=16)
            if isinstance(dram["qTh_src"], list):
                for s_, ap in enumerate(dram["qTh_src"]):
                    dma_in(tl["qTh"][:, :, s_ * TPC:(s_ + 1) * TPC], ap)
            else:
                dma_in(tl["qTh"][:], dram["qTh_src"])
            if isinstance(dram["qrT_src"], list):
                for s_, ap in enumerate(dram["qrT_src"]):
                    dma_in(tl["qrT"][0:DR, s_ * TPC:(s_ + 1) * TPC], ap)
                for s_, ap in enumerate(dram["krT_src"]):
                    dma_in(tl["krT"][0:DR, s_ * TPC:(s_ + 1) * TPC], ap)
            else:
                dma_in(tl["qrT"][0:DR, :], dram["qrT_src"])
                dma_in(tl["krT"][0:DR, :], dram["krT_src"])
            if qn2_rows is None:
                dma_in(tl["qn2"][:], dram["qn2_src"])
            else:
                for h_ in range(HPC):
                    for qt_ in range(16):
                        r = h_ * 16 + qt_
                        dma_in(tl["qn2"][r:r + 1, :], qn2_rows(h_, qt_))
            dma_in(tl["masks"][:], dram["masks"])
            dma_in(tl["eye32"][:], dram["eye32"])
            dma_in(wkup[:], dram["wkup"].rearrange("(c p) f -> p c f", p=P))
            dma_in(wvup[:], dram["wvup"].rearrange("(c p) f -> p c f", p=P))
            LDALL = S["ld"].n
            em.do("vector", lambda v: v.memset(tl["ones_colb"][:], 1.0))
            em.do("vector", lambda v: v.memset(tl["ones_row"][:], 1.0))
            em.do("vector", lambda v: v.memset(tl["ones_r32"][:], 1.0))
            em.do("vector", lambda v: v.memset(ones_colf[:], 1.0))
            em.do("vector", lambda v: v.memset(kmaxkr[:], 0.0))
            em.do("vector", lambda v: v.memset(tl["maxk2"][:], 0.0), inc=S["prek"])

            for st in range(16):
                tsl = slice(st * 512, (st + 1) * 512)
                spar = st % 2
                em.do("gpsimd", lambda g, st=st, spar=spar: g.dma_start(
                    out=ckvs[:, spar, :, :], in_=ckv_src(st)),
                    waits=[(S["kvmm"], max(0, 6 * (st - 1)))],
                    inc=S["cks%d" % spar], k=16)
                CKSN = S["cks%d" % spar].n
                for hs in range(HPC):
                    ji = st * HPC + hs
                    ps = sps[ji % 2]
                    def kmm(t, hs=hs, spar=spar, ps=ps):
                        last = None
                        for c in range(DC_KV):
                            last = t.matmul(ps[:], wkup[:, c, hs * HD:(hs + 1) * HD],
                                            ckvs[:, spar, c, :],
                                            start=(c == 0), stop=(c == DC_KV - 1),
                                            skip_group_check=True)
                        return last
                    em.do("tensor", kmm,
                          waits=[(S["ld"], LDALL), (S["cks%d" % spar], CKSN),
                                 (S["kvcp"], max(0, ji - 1)),
                                 (S["nsq"], max(0, ji - 1))],
                          inc=S["kvmm"])
                    em.do("vector", lambda v, hs=hs, tsl=tsl, ps=ps: v.tensor_copy(
                        tl["kT"][:, hs, tsl], ps[:]),
                        waits=[(S["kvmm"], S["kvmm"].n)], inc=S["kvcp"])
                    em.do("scalar", lambda s, hs=hs, tsl=tsl: s.activation(
                        sqn[:, 0, :], tl["kT"][:, hs, tsl], AF.Square),
                        waits=[(S["kvcp"], S["kvcp"].n), (S["nmm"], S["nmm"].n)],
                        inc=S["nsq"])
                    em.do("tensor", lambda t: t.matmul(
                        nrps[0:1, 0:512], ones_colf[:], sqn[:, 0, :],
                        start=True, stop=True, skip_group_check=True),
                        waits=[(S["nsq"], S["nsq"].n), (S["nred"], S["nred"].n),
                               (S["prek"], 1)],
                        inc=S["nmm"])
                    em.do("vector", lambda v: v.reduce_max(
                        kmax[:], nrps[0:1, 0:512], axis=AX.X),
                        waits=[(S["nmm"], S["nmm"].n)])
                    em.do("vector", lambda v: v.tensor_max(
                        tl["maxk2"][:], tl["maxk2"][:], kmax[:]), inc=S["nred"])
                for sub in range(4):
                    b = st * 4 + sub
                    bsl = slice(sub * P, (sub + 1) * P)
                    ps = avps[b % 2]
                    def vmm(t, spar=spar, bsl=bsl, ps=ps):
                        last = None
                        for c in range(DC_KV):
                            last = t.matmul(ps[:, 0:HPC * HD], ckvs[:, spar, c, bsl],
                                            wvup[:, c, :],
                                            start=(c == 0), stop=(c == DC_KV - 1),
                                            skip_group_check=True)
                        return last
                    em.do("tensor", vmm,
                          waits=[(S["vcp"], max(0, b - 1))],
                          inc=S["kvmm"])
                    em.do("vector", lambda v, b=b, ps=ps: v.tensor_copy(
                        tl["vt"][:, b, :], ps[:, 0:HPC * HD]),
                        waits=[(S["kvmm"], S["kvmm"].n)], inc=S["vcp"])
            for st in range(16):
                tsl = slice(st * 512, (st + 1) * 512)
                em.do("scalar", lambda s, tsl=tsl: s.activation(
                    sqn[0:DR, 0, :], tl["krT"][0:DR, tsl], AF.Square),
                    waits=[(S["ld"], LDALL), (S["nmm"], S["nmm"].n)], inc=S["nsq"])
                em.do("tensor", lambda t: t.matmul(
                    nrps[0:1, 0:512], ones_colf[0:DR, :], sqn[0:DR, 0, :],
                    start=True, stop=True, skip_group_check=True),
                    waits=[(S["nsq"], S["nsq"].n), (S["nred"], S["nred"].n)],
                    inc=S["nmm"])
                em.do("vector", lambda v: v.reduce_max(
                    kmax[:], nrps[0:1, 0:512], axis=AX.X),
                    waits=[(S["nmm"], S["nmm"].n)])
                em.do("vector", lambda v: v.tensor_max(
                    kmaxkr[:], kmaxkr[:], kmax[:]), inc=S["nred"])
            em.do("vector", lambda v: v.tensor_add(
                tl["maxk2"][:], tl["maxk2"][:], kmaxkr[:]),
                waits=[(S["nred"], 48)], inc=S["prek"])

            kv_ready = [(S["ld"], LDALL), (S["kvcp"], 32), (S["vcp"], 64),
                        (S["prek"], 2)]
            emit_attn_core(em, S, tl, L1=True, kv_ready_waits=kv_ready)
            if isinstance(dram["attnout_dst"], list):
                for s_, ap in enumerate(dram["attnout_dst"]):
                    em.do("gpsimd", lambda g, s_=s_, ap=ap: g.dma_start(
                        out=ap, in_=attnout[:, :, s_ * TPC:(s_ + 1) * TPC]),
                        waits=[(S["at_evacd"], HPC * 16)], inc=S["sout"], k=16)
            else:
                em.do("gpsimd", lambda g: g.dma_start(
                    out=dram["attnout_dst"], in_=attnout[:]),
                    waits=[(S["at_evacd"], HPC * 16)], inc=S["sout"], k=16)

        def done(S):
            return [(S["sout"], S["sout"].n)]
        return walk, done


def build_l1attn():
    nc = bass.Bass("TRN2", target_bir_lowering=False, debug=True,
                   detect_race_conditions=False)
    qThd = nc.dram_tensor("qTh", [HPC * HD, T], BF16, kind="ExternalInput").ap()
    qrTd = nc.dram_tensor("qrT", [DR, T], BF16, kind="ExternalInput").ap()
    ckvTd = nc.dram_tensor("ckvT", [512, T], BF16, kind="ExternalInput").ap()
    krTd = nc.dram_tensor("krT", [DR, T], BF16, kind="ExternalInput").ap()
    wkupd = nc.dram_tensor("wkup", [512, HPC * HD], F32, kind="ExternalInput").ap()
    wvupd = nc.dram_tensor("wvup", [512, HPC * HD], F32, kind="ExternalInput").ap()
    qn2d = nc.dram_tensor("qn2", [32, 512], F32, kind="ExternalInput").ap()
    maskd = nc.dram_tensor("masks", [P, 4, 512], F32, kind="ExternalInput").ap()
    eyed = nc.dram_tensor("eye32", [32, 32], F32, kind="ExternalInput").ap()
    outd = nc.dram_tensor("attnout", [HPC * HD, T], BF16, kind="ExternalOutput").ap()
    dram = {
        "qTh_src": qThd.rearrange("(h p) t -> p h t", p=P),
        "qrT_src": qrTd,
        "krT_src": krTd,
        "ckvT": ckvTd,
        "wkup": wkupd,
        "wvup": wvupd,
        "qn2_src": qn2d,
        "masks": maskd,
        "eye32": eyed,
        "attnout_dst": outd.rearrange("(h p) t -> p h t", p=P),
    }
    with ExitStack() as _ctx:
        walk, done = sec_l1attn(nc, _ctx.enter_context, dram)
        block = _ctx.enter_context(nc.Block())
        sems = {n: Sem(nc.alloc_semaphore(n)) for n in ATTN1_SEMS}

        def top(em):
            walk(em, sems)
            em.wait("gpsimd", done(sems))

        run_walk(nc, block, sems, top)
    return nc



KI = 8192
I_ = 1024
NE = 128


MOE_SEMS = LN_SEMS + ["ld", "consts", "vbar", "rmm", "rsf", "nmxd", "topd",
                      "trmm", "trcp", "amm", "acp", "mxd", "esd", "wl0", "wl1",
                      "gumm", "sild", "basd", "mxacc", "mtd", "mtcp", "sgl0",
                      "sgl1", "sgmm", "ssil", "shd", "wdl", "ymm", "resd", "xout"]


def sec_l1moe(nc, E, dram, shared=None, debug_taps=False, upto=7, skip_ln=False):
    xTd = dram.get("x3T")
    wrtd = dram["wrt"]
    wgated = dram["wgate"]
    wupd = dram["wup"]
    wdownd = dram["wdown"]
    amp0d = dram["amp0"]
    amp1d = dram["amp1"]
    esd = dram["es"]
    wsgd = dram["wsg"]
    wsud = dram["wsu"]
    wsdd = dram["wsd"]
    identd = dram["ident"]
    x4Td = dram["x4T"]
    if debug_taps:
        alphad = dram["alphaD"]
        probsd = dram["probsD"]
        m8d = dram["m8D"]
        mskd = dram["mskD"]
        wtbd = dram["wtbD"]
        mixsd = dram["mixsD"]
        ppd = dram["ppD"]
        amp0o = dram["amp0D"]
        mixedd = dram["mixedD"]
        mixTd = dram["mixTD"]

    if True:
        if shared is None:
            xT = E(nc.sbuf_tensor("m_xT", [P, DCH, TPC], F32))
            xnT = E(nc.sbuf_tensor("m_xnT", [P, DCH, TPC], BF16))
        else:
            xT = shared["xT"]
            xnT = shared["xnT"]
        mixed = E(nc.sbuf_tensor("m_mixed", [P, TS, I_], F32))
        mixT = E(nc.sbuf_tensor("m_mixT", [P, 8, TPC], BF16))
        alpha = E(nc.sbuf_tensor("m_alpha", [P, TS, 8], F32))
        wblk = E(nc.sbuf_tensor("m_wblk", [P, 1, DCH, 256], BF16))
        wblku = E(nc.sbuf_tensor("m_wblku", [P, 1, DCH, 256], BF16))
        wdwn = E(nc.sbuf_tensor("m_wdwn", [P, 8, P], BF16))
        wsdb = E(nc.sbuf_tensor("m_wsdb", [P, DCH, P], BF16))
        amp0 = E(nc.sbuf_tensor("m_amp0", [P, 8], F32))
        amp1 = E(nc.sbuf_tensor("m_amp1", [P, 8], F32))
        es = E(nc.sbuf_tensor("m_es", [1, NE], F32))
        esb = E(nc.sbuf_tensor("m_esb", [P, NE], F32))
        ident = E(nc.sbuf_tensor("m_ident", [P, P], BF16))
        probs = E(nc.sbuf_tensor("m_probs", [P, NE], F32))
        m8 = E(nc.sbuf_tensor("m_m8", [P, 8], F32))
        msk = E(nc.sbuf_tensor("m_msk", [P, NE], F32))
        nmax = E(nc.sbuf_tensor("m_nmax", [P, 1], F32))
        ssum = E(nc.sbuf_tensor("m_ssum", [P, 1], F32))
        stw = E(nc.sbuf_tensor("m_stw", [P, 1], F32))
        pp = E(nc.sbuf_tensor("m_pp", [P, 8], F32))
        mixSb = E(nc.sbuf_tensor("m_mixSb", [P, 8], BF16))
        wb16 = E(nc.sbuf_tensor("m_wb16", [P, NE], BF16))
        wtb = E(nc.sbuf_tensor("m_wtb", [P, NE], BF16))
        lnaux = mk_ln_aux(nc, E)
        rps = E(nc.psum_tensor("m_rps", [P, NE], F32))
        tps = E(nc.psum_tensor("m_tps", [P, P], BF16))
        gps = E(nc.psum_tensor("m_gps", [P, 512], F32))
        ups = E(nc.psum_tensor("m_ups", [P, 512], F32))

        def walk(em, S):
            lvl = upto
            yps = (lnaux["bcm"], lnaux["bcr"])
            bas = lnaux["tmpA"]
            mtb = wb16
            shT = mixed.ap().bitcast(BF16).rearrange("p a (b n) -> p (a b) n", b=2)
            wrt = wblk.ap().rearrange("p a c f -> p (a c) f")[:, 0:DCH, 0:NE] if False else None
            def dma_in(out, in_):
                em.do("gpsimd", lambda g, out=out, in_=in_: g.dma_start(out=out, in_=in_),
                      inc=S["ld"], k=16)
            if shared is None:
                dma_in(xT[:], xTd.rearrange("(c p) n -> p c n", p=P))
            dma_in(wblk[:, 0, :, 0:NE], wrtd.rearrange("(c p) n -> p c n", p=P))
            dma_in(amp0[0:NE, :], amp0d)
            dma_in(amp1[0:NE, :], amp1d)
            dma_in(es[:], esd)
            dma_in(ident[:], identd)
            LDA = S["ld"].n
            em.do("vector", lambda v: v.memset(lnaux["ones_col"][:], 1.0))
            em.do("vector", lambda v: v.memset(lnaux["ones_row"][:], 1.0))
            em.do("vector", lambda v: v.memset(lnaux["eps_b"][:], 1e-5), inc=S["consts"])
            for k in range(TS):
                em.do("vector", lambda v, k=k: v.memset(mixed[:, k, :], 0.0),
                      inc=(S["consts"] if k == TS - 1 else None))
            if not skip_ln:
                emit_ln(em, S, xT, xnT, lnaux,
                        pre_waits=[(S["ld"], LDA), (S["consts"], 2)])

            VB = S["vbar"]
            def vb(fn, waits=None, inc=None):
                em.do("vector", fn, waits=(waits or []) + [(VB, VB.n)],
                      inc=(inc if inc is not None else VB))
            vb(lambda v: v.tensor_mul(pp[0:NE, :], amp0[0:NE, :], amp0[0:NE, :]),
               waits=[(S["ld"], LDA)])
            vb(lambda v: v.tensor_mul(m8[0:NE, :], amp1[0:NE, :], amp1[0:NE, :]))
            vb(lambda v: v.tensor_add(pp[0:NE, :], pp[0:NE, :], m8[0:NE, :]))
            vb(lambda v: v.reduce_sum(ssum[0:NE, :], pp[0:NE, :], axis=AX.X))
            vb(lambda v: v.tensor_single_scalar(
                ssum[0:NE, :], ssum[0:NE, :], 1e-8, ALU.add))
            vb(lambda v: v.reciprocal(stw[0:NE, :], ssum[0:NE, :]))
            vb(lambda v: v.scalar_tensor_tensor(
                mixSb[0:NE, :], pp[0:NE, :], stw[0:NE, :], pp[0:NE, :],
                ALU.mult, ALU.bypass), inc=S["mxd"])
            em.do("tensor", lambda t: t.matmul(
                rps[:], lnaux["ones_row"][:], es[:],
                start=True, stop=True, skip_group_check=True),
                waits=[(S["ld"], LDA), (S["consts"], 2)], inc=S["rmm"])
            em.do("vector", lambda v: v.tensor_copy(esb[:], rps[:]),
                  waits=[(S["rmm"], 1)], inc=S["esd"])

            for ts in (range(TS) if lvl >= 2 else []):
                tsl = slice(ts * P, (ts + 1) * P)
                def rtmm(t, tsl=tsl):
                    last = None
                    for c in range(DCH):
                        last = t.matmul(rps[:], xnT[:, c, tsl], wblk[:, 0, c, 0:NE],
                                        start=(c == 0), stop=(c == DCH - 1),
                                        skip_group_check=True)
                    return last
                em.do("tensor", rtmm,
                      waits=[(S["ld"], LDA), (S["ln_nrm"], 32), (S["esd"], 1),
                             (S["rsf"], max(0, ts)), (S["acp"], max(0, ts))],
                      inc=S["rmm"])
                RN = S["rmm"].n
                vb(lambda v: v.tensor_copy(probs[:], rps[:]),
                   waits=[(S["rmm"], RN), (S["topd"], S["topd"].n)])
                vb(lambda v: v.reduce_max(nmax[:], probs[:], axis=AX.X))
                vb(lambda v: v.tensor_single_scalar(
                    nmax[:], nmax[:], -1.0, ALU.mult), inc=S["nmxd"])
                em.do("scalar", lambda s: s.activation(
                    probs[:], probs[:], AF.Exp, bias=nmax[:], accum_out=ssum[:]),
                    waits=[(S["rsf"], S["rsf"].n), (S["nmxd"], S["nmxd"].n)],
                    inc=S["rsf"])
                vb(lambda v: v.reciprocal(stw[:], ssum[:]),
                   waits=[(S["rsf"], S["rsf"].n)])
                vb(lambda v: v.scalar_tensor_tensor(
                    probs[:], probs[:], stw[:], probs[:], ALU.mult, ALU.bypass))
                vb(lambda v: v.max(out=m8[:], in_=probs[:]))
                vb(lambda v: v.scalar_tensor_tensor(
                    msk[:], probs[:], m8[:, 3:4], probs[:], ALU.is_ge, ALU.bypass))
                vb(lambda v: v.reduce_sum(ssum[:], m8[:, 0:4], axis=AX.X))
                vb(lambda v: v.tensor_single_scalar(
                    ssum[:], ssum[:], 1e-8, ALU.add))
                vb(lambda v: v.reciprocal(stw[:], ssum[:]))
                vb(lambda v: v.tensor_mul(msk[:], msk[:], probs[:]))
                vb(lambda v: v.scalar_tensor_tensor(
                    msk[:], msk[:], stw[:], msk[:], ALU.mult, ALU.bypass))
                vb(lambda v: v.tensor_mul(msk[:], msk[:], esb[:]))
                vb(lambda v: v.tensor_copy(wb16[:], msk[:]),
                   inc=S["topd"])
                if lvl < 3:
                    em.do("vector", lambda v, ts=ts: v.memset(alpha[:, ts, :], 0.125),
                          inc=S["acp"])
                    continue
                em.do("tensor", lambda t: t.transpose(tps[:], wb16[:], ident[:]),
                      waits=[(S["topd"], S["topd"].n), (S["trcp"], max(0, ts)),
                             (S["ld"], LDA)],
                      inc=S["trmm"])
                em.do("vector", lambda v: v.tensor_copy(wtb[:], tps[:]),
                      waits=[(S["trmm"], S["trmm"].n)], inc=S["trcp"])
                em.do("tensor", lambda t: t.matmul(
                    rps[:, 0:8], wtb[:], mixSb[0:NE, :],
                    start=True, stop=True, skip_group_check=True),
                    waits=[(S["trcp"], S["trcp"].n), (S["mxd"], 1)],
                    inc=S["amm"])
                em.do("vector", lambda v, ts=ts: v.tensor_copy(alpha[:, ts, :], rps[:, 0:8]),
                      waits=[(S["amm"], S["amm"].n)], inc=S["acp"])
                if debug_taps and ts == 0:
                    em.do("gpsimd", lambda g: g.dma_start(out=probsd, in_=probs[:]),
                          waits=[(S["acp"], 1)], inc=S["xout"], k=16)
                    em.do("gpsimd", lambda g: g.dma_start(out=m8d, in_=m8[:]),
                          waits=[(S["acp"], 1)], inc=S["xout"], k=16)
                    em.do("gpsimd", lambda g: g.dma_start(out=mskd, in_=msk[:]),
                          waits=[(S["acp"], 1)], inc=S["xout"], k=16)
                    em.do("gpsimd", lambda g: g.dma_start(out=wtbd, in_=wtb[:]),
                          waits=[(S["acp"], 1)], inc=S["xout"], k=16)
                    em.do("gpsimd", lambda g: g.dma_start(out=mixsd, in_=mixSb[0:P, :]),
                          waits=[(S["acp"], 1)], inc=S["xout"], k=16)
                    em.do("gpsimd", lambda g: g.dma_start(out=ppd, in_=pp[0:P, :]),
                          waits=[(S["acp"], 1)], inc=S["xout"], k=16)
                    em.do("gpsimd", lambda g: g.dma_start(out=amp0o, in_=amp0[0:P, :]),
                          waits=[(S["acp"], 1)], inc=S["xout"], k=16)

            for hs in (range(32) if lvl >= 4 else []):
                par = 0
                k = hs // 4
                io = (hs % 4) * 256
                em.do("gpsimd", lambda g, hs=hs, par=par: g.dma_start(
                    out=wblk[:, par, :, :],
                    in_=wgated.rearrange("(c p) h -> p c h", p=P)[:, :, hs * 256:(hs + 1) * 256]),
                    waits=[(S["gumm"], 2 * TS * hs)] +
                          ([(S["acp"], TS)] if hs == 0 else []),
                    inc=S["wl0"], k=16)
                em.do("gpsimd", lambda g, hs=hs: g.dma_start(
                    out=wblku[:, 0, :, :],
                    in_=wupd.rearrange("(c p) h -> p c h", p=P)[:, :, hs * 256:(hs + 1) * 256]),
                    inc=S["wl0"], k=16)
                for ts in range(TS):
                    tsl = slice(ts * P, (ts + 1) * P)
                    def gmm(t, par=par, tsl=tsl):
                        last = None
                        for c in range(DCH):
                            last = t.matmul(gps[:, 0:256], xnT[:, c, tsl], wblk[:, par, c, :],
                                            start=(c == 0), stop=(c == DCH - 1),
                                            skip_group_check=True)
                        return last
                    em.do("tensor", gmm,
                          waits=[(S["wl0"], 32 * (hs + 1)),
                                 (S["ln_nrm"], 32),
                                 (S["sild"], max(0, S["sild"].n))],
                          inc=S["gumm"])
                    def umm(t, par=par, tsl=tsl):
                        last = None
                        for c in range(DCH):
                            last = t.matmul(ups[:, 0:256], xnT[:, c, tsl], wblku[:, par, c, :],
                                            start=(c == 0), stop=(c == DCH - 1),
                                            skip_group_check=True)
                        return last
                    em.do("tensor", umm,
                          waits=[(S["basd"], max(0, S["basd"].n))],
                          inc=S["gumm"])
                    em.do("scalar", lambda s: s.activation(lnaux["sq"][:, 0, 0:256], gps[:, 0:256], AF.Silu),
                          waits=[(S["gumm"], S["gumm"].n - 1),
                                 (S["basd"], S["basd"].n)],
                          inc=S["sild"])
                    em.do("vector", lambda v: v.tensor_mul(bas[:, 0:256], lnaux["sq"][:, 0, 0:256], ups[:, 0:256]),
                          waits=[(S["sild"], S["sild"].n), (S["gumm"], S["gumm"].n)],
                          inc=S["basd"])
                    em.do("vector", lambda v, ts=ts, k=k, io=io: v.scalar_tensor_tensor(
                        mixed[:, ts, io:io + 256], bas[:, 0:256], alpha[:, ts, k:k + 1],
                        mixed[:, ts, io:io + 256], ALU.mult, ALU.add),
                        waits=[(S["acp"], TS)], inc=S["mxacc"])

            for ts in (range(TS) if lvl >= 5 else []):
                for ic in range(8):
                    mi = ts * 8 + ic
                    em.do("vector", lambda v, ts=ts, ic=ic: v.tensor_copy(
                        mtb[:, 0:P], mixed[:, ts, ic * P:(ic + 1) * P]),
                        waits=[(S["mxacc"], 256)], inc=S["mtd"])
                    em.do("tensor", lambda t: t.transpose(tps[:], mtb[:, 0:P], ident[:]),
                          waits=[(S["mtd"], S["mtd"].n), (S["mtcp"], max(0, mi))],
                          inc=S["trmm"])
                    em.do("vector", lambda v, ts=ts, ic=ic: v.tensor_copy(
                        mixT[:, ic, ts * P:(ts + 1) * P], tps[:]),
                        waits=[(S["trmm"], S["trmm"].n)], inc=S["mtcp"])

            for fs in (range(DCH) if lvl >= 6 else []):
                par = 0
                em.do("gpsimd", lambda g, fs=fs, par=par: g.dma_start(
                    out=wblk[:, par, :, 0:P],
                    in_=wsgd.rearrange("(c p) h -> p c h", p=P)[:, :, fs * P:(fs + 1) * P]),
                    waits=[(S["sgmm"], 4 * fs)] +
                          ([(S["gumm"], 512)] if fs == 0 else []),
                    inc=S["sgl0"], k=16)
                em.do("gpsimd", lambda g, fs=fs: g.dma_start(
                    out=wblku[:, 0, :, 0:P],
                    in_=wsud.rearrange("(c p) h -> p c h", p=P)[:, :, fs * P:(fs + 1) * P]),
                    inc=S["sgl0"], k=16)
                for th in range(2):
                    cols = slice(th * 512, (th + 1) * 512)
                    def sgmm(t, par=par, cols=cols):
                        last = None
                        for c in range(DCH):
                            last = t.matmul(gps[:], wblk[:, par, c, 0:P], xnT[:, c, cols],
                                            start=(c == 0), stop=(c == DCH - 1),
                                            skip_group_check=True)
                        return last
                    em.do("tensor", sgmm,
                          waits=[(S["sgl0"], 32 * (fs + 1)),
                                 (S["gumm"], 512), (S["sild"], 256),
                                 (S["mtcp"], 64),
                                 (S["ssil"], max(0, S["ssil"].n))],
                          inc=S["sgmm"])
                    def summ(t, par=par, cols=cols):
                        last = None
                        for c in range(DCH):
                            last = t.matmul(ups[:], wblku[:, par, c, 0:P], xnT[:, c, cols],
                                            start=(c == 0), stop=(c == DCH - 1),
                                            skip_group_check=True)
                        return last
                    em.do("tensor", summ,
                          waits=[(S["shd"], max(0, S["shd"].n)), (S["basd"], 256)],
                          inc=S["sgmm"])
                    em.do("scalar", lambda s: s.activation(lnaux["sq"][:, 0, :], gps[:], AF.Silu),
                          waits=[(S["sgmm"], S["sgmm"].n - 1), (S["shd"], S["shd"].n),
                                 (S["basd"], 256)],
                          inc=S["ssil"])
                    em.do("vector", lambda v, fs=fs, cols=cols: v.tensor_mul(
                        shT[:, fs, cols], lnaux["sq"][:, 0, :], ups[:]),
                        waits=[(S["ssil"], S["ssil"].n), (S["sgmm"], S["sgmm"].n)],
                        inc=S["shd"])

            for os_ in (range(DCH) if lvl >= 7 else []):
                par = os_ % 2
                em.do("gpsimd", lambda g, os_=os_: g.dma_start(
                    out=wsdb[:],
                    in_=wsdd.rearrange("(c p) o -> p c o", p=P)[:, :, os_ * P:(os_ + 1) * P]),
                    waits=[(S["ymm"], 2 * os_)],
                    inc=S["wdl"], k=16)
                em.do("gpsimd", lambda g, os_=os_: g.dma_start(
                    out=wdwn[:],
                    in_=wdownd.rearrange("(c p) o -> p c o", p=P)[:, :, os_ * P:(os_ + 1) * P]),
                    inc=S["wdl"], k=16)
                for th in range(2):
                    cols = slice(th * 512, (th + 1) * 512)
                    oi = os_ * 2 + th
                    ps = yps[oi % 2]
                    def ymm(t, os_=os_, cols=cols, ps=ps):
                        last = None
                        for c in range(8):
                            last = t.matmul(ps[:], wdwn[:, c, :],
                                            mixT[:, c, cols],
                                            start=(c == 0), stop=False,
                                            skip_group_check=True)
                        for c in range(DCH):
                            last = t.matmul(ps[:], wsdb[:, c, :], shT[:, c, cols],
                                            start=False, stop=(c == DCH - 1),
                                            skip_group_check=True)
                        return last
                    em.do("tensor", ymm,
                          waits=[(S["wdl"], 32 * (os_ + 1)), (S["ld"], LDA),
                                 (S["mtcp"], 64), (S["shd"], 32),
                                 (S["resd"], max(0, oi - 1))],
                          inc=S["ymm"])
                    em.do("vector", lambda v, os_=os_, cols=cols, ps=ps: v.tensor_add(
                        xT[:, os_, cols], xT[:, os_, cols], ps[:]),
                        waits=[(S["ymm"], S["ymm"].n)], inc=S["resd"])
            em.do("gpsimd", lambda g: g.dma_start(
                out=x4Td.rearrange("(c p) n -> p c n", p=P), in_=xT[:]),
                waits=[(S["resd"], 32 if lvl >= 7 else 0),
                       (S["ln_nrm"], 32)], inc=S["xout"], k=16)
            if debug_taps:
                em.do("gpsimd", lambda g: g.dma_start(out=alphad, in_=alpha[:]),
                      waits=[(S["acp"], 8)], inc=S["xout"], k=16)
                em.do("gpsimd", lambda g: g.dma_start(out=mixedd, in_=mixed[:]),
                      waits=[(S["mxacc"], 256), (S["sgmm"], 0 if lvl < 6 else 64)],
                      inc=S["xout"], k=16)
                em.do("gpsimd", lambda g: g.dma_start(out=mixTd, in_=mixT[:]),
                      waits=[(S["mtcp"], 64)], inc=S["xout"], k=16)

        def done(S):
            return [(S["xout"], S["xout"].n)]
        return walk, done


def build_l1moe(upto=7, debug_taps=False):
    nc = bass.Bass("TRN2", target_bir_lowering=False, debug=True,
                   detect_race_conditions=False)
    dram = {
        "x3T": nc.dram_tensor("x3T", [D, TPC], F32, kind="ExternalInput").ap(),
        "wrt": nc.dram_tensor("wrt", [D, NE], F32, kind="ExternalInput").ap(),
        "wgate": nc.dram_tensor("wgate", [D, KI], F32, kind="ExternalInput").ap(),
        "wup": nc.dram_tensor("wup", [D, KI], F32, kind="ExternalInput").ap(),
        "wdown": nc.dram_tensor("wdown", [I_, D], F32, kind="ExternalInput").ap(),
        "amp0": nc.dram_tensor("amp0", [NE, 8], F32, kind="ExternalInput").ap(),
        "amp1": nc.dram_tensor("amp1", [NE, 8], F32, kind="ExternalInput").ap(),
        "es": nc.dram_tensor("es", [1, NE], F32, kind="ExternalInput").ap(),
        "wsg": nc.dram_tensor("wsg", [D, D], F32, kind="ExternalInput").ap(),
        "wsu": nc.dram_tensor("wsu", [D, D], F32, kind="ExternalInput").ap(),
        "wsd": nc.dram_tensor("wsd", [D, D], F32, kind="ExternalInput").ap(),
        "ident": nc.dram_tensor("ident", [P, P], BF16, kind="ExternalInput").ap(),
        "x4T": nc.dram_tensor("x4T", [D, TPC], F32, kind="ExternalOutput").ap(),
    }
    if debug_taps:
        dram["alphaD"] = nc.dram_tensor("alphaD", [P, TS, 8], F32, kind="ExternalOutput").ap()
        dram["probsD"] = nc.dram_tensor("probsD", [P, NE], F32, kind="ExternalOutput").ap()
        dram["m8D"] = nc.dram_tensor("m8D", [P, 8], F32, kind="ExternalOutput").ap()
        dram["mskD"] = nc.dram_tensor("mskD", [P, NE], F32, kind="ExternalOutput").ap()
        dram["wtbD"] = nc.dram_tensor("wtbD", [P, NE], BF16, kind="ExternalOutput").ap()
        dram["mixsD"] = nc.dram_tensor("mixsD", [P, 8], BF16, kind="ExternalOutput").ap()
        dram["ppD"] = nc.dram_tensor("ppD", [P, 8], F32, kind="ExternalOutput").ap()
        dram["amp0D"] = nc.dram_tensor("amp0D", [P, 8], F32, kind="ExternalOutput").ap()
        dram["mixedD"] = nc.dram_tensor("mixedD", [P, TS, I_], F32, kind="ExternalOutput").ap()
        dram["mixTD"] = nc.dram_tensor("mixTD", [P, 8, TPC], BF16, kind="ExternalOutput").ap()
    with ExitStack() as _ctx:
        walk, done = sec_l1moe(nc, _ctx.enter_context, dram,
                               debug_taps=debug_taps, upto=upto)
        block = _ctx.enter_context(nc.Block())
        sems = {n: Sem(nc.alloc_semaphore(n)) for n in MOE_SEMS}

        def top(em):
            walk(em, sems)
            em.wait("gpsimd", done(sems))

        run_walk(nc, block, sems, top)
    return nc




def _mksems(nc, prefix, names):
    return {n: Sem(nc.alloc_semaphore(prefix + n)) for n in names}


class NcPfx:
    def __init__(self, nc, pfx):
        self._nc = nc
        self._pfx = pfx

    def sbuf_tensor(self, name, *a, **k):
        return self._nc.sbuf_tensor(self._pfx + name, *a, **k)

    def psum_tensor(self, name, *a, **k):
        return self._nc.psum_tensor(self._pfx + name, *a, **k)

    def __getattr__(self, n):
        return getattr(self._nc, n)


def build_g1():
    nc = bass.Bass("TRN2", target_bir_lowering=False, debug=True,
                   detect_race_conditions=False)
    dram = {
        "xT": nc.dram_tensor("xT", [D, TPC], F32, kind="ExternalInput").ap(),
        "wq": nc.dram_tensor("wq", [D, D], F32, kind="ExternalInput").ap(),
        "wkv": nc.dram_tensor("wkv", [D, 256], F32, kind="ExternalInput").ap(),
        "cos4": nc.dram_tensor("cos4", [TPC, 512], F32, kind="ExternalInput").ap(),
        "sin4": nc.dram_tensor("sin4", [TPC, 512], F32, kind="ExternalInput").ap(),
        "qrope": nc.dram_tensor("qrope", [TPC, D], BF16, kind="ExternalOutput").ap(),
        "kt": nc.dram_tensor("kt", [TPC, 128], BF16, kind="ExternalOutput").ap(),
        "vt": nc.dram_tensor("vt", [TPC, 128], BF16, kind="ExternalOutput").ap(),
        "qn2": nc.dram_tensor("qn2", [P, 8, 16], F32, kind="ExternalOutput").ap(),
        "kn2": nc.dram_tensor("kn2", [P, 8], F32, kind="ExternalOutput").ap(),
        "wgu": nc.dram_tensor("wgu", [D, D], F32, kind="ExternalInput").ap(),
        "wpr": nc.dram_tensor("wpr", [256, 4 * D], F32, kind="ExternalInput").ap(),
        "bcselT": nc.dram_tensor("bcselT", [P, 2], BF16, kind="ExternalInput").ap(),
        "bcself": nc.dram_tensor("bcself", [2, P], F32, kind="ExternalInput").ap(),
        "wguT": nc.dram_tensor("wguT", [D, D], BF16, kind="ExternalOutput").ap(),
        "wprT": nc.dram_tensor("wprT", [4 * D, 256], BF16, kind="ExternalOutput").ap(),
    }
    with ExitStack() as top:
        with ExitStack() as s1:
            walkA, doneA = sec_l0pre(NcPfx(nc, "a0"), s1.enter_context, dram)
        with ExitStack() as s2:
            walkB, doneB = sec_quant(NcPfx(nc, "b0"), s2.enter_context, dram)
        block = top.enter_context(nc.Block())
        semsA = _mksems(nc, "a_", L0PRE_SEMS)
        semsB = _mksems(nc, "b_", QUANT_SEMS)
        allsems = dict(semsA)
        allsems.update({"b_" + k: v for k, v in semsB.items()})

        def mega(em):
            walkA(em, semsA)
            em.set_pre(doneA(semsA))
            walkB(em, semsB)
            em.wait("gpsimd", doneA(semsA) + doneB(semsB))

        run_walk(nc, block, allsems, mega)
    return nc


def build_g3():
    nc = bass.Bass("TRN2", target_bir_lowering=False, debug=True,
                   detect_race_conditions=False)
    dram = {
        "xT": nc.dram_tensor("xT", [D, TPC], F32, kind="ExternalInput").ap(),
        "aoT": nc.dram_tensor("aoT", [D, TPC], BF16, kind="ExternalInput").ap(),
        "wout": nc.dram_tensor("wout", [D, D], F32, kind="ExternalInput").ap(),
        "wguT": nc.dram_tensor("wguT", [D, 2 * H], BF16, kind="ExternalInput").ap(),
        "wprT": nc.dram_tensor("wprT", [H, D], BF16, kind="ExternalInput").ap(),
        "wq": nc.dram_tensor("wq", [D, D], F32, kind="ExternalInput").ap(),
        "wkv": nc.dram_tensor("wkv", [D, 512], F32, kind="ExternalInput").ap(),
        "wqr": nc.dram_tensor("wqr", [D, DR], F32, kind="ExternalInput").ap(),
        "wkr": nc.dram_tensor("wkr", [512, DR], F32, kind="ExternalInput").ap(),
        "cos64": nc.dram_tensor("cos64", [TPC, DR], F32, kind="ExternalInput").ap(),
        "sin64": nc.dram_tensor("sin64", [TPC, DR], F32, kind="ExternalInput").ap(),
        "ident": nc.dram_tensor("ident", [P, P], BF16, kind="ExternalInput").ap(),
        "x2T": nc.dram_tensor("x2T", [D, TPC], F32, kind="ExternalOutput").ap(),
        "qT": nc.dram_tensor("qT", [D, TPC], BF16, kind="ExternalOutput").ap(),
        "ckvT": nc.dram_tensor("ckvT", [512, TPC], BF16, kind="ExternalOutput").ap(),
        "qrT": nc.dram_tensor("qrT", [DR, TPC], BF16, kind="ExternalOutput").ap(),
        "krT": nc.dram_tensor("krT", [DR, TPC], BF16, kind="ExternalOutput").ap(),
        "qn2": nc.dram_tensor("qn2", [16, TPC], F32, kind="ExternalOutput").ap(),
    }
    with ExitStack() as top:
        E = top.enter_context
        shared = {"xT": E(nc.sbuf_tensor("sh_xT", [P, DCH, TPC], F32)),
                  "xnT": E(nc.sbuf_tensor("sh_xnT", [P, DCH, TPC], BF16))}
        ffn_dram = {"wguT": dram["wguT"], "wprT": dram["wprT"], "x2T": dram["x2T"]}
        with ExitStack() as s1:
            walkA, doneA = sec_post(NcPfx(nc, "a0"), s1.enter_context, dram, shared=shared)
        with ExitStack() as s2:
            walkB, doneB = sec_l0ffn(NcPfx(nc, "b0"), s2.enter_context, ffn_dram, shared=shared)
        with ExitStack() as s3:
            walkC, doneC = sec_l1pre(NcPfx(nc, "c0"), s3.enter_context, dram, shared=shared)
        block = top.enter_context(nc.Block())
        semsA = _mksems(nc, "a_", POST_SEMS)
        semsB = _mksems(nc, "b_", FFN0_SEMS)
        semsC = _mksems(nc, "c_", L1PRE_SEMS)
        allsems = {}
        for pfx, sd in (("a_", semsA), ("b_", semsB), ("c_", semsC)):
            allsems.update({pfx + k: v for k, v in sd.items()})

        def mega(em):
            walkA(em, semsA)
            em.set_pre(doneA(semsA))
            walkB(em, semsB)
            em.set_pre(doneB(semsB))
            walkC(em, semsC)
            em.wait("gpsimd", doneB(semsB) + doneC(semsC))

        run_walk(nc, block, allsems, mega)
    return nc


def build_g5():
    nc = bass.Bass("TRN2", target_bir_lowering=False, debug=True,
                   detect_race_conditions=False)
    dram = {
        "xT": nc.dram_tensor("xT", [D, TPC], F32, kind="ExternalInput").ap(),
        "aoT": nc.dram_tensor("aoT", [D, TPC], BF16, kind="ExternalInput").ap(),
        "wout": nc.dram_tensor("wout", [D, D], F32, kind="ExternalInput").ap(),
        "wrt": nc.dram_tensor("wrt", [D, NE], F32, kind="ExternalInput").ap(),
        "wgate": nc.dram_tensor("wgate", [D, KI], F32, kind="ExternalInput").ap(),
        "wup": nc.dram_tensor("wup", [D, KI], F32, kind="ExternalInput").ap(),
        "wdown": nc.dram_tensor("wdown", [I_, D], F32, kind="ExternalInput").ap(),
        "amp0": nc.dram_tensor("amp0", [NE, 8], F32, kind="ExternalInput").ap(),
        "amp1": nc.dram_tensor("amp1", [NE, 8], F32, kind="ExternalInput").ap(),
        "es": nc.dram_tensor("es", [1, NE], F32, kind="ExternalInput").ap(),
        "wsg": nc.dram_tensor("wsg", [D, D], F32, kind="ExternalInput").ap(),
        "wsu": nc.dram_tensor("wsu", [D, D], F32, kind="ExternalInput").ap(),
        "wsd": nc.dram_tensor("wsd", [D, D], F32, kind="ExternalInput").ap(),
        "ident": nc.dram_tensor("ident", [P, P], BF16, kind="ExternalInput").ap(),
        "x4T": nc.dram_tensor("x4T", [D, TPC], F32, kind="ExternalOutput").ap(),
    }
    with ExitStack() as top:
        E = top.enter_context
        shared = {"xT": E(nc.sbuf_tensor("sh_xT", [P, DCH, TPC], F32)),
                  "xnT": E(nc.sbuf_tensor("sh_xnT", [P, DCH, TPC], BF16))}
        with ExitStack() as s1:
            walkA, doneA = sec_post(NcPfx(nc, "a0"), s1.enter_context, dram, shared=shared)
        with ExitStack() as s2:
            walkB, doneB = sec_l1moe(NcPfx(nc, "b0"), s2.enter_context, dram,
                                     shared=shared, skip_ln=True)
        block = top.enter_context(nc.Block())
        semsA = _mksems(nc, "a_", POST_SEMS)
        semsB = _mksems(nc, "b_", MOE_SEMS)
        semsB["ln_nrm"] = semsA["ln_nrm"]
        allsems = {"a_" + k: v for k, v in semsA.items()}
        allsems.update({"b_" + k: v for k, v in semsB.items() if k != "ln_nrm"})

        def mega(em):
            walkA(em, semsA)
            em.set_pre(doneA(semsA))
            walkB(em, semsB)
            em.wait("gpsimd", doneB(semsB))

        run_walk(nc, block, allsems, mega)
    return nc


ALU = mybir.AluOpType


def build_mega():
    nc = bass.Bass("TRN2", target_bir_lowering=False, debug=True,
                   detect_race_conditions=False)
    dI, dO = "ExternalInput", "ExternalOutput"

    def din(name, shape, dt=F32):
        return nc.dram_tensor(name, shape, dt, kind=dI).ap()

    def dint(name, shape, dt=BF16):
        return nc.dram_tensor(name, shape, dt).ap()

    ext = {
        "xT": din("xT", [D, TPC]), "wq0": din("wq0", [D, D]),
        "wkv0": din("wkv0", [D, 256]),
        "cos4": din("cos4", [TPC, 512]), "sin4": din("sin4", [TPC, 512]),
        "wgu": din("wgu", [D, D]), "wpr": din("wpr", [256, 4 * D]),
        "bcselT": din("bcselT", [P, 2], BF16), "bcself": din("bcself", [2, P]),
        "wout0": din("wout0", [D, D]),
        "wq1": din("wq1", [D, D]), "wkv1": din("wkv1", [D, 512]),
        "wqr": din("wqr", [D, DR]), "wkr": din("wkr", [512, DR]),
        "cos64": din("cos64", [TPC, DR]), "sin64": din("sin64", [TPC, DR]),
        "ident": din("ident", [P, P], BF16),
        "masks": din("masks", [P, 4, 512]), "eye32": din("eye32", [32, 32]),
        "wkup": din("wkup", [512, HPC * HD]), "wvup": din("wvup", [512, HPC * HD]),
        "wo1": din("wo1", [D, D]),
        "wrt": din("wrt", [D, NE]), "wgate": din("wgate", [D, KI]),
        "wup": din("wup", [D, KI]), "wdown": din("wdown", [I_, D]),
        "amp0": din("amp0", [NE, 8]), "amp1": din("amp1", [NE, 8]),
        "es": din("es", [1, NE]),
        "wsg": din("wsg", [D, D]), "wsu": din("wsu", [D, D]),
        "wsd": din("wsd", [D, D]),
    }
    x4Td = nc.dram_tensor("x4T", [D, TPC], F32, kind=dO).ap()

    qTd0 = dint("i_qT0", [D, TPC])
    ktd = dint("i_kt", [HD, TPC])
    vtd = dint("i_vt", [TPC, HD])
    qn2d0 = dint("i_qn20", [NH, TPC], F32)
    kn2d0 = dint("i_kn20", [TPC, 1], F32)
    wguTs = dint("i_wguTs", [D, D])
    wprTs = dint("i_wprTs", [4 * D, 256])
    qTa = dint("i_qTa", [D, TPC])
    kta = dint("i_kta", [NCORES * HD, TPC])
    vta = dint("i_vta", [T, HD])
    qn2a = dint("i_qn2a", [NH, TPC], F32)
    kn2a = dint("i_kn2a", [T, 1], F32)
    wguTg = dint("i_wguTg", [NCORES * D, D])
    wprTg = dint("i_wprTg", [NCORES * 4 * D, 256])
    aod = dint("i_aod", [D, TPC])
    aoa = dint("i_aoa", [D, TPC])
    x2Td = dint("i_x2T", [D, TPC], F32)
    qTd1 = dint("i_qT1", [D, TPC])
    ckvTd = dint("i_ckvT", [512, TPC])
    qrTd = dint("i_qrT", [DR, TPC])
    krTd = dint("i_krT", [DR, TPC])
    qn2d1 = dint("i_qn21", [NH, TPC], F32)
    ckvg = dint("i_ckvg", [NCORES * 512, TPC])
    qrg = dint("i_qrg", [NCORES * DR, TPC])
    krg = dint("i_krg", [NCORES * DR, TPC])
    qTa1 = dint("i_qTa1", [D, TPC])
    qn2a1 = dint("i_qn2a1", [NH, TPC], F32)
    ao1d = dint("i_ao1d", [D, TPC])
    ao1a = dint("i_ao1a", [D, TPC])

    dram_pre = {"xT": ext["xT"], "wq": ext["wq0"], "wkv": ext["wkv0"],
                "cos4": ext["cos4"], "sin4": ext["sin4"], "ident": ext["ident"],
                "qrope": qTd0, "kt": ktd, "vt": vtd, "qn2": qn2d0, "kn2": kn2d0}
    dram_q = {"wgu": ext["wgu"], "wpr": ext["wpr"], "bcselT": ext["bcselT"],
              "bcself": ext["bcself"], "wguT": wguTs, "wprT": wprTs}
    qTa4 = qTa.rearrange("(s hh p) t -> s p hh t", p=P, hh=HPC)
    kta4 = kta.rearrange("(s p) t -> s p t", p=P)
    kn2a4 = kn2a.rearrange("(s t) o -> s o t", s=NCORES)
    dram_at0 = {
        "qTh_src": [qTa4[s] for s in range(NCORES)],
        "kT_src": [kta4[s] for s in range(NCORES)],
        "vt_src": vta.rearrange("(b p) f -> p b f", p=P),
        "kn2_src": [kn2a4[s] for s in range(NCORES)],
        "masks": ext["masks"], "eye32": ext["eye32"],
        "attnout_dst": [aod.rearrange("(s h p) t -> s p h t", p=P, h=HPC)[s]
                        for s in range(NCORES)],
    }

    def qn2_rows0(h, qt):
        r = 2 * (qt // 2) + h
        c0 = (qt % 2) * 512
        return qn2a[r:r + 1, c0:c0 + 512]

    dram_post0 = {"xT": ext["xT"], "aoT": aoa, "wout": ext["wout0"]}
    dram_ffn = {"x2T": x2Td}

    def gu_src(i, up):
        j0 = (H if up else 0) + i * P
        s = j0 // D
        cc0 = j0 % D
        return wguTg.rearrange("(s c p) h -> p s c h", p=P, c=DCH)[:, s, :, cc0:cc0 + P]

    def pr_src(os_):
        s = os_ // 2
        cc0 = (os_ % 2) * P
        return wprTg.rearrange("(s c p) o -> p s c o", p=P, c=64)[:, s, :, cc0:cc0 + P]

    dram_l1pre = {"wq": ext["wq1"], "wkv": ext["wkv1"], "wqr": ext["wqr"],
                  "wkr": ext["wkr"], "cos64": ext["cos64"], "sin64": ext["sin64"],
                  "ident": ext["ident"], "qT": qTd1, "ckvT": ckvTd,
                  "qrT": qrTd, "krT": krTd, "qn2": qn2d1}
    qTa14 = qTa1.rearrange("(s hh p) t -> s p hh t", p=P, hh=HPC)
    qrg4 = qrg.rearrange("(s r) t -> s r t", r=DR)
    krg4 = krg.rearrange("(s r) t -> s r t", r=DR)
    dram_at1 = {
        "qTh_src": [qTa14[s] for s in range(NCORES)],
        "qrT_src": [qrg4[s] for s in range(NCORES)],
        "krT_src": [krg4[s] for s in range(NCORES)],
        "wkup": ext["wkup"], "wvup": ext["wvup"],
        "masks": ext["masks"], "eye32": ext["eye32"],
        "attnout_dst": [ao1d.rearrange("(s h p) t -> s p h t", p=P, h=HPC)[s]
                        for s in range(NCORES)],
    }

    def ckv_src(st):
        return ckvg.rearrange("(s c p) t -> p s c t", p=P, c=DC_KV)[
            :, st // 2, :, (st % 2) * 512:(st % 2) * 512 + 512]

    def qn2_rows1(h, qt):
        r = 2 * (qt // 2) + h
        c0 = (qt % 2) * 512
        return qn2a1[r:r + 1, c0:c0 + 512]

    dram_post1 = {"xT": x2Td, "aoT": ao1a, "wout": ext["wo1"]}
    dram_moe = {"wrt": ext["wrt"], "wgate": ext["wgate"], "wup": ext["wup"],
                "wdown": ext["wdown"], "amp0": ext["amp0"], "amp1": ext["amp1"],
                "es": ext["es"], "wsg": ext["wsg"], "wsu": ext["wsu"],
                "wsd": ext["wsd"], "ident": ext["ident"], "x4T": x4Td}

    with ExitStack() as top:
        with ExitStack() as s1:
            w_pre, d_pre = sec_l0pre(NcPfx(nc, "p0"), s1.enter_context,
                                     dram_pre, fused=True)
        with ExitStack() as s2:
            w_q, d_q = sec_quant(NcPfx(nc, "q0"), s2.enter_context, dram_q)
        with ExitStack() as s3:
            w_a0, d_a0 = sec_attn0(NcPfx(nc, "t0"), s3.enter_context, dram_at0,
                                   qn2_rows=qn2_rows0)
        with ExitStack() as s4:
            E4 = s4.enter_context
            shared1 = {"xT": E4(nc.sbuf_tensor("g3_xT", [P, DCH, TPC], F32)),
                       "xnT": E4(nc.sbuf_tensor("g3_xnT", [P, DCH, TPC], BF16))}
            with ExitStack() as s4a:
                w_p0, d_p0 = sec_post(NcPfx(nc, "u0"), s4a.enter_context,
                                      dram_post0, shared=shared1)
            with ExitStack() as s4b:
                w_f, d_f = sec_l0ffn(NcPfx(nc, "f0"), s4b.enter_context,
                                     dram_ffn, shared=shared1,
                                     gu_src=gu_src, pr_src=pr_src)
            with ExitStack() as s4c:
                w_l1p, d_l1p = sec_l1pre(NcPfx(nc, "v0"), s4c.enter_context,
                                         dram_l1pre, shared=shared1)
        with ExitStack() as s5:
            w_a1, d_a1 = sec_l1attn(NcPfx(nc, "w0"), s5.enter_context, dram_at1,
                                    qn2_rows=qn2_rows1, ckv_src=ckv_src)
        with ExitStack() as s6:
            E6 = s6.enter_context
            shared2 = {"xT": E6(nc.sbuf_tensor("g5_xT", [P, DCH, TPC], F32)),
                       "xnT": E6(nc.sbuf_tensor("g5_xnT", [P, DCH, TPC], BF16))}
            with ExitStack() as s6a:
                w_p1, d_p1 = sec_post(NcPfx(nc, "y0"), s6a.enter_context,
                                      dram_post1, shared=shared2)
            with ExitStack() as s6b:
                w_moe, d_moe = sec_l1moe(NcPfx(nc, "z0"), s6b.enter_context,
                                         dram_moe, shared=shared2, skip_ln=True)

        block = top.enter_context(nc.Block())
        pool = {"dma": [], "cmp": []}

        def pool_get(names, dma_names):
            used = {"dma": 0, "cmp": 0}
            out = {}
            for n in names:
                t = "dma" if n in dma_names else "cmp"
                i = used[t]
                used[t] += 1
                if i >= len(pool[t]):
                    pool[t].append(Sem(nc.alloc_semaphore("%s%02d" % (t, i))))
                out[n] = pool[t][i]
            return out

        DMA = {
            "p": {"ld", "wql0", "wql1", "qout0", "qout1", "kvout0", "kvout1", "nout"},
            "q": {"csel", "cl0", "cl1", "cst0", "cst1", "tl0", "tl1", "qout0", "qout1"},
            "t": {"ld", "sout"},
            "u": {"ld", "wl0", "wl1", "xout", "nout"},
            "f": {"ld", "wl0", "wl1", "wpl", "xout"},
            "v": {"ld", "wl0", "wl1", "out"},
            "w": {"ld", "cks0", "cks1", "sout"},
            "y": {"ld", "wl0", "wl1", "xout", "nout"},
            "z": {"ld", "wl0", "wl1", "sgl0", "sgl1", "wdl", "xout"},
        }
        NAMES = {"p": L0PRE_SEMS, "q": QUANT_SEMS, "t": ATTN0_SEMS,
                 "u": POST_SEMS, "f": FFN0_SEMS, "v": L1PRE_SEMS,
                 "w": ATTN1_SEMS, "y": POST_SEMS, "z": MOE_SEMS}
        phys = {pfx: pool_get(NAMES[pfx], DMA[pfx]) for pfx in
                ("p", "q", "t", "u", "f", "v", "w", "y", "z")}
        cc1 = Sem(nc.alloc_semaphore("cc1"))
        cc2 = Sem(nc.alloc_semaphore("cc2"))
        cc3 = Sem(nc.alloc_semaphore("cc3"))
        cc4 = Sem(nc.alloc_semaphore("cc4"))
        sems = {"cc1": cc1, "cc2": cc2, "cc3": cc3, "cc4": cc4}
        for i, s in enumerate(pool["dma"]):
            sems["dma%02d" % i] = s
        for i, s in enumerate(pool["cmp"]):
            sems["cmp%02d" % i] = s

        def views(pfx):
            return {n: SemView(s) for n, s in phys[pfx].items()}

        import concourse.aot_env as _aot_env

        def _cc(g, src, dst, kind):
            with _aot_env.allow_ici_all_to_all():
                return g.collective_compute(kind, ALU.bypass,
                                            replica_groups=[list(range(NCORES))],
                                            ins=[src], outs=[dst])

        def coll(em, src, dst, kind, waits, inc):
            em.do("gpsimd", lambda g, src=src, dst=dst, kind=kind:
                  _cc(g, src, dst, kind),
                  waits=waits, inc=inc, k=1)

        def mega(em):
            sP = views("p")
            w_pre(em, sP)
            em.set_pre(d_pre(sP))
            sQ = views("q")
            w_q(em, sQ)
            pre_done = d_pre(sP)
            q_done = d_q(sQ)
            coll(em, ktd, kta, "AllGather", pre_done, cc1)
            coll(em, vtd, vta, "AllGather", [], cc1)
            coll(em, kn2d0, kn2a, "AllGather", [], cc1)
            coll(em, qTd0, qTa, "AllToAll", [], cc1)
            coll(em, qn2d0, qn2a, "AllToAll", [], cc1)
            coll(em, wguTs, wguTg, "AllGather", q_done, cc1)
            coll(em, wprTs, wprTg, "AllGather", [], cc1)
            em.set_pre(pre_done + q_done + [(cc1, 5)])
            sA0 = views("t")
            w_a0(em, sA0)
            coll(em, aod, aoa, "AllToAll", d_a0(sA0), cc2)
            em.set_pre(d_a0(sA0) + [(cc2, 1)])
            sU0 = views("u")
            w_p0(em, sU0)
            em.set_pre(d_p0(sU0) + [(cc1, 7)])
            sF = views("f")
            w_f(em, sF)
            em.set_pre(d_f(sF))
            sV = views("v")
            w_l1p(em, sV)
            l1p_done = d_l1p(sV)
            coll(em, ckvTd, ckvg, "AllGather", l1p_done, cc3)
            coll(em, qrTd, qrg, "AllGather", [], cc3)
            coll(em, krTd, krg, "AllGather", [], cc3)
            coll(em, qTd1, qTa1, "AllToAll", [], cc3)
            coll(em, qn2d1, qn2a1, "AllToAll", [], cc3)
            em.set_pre(d_p0(sU0) + d_f(sF) + l1p_done + [(cc3, 5)])
            sW = views("w")
            w_a1(em, sW)
            coll(em, ao1d, ao1a, "AllToAll", d_a1(sW), cc4)
            em.set_pre(d_a1(sW) + [(cc4, 1)])
            sY = views("y")
            w_p1(em, sY)
            em.set_pre(d_p1(sY))
            sZ = views("z")
            sZ["ln_nrm"] = sY["ln_nrm"]
            w_moe(em, sZ)
            em.wait("gpsimd", d_moe(sZ))

        run_walk(nc, block, sems, mega)
    return nc


_BF = ml_dtypes.bfloat16
_CORES = list(range(NCORES))


def _mk_masks():
    masks = np.zeros((P, 4, 512), dtype=np.float32)
    for j in range(4):
        for p in range(P):
            masks[p, j][np.arange(512) < (128 * j + p)] = NEG
    return masks


def kernel(**inputs):
    f32 = np.float32
    x = np.ascontiguousarray(np.asarray(inputs["L0_attn_x"], dtype=f32)[0])
    masks = _mk_masks()
    eye32 = np.eye(32, dtype=f32)
    ident = np.eye(P, dtype=_BF)
    cos4, sin4 = host_rope_tables(T, 128, 4)
    cos64, sin64 = host_rope_tables(T, 64, 1)
    bcselT = np.zeros((P, 2), dtype=_BF); bcselT[0:64, 0] = 1; bcselT[64:128, 1] = 1
    bcself = np.zeros((2, P), dtype=f32); bcself[0, 0:64] = 1; bcself[1, 64:128] = 1
    wgu = inputs["L0_ffn_w_gate_up"]
    wpr = inputs["L0_ffn_w_proj"]
    amps = inputs["L1_ffn_amplitudes"]

    nc = build_mega()
    im = []
    for c in range(NCORES):
        sl = slice(c * TPC, (c + 1) * TPC)
        hs = slice(2 * c * HD, (2 * c + 2) * HD)
        im.append({
            "xT": np.ascontiguousarray(x[sl].T),
            "wq0": inputs["L0_attn_Wq"], "wkv0": inputs["L0_attn_Wkv"],
            "cos4": np.ascontiguousarray(cos4[sl]),
            "sin4": np.ascontiguousarray(sin4[sl]),
            "wgu": wgu[c * D:(c + 1) * D], "wpr": wpr[c * 256:(c + 1) * 256],
            "bcselT": bcselT, "bcself": bcself,
            "wout0": inputs["L0_attn_Wout"],
            "wq1": inputs["L1_attn_Wq"], "wkv1": inputs["L1_attn_Wkv"],
            "wqr": inputs["L1_attn_Wqr"], "wkr": inputs["L1_attn_Wkr"],
            "cos64": np.ascontiguousarray(cos64[sl]),
            "sin64": np.ascontiguousarray(sin64[sl]),
            "ident": ident, "masks": masks, "eye32": eye32,
            "wkup": np.ascontiguousarray(inputs["L1_attn_Wkup"][:, hs]),
            "wvup": np.ascontiguousarray(inputs["L1_attn_Wvup"][:, hs]),
            "wo1": inputs["L1_attn_Wo"],
            "wrt": inputs["L1_ffn_w_router"], "wgate": inputs["L1_ffn_w_gate"],
            "wup": inputs["L1_ffn_w_up"], "wdown": inputs["L1_ffn_w_down"],
            "amp0": np.ascontiguousarray(amps[:, :, 0]),
            "amp1": np.ascontiguousarray(amps[:, :, 1]),
            "es": np.ascontiguousarray(inputs["L1_ffn_expert_scale"].reshape(1, NE)),
            "wsg": inputs["L1_ffn_shared_w_gate"],
            "wsu": inputs["L1_ffn_shared_w_up"],
            "wsd": inputs["L1_ffn_shared_w_down"],
        })
    r = run_bass_kernel_spmd(nc, im, core_ids=_CORES).results

    out = np.empty((T, D), dtype=f32)
    for c in range(NCORES):
        out[c * TPC:(c + 1) * TPC] = np.asarray(r[c]["x4T"]).reshape(D, TPC).T
    return out.reshape(1, T, D)



# version 05
# speedup vs baseline: 1.5012x; 1.5012x over previous
import numpy as np
import ml_dtypes
from contextlib import ExitStack
import concourse.bass as bass
import concourse.mybir as mybir
from concourse.bass_utils import run_bass_kernel_spmd



"""Launch builders for mk_seq_preln_d2 (dev copy; later inlined into kernel.py)."""

F32 = mybir.dt.float32
BF16 = mybir.dt.bfloat16
AF = mybir.ActivationFunctionType
ALU = mybir.AluOpType
AX = mybir.AxisListType

P = 128
D = 2048
T = 8192
NCORES = 8
TPC = T // NCORES
DCH = D // P
TS = TPC // P
NH = 16
HD = 128
HPC = NH // NCORES
LN_EPS = 1e-5
RMS_EPS = float(np.finfo(np.float32).eps)
SCALE0 = 1.0 / float(np.sqrt(128.0))
SCALE1 = 1.0 / float(np.sqrt(192.0))
NEG = -1e30
ROLES = ("gpsimd", "sync", "tensor", "vector", "scalar")


class Sem:
    def __init__(self, h):
        self.h = h
        self.n = 0
        self.base = 0


class SemView:
    def __init__(self, phys):
        self.phys = phys
        self.base = phys.n
        self.h = phys.h

    @property
    def n(self):
        return self.phys.n - self.base

    @n.setter
    def n(self, val):
        self.phys.n = self.base + val


class Em:
    def __init__(self, role, eng, sems):
        self.role = role
        self.eng = eng
        self.sems = sems
        self.pre = {}

    def set_pre(self, waits):
        for e in ROLES:
            self.pre[e] = list(waits)

    def do(self, engine, fn, waits=(), inc=None, k=1):
        pw = self.pre.pop(engine, ())
        if engine == self.role:
            for (s, v) in list(pw) + list(waits):
                if v > 0:
                    self.eng.wait_ge(s.h, getattr(s, "base", 0) + v)
            ins = fn(self.eng)
            if inc is not None:
                ins.then_inc(inc.h, k)
        if inc is not None:
            inc.n += k

    def wait(self, engine, waits):
        if engine == self.role:
            for (s, v) in waits:
                if v > 0:
                    self.eng.wait_ge(s.h, getattr(s, "base", 0) + v)


def run_walk(nc, block, sems, walk):
    import functools
    for role in ROLES:
        for s in sems.values():
            s.n = 0
        def body(eng, role=role):
            walk(Em(role, eng, sems))
        getattr(block, role)(body)


L0PRE_SEMS = ["ln_sqd", "ln_mmst", "ln_rowd", "ln_rstd", "ln_bcd", "ln_nrm",
              "ld", "wql0", "wql1", "mmq", "roped", "qout0", "qout1",
              "mmkv", "kvd", "kvout0", "kvout1", "fin", "nout",
              "trq", "tqcp", "trk", "tkcp"]


def sec_l0pre(nc, E, dram, shared=None, fused=False):
    xTd = dram["xT"]
    wqd = dram["wq"]
    wkvd = dram["wkv"]
    cosd = dram["cos4"]
    sind = dram["sin4"]
    qroped = dram["qrope"]
    ktd = dram["kt"]
    vtd = dram["vt"]
    qn2d = dram["qn2"]
    kn2d = dram["kn2"]
    if True:
        if shared is None:
            xT = E(nc.sbuf_tensor("xTs", [P, DCH, TPC], F32))
            xnT = E(nc.sbuf_tensor("xnT", [P, DCH, TPC], BF16))
        else:
            xT = shared["xT"]
            xnT = shared["xnT"]
        wqb = E(nc.sbuf_tensor("wqb", [P, 2, DCH, 512], BF16))
        wkvb = E(nc.sbuf_tensor("wkvb", [P, DCH, 256], BF16))
        cosb = E(nc.sbuf_tensor("cos", [P, TS, 512], F32))
        sinb = E(nc.sbuf_tensor("sin", [P, TS, 512], F32))
        shuf = E(nc.sbuf_tensor("shuf", [P, 512], F32))
        tmpA = E(nc.sbuf_tensor("tmpA", [P, 512], F32))
        tmpB = E(nc.sbuf_tensor("tmpB", [P, 512], F32))
        qtm = E(nc.sbuf_tensor("qtm", [P, 2, 512], BF16))
        sqscr = E(nc.sbuf_tensor("sqscr", [P, 512], F32))
        qn2b = E(nc.sbuf_tensor("qn2s", [P, TS, NH], F32))
        kn2b = E(nc.sbuf_tensor("kn2s", [P, TS, 1], F32))
        ktm = E(nc.sbuf_tensor("ktm", [P, 2, HD], BF16))
        vtm = E(nc.sbuf_tensor("vtm", [P, 2, HD], BF16))
        qpsA = E(nc.psum_tensor("qpsA", [P, 512], F32))
        qpsB = E(nc.psum_tensor("qpsB", [P, 512], F32))
        kvps = E(nc.psum_tensor("kvps", [P, 256], F32))
        lnaux = mk_ln_aux(nc, E)
        if fused:
            identq = E(nc.sbuf_tensor("identq", [P, P], BF16))
            qfm = E(nc.sbuf_tensor("qfm", [P, 2, P], BF16))
            kfm = E(nc.sbuf_tensor("kfm", [P, 2, P], BF16))
            tpsq = E(nc.psum_tensor("tpsq", [P, P], BF16))

        def walk(em, S):
            qps = (qpsA, qpsB)
            em.do("gpsimd", lambda g: g.dma_start(
                out=xT[:], in_=xTd.rearrange("(c p) n -> p c n", p=P)), inc=S["ld"], k=16)
            em.do("gpsimd", lambda g: g.dma_start(
                out=wkvb[:], in_=wkvd.rearrange("(c p) n -> p c n", p=P)), inc=S["ld"], k=16)
            em.do("gpsimd", lambda g: g.dma_start(
                out=cosb[:], in_=cosd.rearrange("(s p) n -> p s n", p=P)), inc=S["ld"], k=16)
            em.do("gpsimd", lambda g: g.dma_start(
                out=sinb[:], in_=sind.rearrange("(s p) n -> p s n", p=P)), inc=S["ld"], k=16)
            if fused:
                em.do("gpsimd", lambda g: g.dma_start(
                    out=identq[:], in_=dram["ident"]), inc=S["ld"], k=16)
            LDA = S["ld"].n
            em.do("vector", lambda v: v.memset(lnaux["ones_col"][:], 1.0))
            em.do("vector", lambda v: v.memset(lnaux["ones_colb"][:], 1.0))
            em.do("vector", lambda v: v.memset(lnaux["ones_row"][:], 1.0))
            em.do("vector", lambda v: v.memset(lnaux["eps_b"][:], LN_EPS), inc=S["fin"])

            emit_ln(em, S, xT, xnT, lnaux,
                    pre_waits=[(S["ld"], LDA), (S["fin"], 1)])

            nchunks = 0
            for fc in range(4):
                for cpart in range(4):
                    csl = slice(cpart * 4, (cpart + 1) * 4)
                    em.do("gpsimd", lambda g, fc=fc, csl=csl: g.dma_start(
                        out=wqb[:, fc % 2, csl, :],
                        in_=wqd.rearrange("(c p) f -> p c f", p=P)[
                            :, csl, fc * 512:(fc + 1) * 512]),
                        waits=([(S["mmq"], max(0, 8 * (fc - 1)))] if cpart == 0 else []),
                        inc=S["wql%d" % (fc % 2)], k=16)
                for ts in range(TS):
                    idx = fc * TS + ts
                    tsl = slice(ts * P, (ts + 1) * P)
                    ps = qps[idx % 2]
                    def mmq(t, fc=fc, ts=ts, tsl=tsl, ps=ps):
                        last = None
                        for c in range(DCH):
                            last = t.matmul(ps[:], xnT[:, c, tsl], wqb[:, fc % 2, c, :],
                                            start=(c == 0), stop=(c == DCH - 1),
                                            skip_group_check=True)
                        return last
                    em.do("tensor", mmq,
                          waits=[(S["wql%d" % (fc % 2)], 64 * (fc // 2 + 1)), (S["ln_nrm"], 32),
                                 (S["roped"], max(0, idx - 1))],
                          inc=S["mmq"])
                    em.do("vector", lambda v, ps=ps: v.tensor_copy(
                        shuf[:].rearrange("p (h a b) -> p h a b", a=2, b=64)[:, :, 0, :],
                        ps[:].rearrange("p (h a b) -> p h a b", a=2, b=64)[:, :, 1, :]),
                        waits=[(S["mmq"], idx + 1)] +
                              ([(S["trq"], max(0, 4 * (idx - 1)))] if fused else
                               [(S["qout%d" % (idx % 2)], 16 * (idx // 2))]))
                    em.do("vector", lambda v, ps=ps: v.tensor_copy(
                        shuf[:].rearrange("p (h a b) -> p h a b", a=2, b=64)[:, :, 1, :],
                        ps[:].rearrange("p (h a b) -> p h a b", a=2, b=64)[:, :, 0, :]))
                    em.do("vector", lambda v, ps=ps, ts=ts: v.tensor_mul(
                        tmpA[:], ps[:], cosb[:, ts, :]))
                    em.do("vector", lambda v, ts=ts: v.tensor_mul(
                        tmpB[:], shuf[:], sinb[:, ts, :]))
                    em.do("vector", lambda v, idx=idx: v.tensor_add(
                        qtm[:, idx % 2, :], tmpA[:], tmpB[:]))
                    em.do("vector", lambda v, idx=idx: v.tensor_mul(
                        sqscr[:], qtm[:, idx % 2, :], qtm[:, idx % 2, :]))
                    em.do("vector", lambda v, ts=ts, fc=fc: v.reduce_sum(
                        qn2b[:, ts, fc * 4:(fc + 1) * 4],
                        sqscr[:].rearrange("p (h f) -> p h f", f=P), axis=AX.X),
                        inc=S["roped"])
                    if not fused:
                        em.do("gpsimd", lambda g, idx=idx, ts=ts, fc=fc: g.dma_start(
                            out=qroped[ts * P:(ts + 1) * P, fc * 512:(fc + 1) * 512],
                            in_=qtm[:, idx % 2, :]),
                            waits=[(S["roped"], idx + 1)], inc=S["qout%d" % (idx % 2)], k=16)
                    else:
                        for j in range(4):
                            qi = idx * 4 + j
                            em.do("tensor", lambda t, idx=idx, j=j: t.transpose(
                                tpsq[:], qtm[:, idx % 2, j * P:(j + 1) * P], identq[:]),
                                waits=[(S["roped"], idx + 1),
                                       (S["tqcp"], max(0, qi))],
                                inc=S["trq"])
                            em.do("vector", lambda v, qi=qi: v.tensor_copy(
                                qfm[:, qi % 2, :], tpsq[:]),
                                waits=[(S["trq"], S["trq"].n),
                                       (S["qout%d" % (qi % 2)], 16 * (qi // 2))],
                                inc=S["tqcp"])
                            em.do("gpsimd", lambda g, fc=fc, j=j, ts=ts, qi=qi: g.dma_start(
                                out=qroped[fc * 512 + j * P:fc * 512 + (j + 1) * P,
                                           ts * P:(ts + 1) * P],
                                in_=qfm[:, qi % 2, :]),
                                waits=[(S["tqcp"], S["tqcp"].n)],
                                inc=S["qout%d" % (qi % 2)], k=16)
                    nchunks += 1

            for ts in range(TS):
                tsl = slice(ts * P, (ts + 1) * P)
                def mmkv(t, tsl=tsl):
                    last = None
                    for c in range(DCH):
                        last = t.matmul(kvps[:], xnT[:, c, tsl], wkvb[:, c, :],
                                        start=(c == 0), stop=(c == DCH - 1),
                                        skip_group_check=True)
                    return last
                em.do("tensor", mmkv,
                      waits=[(S["ld"], LDA), (S["ln_nrm"], 32),
                             (S["kvd"], max(0, ts))],
                      inc=S["mmkv"])
                em.do("vector", lambda v: v.tensor_copy(
                    shuf[:, 0:64], kvps[:, 64:128]),
                    waits=[(S["mmkv"], ts + 1),
                           (S["kvout%d" % (ts % 2)], 32 * (ts // 2))] +
                          ([(S["trk"], max(0, ts - 1))] if fused else []))
                em.do("vector", lambda v: v.tensor_copy(
                    shuf[:, 64:128], kvps[:, 0:64]))
                em.do("vector", lambda v, ts=ts: v.tensor_mul(
                    tmpA[:, 0:128], kvps[:, 0:128], cosb[:, ts, 0:128]))
                em.do("vector", lambda v, ts=ts: v.tensor_mul(
                    tmpB[:, 0:128], shuf[:, 0:128], sinb[:, ts, 0:128]))
                em.do("vector", lambda v, ts=ts: v.tensor_add(
                    ktm[:, ts % 2, :], tmpA[:, 0:128], tmpB[:, 0:128]))
                em.do("vector", lambda v, ts=ts: v.tensor_copy(
                    vtm[:, ts % 2, :], kvps[:, 128:256]))
                em.do("vector", lambda v, ts=ts: v.tensor_mul(
                    sqscr[:, 0:128], ktm[:, ts % 2, :], ktm[:, ts % 2, :]))
                em.do("vector", lambda v, ts=ts: v.reduce_sum(
                    kn2b[:, ts, :], sqscr[:, 0:128], axis=AX.X), inc=S["kvd"])
                if not fused:
                    em.do("gpsimd", lambda g, ts=ts: g.dma_start(
                        out=ktd[ts * P:(ts + 1) * P, :], in_=ktm[:, ts % 2, :]),
                        waits=[(S["kvd"], ts + 1)], inc=S["kvout%d" % (ts % 2)], k=16)
                else:
                    em.do("tensor", lambda t, ts=ts: t.transpose(
                        tpsq[:], ktm[:, ts % 2, :], identq[:]),
                        waits=[(S["kvd"], ts + 1), (S["tkcp"], max(0, ts))] +
                              ([(S["tqcp"], S["tqcp"].n)] if ts == 0 else []),
                        inc=S["trk"])
                    em.do("vector", lambda v, ts=ts: v.tensor_copy(
                        kfm[:, ts % 2, :], tpsq[:]),
                        waits=[(S["trk"], S["trk"].n),
                               (S["kvout%d" % (ts % 2)], 32 * (ts // 2))],
                        inc=S["tkcp"])
                    em.do("gpsimd", lambda g, ts=ts: g.dma_start(
                        out=ktd[0:HD, ts * P:(ts + 1) * P], in_=kfm[:, ts % 2, :]),
                        waits=[(S["tkcp"], S["tkcp"].n)],
                        inc=S["kvout%d" % (ts % 2)], k=16)
                em.do("gpsimd", lambda g, ts=ts: g.dma_start(
                    out=vtd[ts * P:(ts + 1) * P, :], in_=vtm[:, ts % 2, :]),
                    inc=S["kvout%d" % (ts % 2)], k=16)

            if not fused:
                em.do("gpsimd", lambda g: g.dma_start(
                    out=qn2d, in_=qn2b[:]),
                    waits=[(S["roped"], 32)], inc=S["nout"], k=16)
            else:
                for s_ in range(TS):
                    em.do("gpsimd", lambda g, s_=s_: g.dma_start(
                        out=qn2d.rearrange("h (s p) -> s p h", p=P)[s_],
                        in_=qn2b[:, s_, :], allow_slow_non_contiguous=True),
                        waits=[(S["roped"], 32)], inc=S["nout"], k=16)
            em.do("gpsimd", lambda g: g.dma_start(
                out=(kn2d.rearrange("(s p) o -> p s o", p=P) if fused else kn2d),
                in_=(kn2b[:] if fused else kn2b[:, :, 0]),
                allow_slow_non_contiguous=fused),
                waits=[(S["kvd"], 8)], inc=S["nout"], k=16)
        def done(S):
            return [(S["qout0"], S["qout0"].n), (S["qout1"], S["qout1"].n),
                    (S["kvout0"], S["kvout0"].n), (S["kvout1"], S["kvout1"].n),
                    (S["nout"], S["nout"].n)]
        return walk, done


def build_l0pre():
    nc = bass.Bass("TRN2", target_bir_lowering=False, debug=True, detect_race_conditions=False)
    dram = {
        "xT": nc.dram_tensor("xT", [D, TPC], F32, kind="ExternalInput").ap(),
        "wq": nc.dram_tensor("wq", [D, D], F32, kind="ExternalInput").ap(),
        "wkv": nc.dram_tensor("wkv", [D, 256], F32, kind="ExternalInput").ap(),
        "cos4": nc.dram_tensor("cos4", [TPC, 512], F32, kind="ExternalInput").ap(),
        "sin4": nc.dram_tensor("sin4", [TPC, 512], F32, kind="ExternalInput").ap(),
        "qrope": nc.dram_tensor("qrope", [TPC, D], BF16, kind="ExternalOutput").ap(),
        "kt": nc.dram_tensor("kt", [TPC, HD], BF16, kind="ExternalOutput").ap(),
        "vt": nc.dram_tensor("vt", [TPC, HD], BF16, kind="ExternalOutput").ap(),
        "qn2": nc.dram_tensor("qn2", [P, TS, NH], F32, kind="ExternalOutput").ap(),
        "kn2": nc.dram_tensor("kn2", [P, TS], F32, kind="ExternalOutput").ap(),
    }
    with ExitStack() as _ctx:
        walk, done = sec_l0pre(nc, _ctx.enter_context, dram)
        block = _ctx.enter_context(nc.Block())
        sems = {n: Sem(nc.alloc_semaphore(n)) for n in L0PRE_SEMS}

        def top(em):
            walk(em, sems)
            em.wait("gpsimd", done(sems))

        run_walk(nc, block, sems, top)
    return nc


def host_rope_tables(n, dim, rep):
    inv = 1.0 / (10000.0 ** (np.arange(0, dim, 2, dtype=np.float32) / np.float32(dim)))
    t = np.arange(n, dtype=np.float32)
    f = np.outer(t, inv).astype(np.float32)
    pos = np.concatenate([f, f], axis=1)
    cos = np.cos(pos).astype(np.float32)
    sin = np.sin(pos).astype(np.float32)
    half = dim // 2
    sin_signed = sin.copy()
    sin_signed[:, :half] = -sin[:, half:]
    sin_signed = sin.copy()
    sin_signed[:, :half] *= -1.0
    cosr = np.tile(cos, (1, rep))
    sinr = np.tile(sin_signed, (1, rep))
    return np.ascontiguousarray(cosr), np.ascontiguousarray(sinr)



def emit_ln(em, S, xT, xnT, aux, pre_waits=(), tag=""):
    a = aux
    first = {"scalar": True, "tensor": True, "vector": True}
    def pw(eng):
        if first[eng]:
            first[eng] = False
            return list(pre_waits)
        return []
    for th in range(2):
        cols = slice(th * 512, (th + 1) * 512)
        for c in range(DCH):
            idx = th * DCH + c
            em.do("scalar", lambda s, c=c, cols=cols, idx=idx: s.activation(
                a["sqb"][:, idx % 2, :], xT[:, c, cols], AF.Square),
                waits=pw("scalar") + [(S["ln_mmst"], max(0, 2 * (idx - 1)))],
                inc=S["ln_sqd"])
            em.do("tensor", lambda t, c=c, cols=cols: t.matmul(
                a["ssum"][:], a["ones_col"][:], xT[:, c, cols],
                start=(c == 0), stop=(c == DCH - 1), skip_group_check=True),
                waits=pw("tensor") + [(S["ln_rowd"], 2 * th)],
                inc=S["ln_mmst"])
            em.do("tensor", lambda t, c=c, idx=idx: t.matmul(
                a["ssq"][:], a["ones_colb"][:], a["sqb"][:, idx % 2, :],
                start=(c == 0), stop=(c == DCH - 1), skip_group_check=True),
                waits=[(S["ln_sqd"], idx + 1)], inc=S["ln_mmst"])
        em.do("vector", lambda v: v.tensor_single_scalar(
            a["mrow"][:], a["ssum"][:], 1.0 / D, ALU.mult),
            waits=pw("vector") + [(S["ln_mmst"], 32 * (th + 1))], inc=S["ln_rowd"])
        em.do("vector", lambda v: v.tensor_mul(a["m2row"][:], a["mrow"][:], a["mrow"][:]))
        em.do("vector", lambda v: v.tensor_single_scalar(
            a["vrow"][:], a["ssq"][:], 1.0 / D, ALU.mult))
        em.do("vector", lambda v: v.tensor_sub(a["vrow"][:], a["vrow"][:], a["m2row"][:]),
              inc=S["ln_rowd"])
        em.do("scalar", lambda s: s.activation(
            a["vrow"][:], a["vrow"][:], AF.Ln, bias=a["eps_b"][:]),
            waits=[(S["ln_rowd"], 2 * (th + 1))])
        em.do("scalar", lambda s: s.activation(
            a["rrow"][:], a["vrow"][:], AF.Exp, scale=-0.5), inc=S["ln_rstd"])
        em.do("tensor", lambda t: t.matmul(
            a["bcm"][:], a["ones_row"][:], a["mrow"][:], start=True, stop=True,
            skip_group_check=True),
            waits=[(S["ln_rstd"], th + 1), (S["ln_nrm"], 16 * th)], inc=S["ln_bcd"])
        em.do("tensor", lambda t: t.matmul(
            a["bcr"][:], a["ones_row"][:], a["rrow"][:], start=True, stop=True,
            skip_group_check=True), inc=S["ln_bcd"])
        for c in range(DCH):
            em.do("vector", lambda v, c=c, cols=cols: v.tensor_sub(
                a["tmpA"][:], xT[:, c, cols], a["bcm"][:]),
                waits=[(S["ln_bcd"], 2 * (th + 1))])
            em.do("vector", lambda v, c=c, cols=cols: v.tensor_mul(
                xnT[:, c, cols], a["tmpA"][:], a["bcr"][:]), inc=S["ln_nrm"])


LN_SEMS = ["ln_sqd", "ln_mmst", "ln_rowd", "ln_rstd", "ln_bcd", "ln_nrm"]


def mk_ln_aux(nc, E):
    d = _mk_ln_aux_inner(nc, E)
    d["m2row"] = d["rrow"]
    return d


def _mk_ln_aux_inner(nc, E):
    return {
        "ones_col": E(nc.sbuf_tensor("lnx_ones_col", [P, 1], F32)),
        "ones_colb": E(nc.sbuf_tensor("lnx_ones_colb", [P, 1], BF16)),
        "ones_row": E(nc.sbuf_tensor("lnx_ones_row", [1, P], F32)),
        "eps_b": E(nc.sbuf_tensor("lnx_eps", [1, 1], F32)),
        "sq": E(nc.sbuf_tensor("lnx_sq", [P, 2, 512], F32)),
        "sqb": E(nc.sbuf_tensor("lnx_sqb", [P, 2, 512], BF16)),
        "mrow": E(nc.sbuf_tensor("lnx_mrow", [1, 512], F32)),
        "vrow": E(nc.sbuf_tensor("lnx_vrow", [1, 512], F32)),
        "rrow": E(nc.sbuf_tensor("lnx_rrow", [1, 512], F32)),
        "tmpA": E(nc.sbuf_tensor("lnx_tmpA", [P, 512], F32)),
        "ssum": E(nc.psum_tensor("lnx_ssum", [1, 512], F32)),
        "ssq": E(nc.psum_tensor("lnx_ssq", [1, 512], F32)),
        "bcm": E(nc.psum_tensor("lnx_bcm", [P, 512], F32)),
        "bcr": E(nc.psum_tensor("lnx_bcr", [P, 512], F32)),
    }


def emit_ln_consts(em, S, aux):
    em.do("vector", lambda v: v.memset(aux["ones_col"][:], 1.0))
    em.do("vector", lambda v: v.memset(aux["ones_row"][:], 1.0))
    em.do("vector", lambda v: v.memset(aux["eps_b"][:], LN_EPS), inc=S["consts"])



ATTN_SEMS = ["at_pre", "at_qmr", "at_expd", "at_lavd", "at_biasd", "at_maskd",
             "at_lpostd", "at_bcd", "at_lbcd", "at_evacd", "at_smm"]


def emit_attn_core(em, S, tl, L1, kv_ready_waits=()):
    scale = SCALE1 if L1 else SCALE0
    NQT = T // 512
    sps = tl["sps"]
    avps = tl["avps"]
    lps, bcps, smallps = tl["lps"], tl["bcps"], tl["smallps"]
    qTh, kT, vt, qn2 = tl["qTh"], tl["kT"], tl["vt"], tl["qn2"]
    masks, eye32 = tl["masks"], tl["eye32"]
    pT, biascol, selrow = tl["pT"], tl["biascol"], tl["selrow"]
    linv, lbc = tl["linv"], tl["lbc"]
    qm, kbc, mneg = tl["qm"], tl["kbc"], tl["mneg"]
    ones_colb, ones_row, ones_r32 = tl["ones_colb"], tl["ones_row"], tl["ones_r32"]

    em.do("vector", lambda v: v.reduce_max(qm[:], qn2[:], axis=AX.X),
          waits=list(kv_ready_waits), inc=S["at_qmr"])
    em.do("scalar", lambda s: s.activation(tl["maxk2"][:], tl["maxk2"][:], AF.Sqrt),
          waits=list(kv_ready_waits), inc=S["at_pre"])
    em.do("scalar", lambda s: s.activation(qm[:], qm[:], AF.Sqrt),
          waits=[(S["at_qmr"], 1)], inc=S["at_pre"])
    em.do("tensor", lambda t: t.matmul(
        smallps[0:32, 0:1], ones_r32[:], tl["maxk2"][:],
        start=True, stop=True, skip_group_check=True),
        waits=[(S["at_pre"], S["at_pre"].n)], inc=S["at_smm"])
    em.do("vector", lambda v: v.tensor_copy(kbc[:], smallps[0:32, 0:1]),
          waits=[(S["at_smm"], S["at_smm"].n)])
    em.do("vector", lambda v: v.tensor_mul(mneg[:], qm[:], kbc[:]))
    em.do("vector", lambda v: v.tensor_single_scalar(
        mneg[:], mneg[:], -scale, ALU.mult), inc=S["at_pre"])
    PREALL = S["at_pre"].n

    gidx = 0
    hq = 0
    glast = {}
    for h in range(HPC):
        for qt in range(NQT):
            nb = qt * 4 + 4
            em.do("tensor", lambda t, h=h, qt=qt, hq=hq: t.matmul(
                smallps[0:1, 1:2], eye32[:, h * NQT + qt:h * NQT + qt + 1],
                mneg[:], start=True, stop=True, skip_group_check=True),
                waits=[(S["at_pre"], PREALL),
                       (S["at_biasd"], max(0, 2 * (hq - 1)))],
                inc=S["at_smm"])
            em.do("vector", lambda v, hq=hq: v.tensor_copy(
                selrow[:, hq % 2:hq % 2 + 1], smallps[0:1, 1:2]),
                waits=[(S["at_smm"], S["at_smm"].n),
                       (S["at_expd"], glast.get(hq - 2, 0))],
                inc=S["at_biasd"])
            em.do("tensor", lambda t, hq=hq: t.matmul(
                smallps[:, 2 + hq % 2:3 + hq % 2], ones_row[:],
                selrow[:, hq % 2:hq % 2 + 1],
                start=True, stop=True, skip_group_check=True),
                waits=[(S["at_biasd"], S["at_biasd"].n)], inc=S["at_smm"])
            em.do("vector", lambda v, hq=hq: v.tensor_copy(
                biascol[:, hq % 2:hq % 2 + 1], smallps[:, 2 + hq % 2:3 + hq % 2]),
                waits=[(S["at_smm"], S["at_smm"].n)], inc=S["at_biasd"])
            BIASRDY = S["at_biasd"].n
            gidx0 = gidx
            for b in range(nb + 1):
                if b < nb:
                    par = gidx % 2
                    qsl = slice(qt * 512, (qt + 1) * 512)
                    ksl = slice(b * P, (b + 1) * P)
                    def smm(t, h=h, qsl=qsl, ksl=ksl, par=par):
                        if L1:
                            t.matmul(sps[par][:], kT[:, h, ksl], qTh[:, h, qsl],
                                     start=True, stop=False, skip_group_check=True)
                            return t.matmul(sps[par][:], tl["krT"][0:64, ksl],
                                            tl["qrT"][0:64, qsl],
                                            start=False, stop=True, skip_group_check=True)
                        return t.matmul(sps[par][:], kT[:, 0, ksl], qTh[:, h, qsl],
                                        start=True, stop=True, skip_group_check=True)
                    em.do("tensor", smm,
                          waits=[(S["at_expd"], max(0, gidx - 1))] +
                                (list(kv_ready_waits) if gidx == 0 else []),
                          inc=S["at_smm"])
                    SMMN = S["at_smm"].n
                    is_diag = b >= qt * 4
                    if is_diag:
                        em.do("vector", lambda v, b=b, qt=qt, par=par: v.tensor_add(
                            sps[par][:], sps[par][:], masks[:, b - qt * 4, :]),
                            waits=[(S["at_smm"], SMMN)], inc=S["at_maskd"])
                    MASKN = S["at_maskd"].n
                    em.do("scalar", lambda s, par=par, hq=hq: s.activation(
                        pT[:, par, :], sps[par][:], AF.Exp,
                        bias=biascol[:, hq % 2:hq % 2 + 1], scale=scale),
                        waits=[(S["at_smm"], SMMN), (S["at_maskd"], MASKN),
                               (S["at_biasd"], BIASRDY),
                               (S["at_lavd"], max(0, 2 * (gidx - 1)))],
                        inc=S["at_expd"])
                    gidx += 1
                if b > 0:
                    bb = b - 1
                    gb = gidx0 + bb
                    parb = gb % 2
                    em.do("tensor", lambda t, parb=parb, hq=hq, bb=bb, nb=nb: t.matmul(
                        lps[0:1, (hq % 2) * 512:(hq % 2 + 1) * 512],
                        ones_colb[:], pT[:, parb, :],
                        start=(bb == 0), stop=(bb == nb - 1), skip_group_check=True),
                        waits=[(S["at_expd"], gb + 1)] +
                              ([(S["at_lpostd"], max(0, hq - 1)),
                                (S["at_evacd"], max(0, hq - 1))] if bb == 0 else []),
                        inc=S["at_lavd"])
                    def avmm(t, parb=parb, hq=hq, bb=bb, nb=nb, h=h):
                        vsl = vt[:, bb, h * HD:(h + 1) * HD] if L1 else vt[:, bb, :]
                        return t.matmul(avps[hq % 2][:], vsl, pT[:, parb, :],
                                        start=(bb == 0), stop=(bb == nb - 1),
                                        skip_group_check=True)
                    em.do("tensor", avmm, inc=S["at_lavd"])
            glast[hq] = gidx
            em.do("vector", lambda v, hq=hq: v.reciprocal(
                linv[:, hq % 2, :], lps[0:1, (hq % 2) * 512:(hq % 2 + 1) * 512]),
                waits=[(S["at_lavd"], 2 * gidx)], inc=S["at_lpostd"])
            em.do("tensor", lambda t, hq=hq: t.matmul(
                bcps[:], ones_row[:], linv[:, hq % 2, :],
                start=True, stop=True, skip_group_check=True),
                waits=[(S["at_lpostd"], hq + 1), (S["at_lbcd"], max(0, hq))],
                inc=S["at_smm"])
            BCN = S["at_smm"].n
            em.do("vector", lambda v, hq=hq: v.tensor_copy(
                lbc[:, hq % 2, :], bcps[:]),
                waits=[(S["at_smm"], BCN)], inc=S["at_lbcd"])
            em.do("vector", lambda v, hq=hq, h=h, qt=qt: v.tensor_mul(
                tl["attnout"][:, h, qt * 512:(qt + 1) * 512], avps[hq % 2][:],
                lbc[:, hq % 2, :]),
                waits=[(S["at_lbcd"], hq + 1)], inc=S["at_evacd"])
            hq += 1


def mk_attn_tiles(nc, E, L1):
    HK = HPC if L1 else 1
    NKB = T // P
    tl = {}
    tl["qTh"] = E(nc.sbuf_tensor("at_qTh", [P, HPC, T], BF16))
    tl["kT"] = E(nc.sbuf_tensor("at_kT", [P, HK, T], BF16))
    tl["vt"] = E(nc.sbuf_tensor("at_vt", [P, NKB, HK * HD], BF16))
    tl["qn2"] = E(nc.sbuf_tensor("at_qn2", [32, 512], F32))
    tl["masks"] = E(nc.sbuf_tensor("at_masks", [P, 4, 512], F32))
    tl["eye32"] = E(nc.sbuf_tensor("at_eye32", [32, 32], F32))
    tl["pT"] = E(nc.sbuf_tensor("at_pT", [P, 2, 512], BF16))
    tl["ones_colb"] = E(nc.sbuf_tensor("at_onescb", [P, 1], BF16))
    tl["ones_row"] = E(nc.sbuf_tensor("at_onesrf", [1, P], F32))
    tl["ones_r32"] = E(nc.sbuf_tensor("at_ones32", [1, 32], F32))
    tl["maxk2"] = E(nc.sbuf_tensor("at_maxk2", [1, 1], F32))
    tl["qm"] = E(nc.sbuf_tensor("at_qm", [32, 1], F32))
    tl["kbc"] = E(nc.sbuf_tensor("at_kbc", [32, 1], F32))
    tl["mneg"] = E(nc.sbuf_tensor("at_mneg", [32, 1], F32))
    tl["selrow"] = E(nc.sbuf_tensor("at_selrow", [1, 2], F32))
    tl["biascol"] = E(nc.sbuf_tensor("at_biascol", [P, 2], F32))
    tl["linv"] = E(nc.sbuf_tensor("at_linv", [1, 2, 512], F32))
    tl["lbc"] = E(nc.sbuf_tensor("at_lbc", [P, 2, 512], F32))
    tl["sps"] = (E(nc.psum_tensor("at_sps0", [P, 512], F32)),
                 E(nc.psum_tensor("at_sps1", [P, 512], F32)))
    tl["avps"] = (E(nc.psum_tensor("at_avps0", [P, 512], F32)),
                  E(nc.psum_tensor("at_avps1", [P, 512], F32)))
    tl["lps"] = E(nc.psum_tensor("at_lps", [1, 1024], F32))
    tl["bcps"] = E(nc.psum_tensor("at_bcps", [P, 512], F32))
    tl["smallps"] = E(nc.psum_tensor("at_smallps", [P, 4], F32))
    return tl


ATTN0_SEMS = ATTN_SEMS + ["ld", "sout"]


def sec_attn0(nc, E, dram, qn2_rows=None):
    tl = mk_attn_tiles(nc, E, L1=False)
    kn2 = E(nc.sbuf_tensor("at_kn2", [1, T], F32))
    attnout = E(nc.sbuf_tensor("at_out", [P, HPC, T], BF16))
    tl["attnout"] = attnout

    def walk(em, S):
        def dma_in(out, in_):
            em.do("gpsimd", lambda g, out=out, in_=in_: g.dma_start(out=out, in_=in_),
                  inc=S["ld"], k=16)
        if isinstance(dram["qTh_src"], list):
            for s, ap in enumerate(dram["qTh_src"]):
                dma_in(tl["qTh"][:, :, s * TPC:(s + 1) * TPC], ap)
        else:
            dma_in(tl["qTh"][:], dram["qTh_src"])
        if isinstance(dram["kT_src"], list):
            for s, ap in enumerate(dram["kT_src"]):
                dma_in(tl["kT"][:, 0, s * TPC:(s + 1) * TPC], ap)
        else:
            dma_in(tl["kT"][:, 0, :], dram["kT_src"])
        dma_in(tl["vt"][:], dram["vt_src"])
        if qn2_rows is None:
            dma_in(tl["qn2"][:], dram["qn2_src"])
        else:
            for h in range(HPC):
                for qt in range(16):
                    r = h * 16 + qt
                    dma_in(tl["qn2"][r:r + 1, :], qn2_rows(h, qt))
        if isinstance(dram["kn2_src"], list):
            for s, ap in enumerate(dram["kn2_src"]):
                dma_in(kn2[0:1, s * TPC:(s + 1) * TPC], ap)
        else:
            dma_in(kn2[:], dram["kn2_src"])
        dma_in(tl["masks"][:], dram["masks"])
        dma_in(tl["eye32"][:], dram["eye32"])
        LDALL = S["ld"].n
        em.do("vector", lambda v: v.memset(tl["ones_colb"][:], 1.0))
        em.do("vector", lambda v: v.memset(tl["ones_row"][:], 1.0))
        em.do("vector", lambda v: v.memset(tl["ones_r32"][:], 1.0))
        em.do("vector", lambda v: v.reduce_max(tl["maxk2"][:], kn2[:], axis=AX.X),
              waits=[(S["ld"], LDALL)], inc=S["at_pre"])
        emit_attn_core(em, S, tl, L1=False,
                       kv_ready_waits=[(S["at_pre"], 1), (S["ld"], LDALL)])
        if isinstance(dram["attnout_dst"], list):
            for s, ap in enumerate(dram["attnout_dst"]):
                em.do("gpsimd", lambda g, s=s, ap=ap: g.dma_start(
                    out=ap, in_=attnout[:, :, s * TPC:(s + 1) * TPC]),
                    waits=[(S["at_evacd"], HPC * 16)], inc=S["sout"], k=16)
        else:
            em.do("gpsimd", lambda g: g.dma_start(
                out=dram["attnout_dst"], in_=attnout[:]),
                waits=[(S["at_evacd"], HPC * 16)], inc=S["sout"], k=16)

    def done(S):
        return [(S["sout"], S["sout"].n)]
    return walk, done


def build_attn(layer):
    assert layer == 0
    nc = bass.Bass("TRN2", target_bir_lowering=False, debug=True,
                   detect_race_conditions=False)
    qThd = nc.dram_tensor("qTh", [HPC * HD, T], BF16, kind="ExternalInput").ap()
    kTd = nc.dram_tensor("kT", [HD, T], BF16, kind="ExternalInput").ap()
    vtd = nc.dram_tensor("vt", [T, HD], BF16, kind="ExternalInput").ap()
    qn2d = nc.dram_tensor("qn2", [32, 512], F32, kind="ExternalInput").ap()
    kn2d = nc.dram_tensor("kn2", [1, T], F32, kind="ExternalInput").ap()
    maskd = nc.dram_tensor("masks", [P, 4, 512], F32, kind="ExternalInput").ap()
    eyed = nc.dram_tensor("eye32", [32, 32], F32, kind="ExternalInput").ap()
    outd = nc.dram_tensor("attnout", [HPC * HD, T], BF16, kind="ExternalOutput").ap()
    dram = {
        "qTh_src": qThd.rearrange("(h p) t -> p h t", p=P),
        "kT_src": kTd,
        "vt_src": vtd.rearrange("(b p) f -> p b f", p=P),
        "qn2_src": qn2d,
        "kn2_src": kn2d,
        "masks": maskd,
        "eye32": eyed,
        "attnout_dst": outd.rearrange("(h p) t -> p h t", p=P),
    }
    with ExitStack() as _ctx:
        walk, done = sec_attn0(nc, _ctx.enter_context, dram)
        block = _ctx.enter_context(nc.Block())
        sems = {n: Sem(nc.alloc_semaphore(n)) for n in ATTN0_SEMS}

        def top(em):
            walk(em, sems)
            em.wait("gpsimd", done(sems))

        run_walk(nc, block, sems, top)
    return nc




POST_SEMS = LN_SEMS + ["ld", "wl0", "wl1", "ymm", "resd", "xout", "nout"]


def sec_post(nc, E, dram, shared=None, load_x=True):
    aoTd = dram["aoT"]
    woutd = dram["wout"]
    if True:
        if shared is None:
            xT = E(nc.sbuf_tensor("xTs", [P, DCH, TPC], F32))
            xnT = E(nc.sbuf_tensor("xnTs", [P, DCH, TPC], BF16))
        else:
            xT = shared["xT"]
            xnT = shared["xnT"]
        aoT = E(nc.sbuf_tensor("aoTs", [P, DCH, TPC], BF16))
        wblk = E(nc.sbuf_tensor("wblks", [P, 2, DCH, P], BF16))
        yps0 = E(nc.psum_tensor("yps0", [P, 512], F32))
        yps1 = E(nc.psum_tensor("yps1", [P, 512], F32))
        lnaux = mk_ln_aux(nc, E)

        def walk(em, S):
            yps = (yps0, yps1)
            if load_x:
                em.do("gpsimd", lambda g: g.dma_start(
                    out=xT[:], in_=dram["xT"].rearrange("(c p) n -> p c n", p=P)),
                    inc=S["ld"], k=16)
            em.do("gpsimd", lambda g: g.dma_start(
                out=aoT[:], in_=aoTd.rearrange("(c p) n -> p c n", p=P)), inc=S["ld"], k=16)
            LD = S["ld"].n
            idx = 0
            for os_ in range(DCH):
                for cpart, csl in ((0, slice(0, 8)), (1, slice(8, DCH))):
                    em.do("gpsimd", lambda g, os_=os_, csl=csl: g.dma_start(
                        out=wblk[:, os_ % 2, csl, :],
                        in_=woutd.rearrange("(c p) f -> p c f", p=P)[
                            :, csl, os_ * P:(os_ + 1) * P]),
                        waits=([(S["ymm"], max(0, 2 * (os_ - 1)))] if cpart == 0 else []),
                        inc=S["wl%d" % (os_ % 2)], k=16)
                for th in range(2):
                    cols = slice(th * 512, (th + 1) * 512)
                    ps = yps[idx % 2]
                    def ymm(t, os_=os_, cols=cols, ps=ps):
                        last = None
                        for c in range(DCH):
                            last = t.matmul(ps[:], wblk[:, os_ % 2, c, :], aoT[:, c, cols],
                                            start=(c == 0), stop=(c == DCH - 1),
                                            skip_group_check=True)
                        return last
                    em.do("tensor", ymm,
                          waits=[(S["wl%d" % (os_ % 2)], 32 * (os_ // 2 + 1)),
                                 (S["ld"], LD), (S["resd"], max(0, idx - 1))],
                          inc=S["ymm"])
                    em.do("vector", lambda v, os_=os_, cols=cols, ps=ps: v.tensor_add(
                        xT[:, os_, cols], xT[:, os_, cols], ps[:]),
                        waits=[(S["ymm"], idx + 1)], inc=S["resd"])
                    idx += 1
            if "x1T" in dram:
                em.do("gpsimd", lambda g: g.dma_start(
                    out=dram["x1T"].rearrange("(c p) n -> p c n", p=P), in_=xT[:]),
                    waits=[(S["resd"], 32)], inc=S["xout"], k=16)
            em.do("vector", lambda v: v.memset(lnaux["ones_col"][:], 1.0))
            em.do("vector", lambda v: v.memset(lnaux["ones_colb"][:], 1.0))
            em.do("vector", lambda v: v.memset(lnaux["ones_row"][:], 1.0))
            em.do("vector", lambda v: v.memset(lnaux["eps_b"][:], 1e-5), inc=S["resd"])
            emit_ln(em, S, xT, xnT, lnaux,
                    pre_waits=[(S["resd"], 33)])
            if "xnT" in dram:
                em.do("gpsimd", lambda g: g.dma_start(
                    out=dram["xnT"].rearrange("(c p) n -> p c n", p=P), in_=xnT[:]),
                    waits=[(S["ln_nrm"], 32)], inc=S["nout"], k=16)

        def done(S):
            return [(S["xout"], S["xout"].n), (S["nout"], S["nout"].n),
                    (S["ln_nrm"], 32), (S["resd"], S["resd"].n)]
        return walk, done


def build_post():
    nc = bass.Bass("TRN2", target_bir_lowering=False, debug=True,
                   detect_race_conditions=False)
    dram = {
        "xT": nc.dram_tensor("xT", [D, TPC], F32, kind="ExternalInput").ap(),
        "aoT": nc.dram_tensor("aoT", [D, TPC], BF16, kind="ExternalInput").ap(),
        "wout": nc.dram_tensor("wout", [D, D], F32, kind="ExternalInput").ap(),
        "x1T": nc.dram_tensor("x1T", [D, TPC], F32, kind="ExternalOutput").ap(),
        "xnT": nc.dram_tensor("xnT", [D, TPC], BF16, kind="ExternalOutput").ap(),
    }
    with ExitStack() as _ctx:
        walk, done = sec_post(nc, _ctx.enter_context, dram)
        block = _ctx.enter_context(nc.Block())
        sems = {n: Sem(nc.alloc_semaphore(n)) for n in POST_SEMS}

        def top(em):
            walk(em, sems)
            em.wait("gpsimd", done(sems))

        run_walk(nc, block, sems, top)
    return nc


QUANT_SEMS = ["csel", "cl0", "cl1", "cst0", "cst1", "tl0", "tl1", "absd",
              "summ", "srd", "prepd", "bcmm", "dved", "qout0", "qout1"]


def sec_quant(nc, E, dram, shared=None):
    wgud = dram["wgu"]
    wprd = dram["wpr"]
    bcselTd = dram["bcselT"]
    bcselfd = dram["bcself"]
    wguTd = dram["wguT"]
    wprTd = dram["wprT"]
    wgub = nc.dram_tensor("wgub", [D, D], BF16).ap()
    wprb = nc.dram_tensor("wprb", [256, 4 * D], BF16).ap()
    if True:
        ldbuf = E(nc.sbuf_tensor("ldbuf", [P, 2, D], BF16))
        ldbufp = E(nc.sbuf_tensor("ldbufp", [P, 2, 4 * D], BF16))
        wt = E(nc.sbuf_tensor("wt", [P, 2, D], BF16))
        aw = E(nc.sbuf_tensor("aw", [P, D], BF16))
        tq = E(nc.sbuf_tensor("tq", [P, D], BF16))
        qa = E(nc.sbuf_tensor("qa", [P, D], BF16))
        qb = E(nc.sbuf_tensor("qb", [P, D], BF16))
        wq = E(nc.sbuf_tensor("wq2", [P, 2, D], BF16))
        srow = E(nc.sbuf_tensor("srow", [2, 512], BF16))
        sf32 = E(nc.sbuf_tensor("sf32", [2, 512], F32))
        r32 = E(nc.sbuf_tensor("r32", [2, 512], F32))
        bcselT = E(nc.sbuf_tensor("bcselTs", [P, 2], BF16))
        bcself = E(nc.sbuf_tensor("bcselfs", [2, P], F32))
        sumps = E(nc.psum_tensor("sumps", [2, 512], F32))
        sbps = E(nc.psum_tensor("sbps", [P, 512], F32))
        rbps = E(nc.psum_tensor("rbps", [P, 512], F32))

        def walk(em, S):
            em.do("gpsimd", lambda g: g.dma_start(out=bcselT[:], in_=bcselTd),
                  inc=S["csel"], k=16)
            em.do("gpsimd", lambda g: g.dma_start(out=bcself[:], in_=bcselfd),
                  inc=S["csel"], k=16)
            for i in range(DCH):
                em.do("gpsimd", lambda g, i=i: g.dma_start(
                    out=ldbuf[:, i % 2, :], in_=wgud[i * P:(i + 1) * P, :]),
                    waits=[(S["cst%d" % (i % 2)], 16 * (i // 2))],
                    inc=S["cl%d" % (i % 2)], k=16)
                em.do("sync", lambda g, i=i: g.dma_start(
                    out=wgub[i * P:(i + 1) * P, :], in_=ldbuf[:, i % 2, :]),
                    waits=[(S["cl%d" % (i % 2)], 16 * (i // 2 + 1))],
                    inc=S["cst%d" % (i % 2)], k=16)
            for i in range(2):
                em.do("gpsimd", lambda g, i=i: g.dma_start(
                    out=ldbufp[:, i, :], in_=wprd[i * P:(i + 1) * P, :]),
                    inc=S["cl%d" % (i % 2)], k=16)
                em.do("sync", lambda g, i=i: g.dma_start(
                    out=wprb[i * P:(i + 1) * P, :], in_=ldbufp[:, i, :]),
                    waits=[[(S["cl0"], S["cl0"].n), (S["cl1"], S["cl1"].n)][i]],
                    inc=S["cst%d" % (i % 2)], k=16)
            CST0, CST1 = S["cst0"].n, S["cst1"].n

            jobs = [("gu", c) for c in range(DCH)] + [("pr", c) for c in range(64)]
            dved_after = {}
            npar = {0: 0, 1: 0}
            for j, (kind, c) in enumerate(jobs):
                par = j % 2
                F = D if kind == "gu" else 256
                src = (wgub[:, c * P:(c + 1) * P] if kind == "gu"
                       else wprb[:, c * P:(c + 1) * P])
                em.do("sync", lambda g, src=src, par=par, F=F: g.dma_start(
                    out=wt[:, par, 0:F], in_=src, transpose=True),
                    waits=[(S["cst0"], CST0), (S["cst1"], CST1),
                           (S["dved"], dved_after.get(j - 2, 0))],
                    inc=S["tl%d" % par], k=16)
                em.do("scalar", lambda s, par=par, F=F: s.activation(
                    aw[:, 0:F], wt[:, par, 0:F], AF.Abs),
                    waits=[(S["tl%d" % par], 16 * (j // 2 + 1)),
                           (S["summ"], S["summ"].n)],
                    inc=S["absd"])
                nstr = max(1, F // 512)
                W = min(F, 512)
                for st in range(nstr):
                    sl = slice(st * W, (st + 1) * W)
                    em.do("tensor", lambda t, sl=sl, W=W: t.matmul(
                        sumps[:, 0:W], bcselT[:], aw[:, sl],
                        start=True, stop=True, skip_group_check=True),
                        waits=[(S["absd"], S["absd"].n), (S["csel"], 32),
                               (S["srd"], S["srd"].n)],
                        inc=S["summ"])
                    em.do("vector", lambda v, W=W: v.tensor_single_scalar(
                        srow[:, 0:W], sumps[:, 0:W], 1.0 / 64, ALU.mult),
                        waits=[(S["summ"], S["summ"].n),
                               (S["bcmm"], S["bcmm"].n)],
                        inc=S["srd"])
                    em.do("vector", lambda v, W=W: v.tensor_single_scalar(
                        srow[:, 0:W], srow[:, 0:W], 1e-8, ALU.max))
                    em.do("vector", lambda v, W=W: v.tensor_copy(
                        sf32[:, 0:W], srow[:, 0:W]))
                    em.do("vector", lambda v, W=W: v.reciprocal(
                        r32[:, 0:W], sf32[:, 0:W]), inc=S["prepd"])
                    em.do("tensor", lambda t, W=W: t.matmul(
                        sbps[:, 0:W], bcself[:], sf32[:, 0:W],
                        start=True, stop=True, skip_group_check=True),
                        waits=[(S["prepd"], S["prepd"].n)])
                    em.do("tensor", lambda t, W=W: t.matmul(
                        rbps[:, 0:W], bcself[:], r32[:, 0:W],
                        start=True, stop=True, skip_group_check=True),
                        inc=S["bcmm"])
                    BC = S["bcmm"].n
                    em.do("vector", lambda v, par=par, sl=sl, W=W: v.tensor_mul(
                        tq[:, sl], wt[:, par, sl], rbps[:, 0:W]),
                        waits=[(S["bcmm"], BC)], inc=S["dved"])
                    em.do("vector", lambda v, sl=sl: v.tensor_single_scalar(
                        qa[:, sl], tq[:, sl], 0.5, ALU.is_gt), inc=S["dved"])
                    em.do("vector", lambda v, sl=sl: v.tensor_single_scalar(
                        qb[:, sl], tq[:, sl], -0.5, ALU.is_lt), inc=S["dved"])
                    em.do("vector", lambda v, sl=sl: v.tensor_sub(
                        qa[:, sl], qa[:, sl], qb[:, sl]), inc=S["dved"])
                    nq = npar[par]
                    em.do("vector", lambda v, par=par, sl=sl, W=W: v.tensor_mul(
                        wq[:, par, sl], qa[:, sl], sbps[:, 0:W]),
                        waits=[(S["qout%d" % par], 16 * nq)], inc=S["dved"])
                dved_after[j] = S["dved"].n
                def qst(g, kind=kind, c=c, par=par, F=F):
                    dst = (wguTd[c * P:(c + 1) * P, :] if kind == "gu"
                           else wprTd[c * P:(c + 1) * P, :])
                    return g.dma_start(out=dst, in_=wq[:, par, 0:F])
                em.do("gpsimd", qst, waits=[(S["dved"], S["dved"].n)],
                      inc=S["qout%d" % par], k=16)
                npar[par] += 1

        def done(S):
            return [(S["qout0"], S["qout0"].n), (S["qout1"], S["qout1"].n)]
        return walk, done


def build_quant():
    nc = bass.Bass("TRN2", target_bir_lowering=False, debug=True,
                   detect_race_conditions=False)
    dram = {
        "wgu": nc.dram_tensor("wgu", [D, D], F32, kind="ExternalInput").ap(),
        "wpr": nc.dram_tensor("wpr", [256, 4 * D], F32, kind="ExternalInput").ap(),
        "bcselT": nc.dram_tensor("bcselT", [P, 2], BF16, kind="ExternalInput").ap(),
        "bcself": nc.dram_tensor("bcself", [2, P], F32, kind="ExternalInput").ap(),
        "wguT": nc.dram_tensor("wguT", [D, D], BF16, kind="ExternalOutput").ap(),
        "wprT": nc.dram_tensor("wprT", [4 * D, 256], BF16, kind="ExternalOutput").ap(),
    }
    with ExitStack() as _ctx:
        walk, done = sec_quant(nc, _ctx.enter_context, dram)
        block = _ctx.enter_context(nc.Block())
        sems = {n: Sem(nc.alloc_semaphore(n)) for n in QUANT_SEMS}

        def top(em):
            walk(em, sems)
            em.wait("gpsimd", done(sems))

        run_walk(nc, block, sems, top)
    return nc



H = 4 * D
NH64 = H // P


FFN0_SEMS = ["ld", "consts", "wl0", "wl1", "gmm", "sild", "hmd", "hcp",
             "sqd", "sqmm", "rstd", "rbcmm", "rbd", "hnd", "wpl",
             "ymm", "resd", "xout"]


def sec_l0ffn(nc, E, dram, shared=None, gu_src=None, pr_src=None):
    wguTd = dram.get("wguT")
    wprTd = dram.get("wprT")
    if gu_src is None:
        def gu_src(i, up, csl):
            base = H + i * P if up else i * P
            return wguTd.rearrange("(c p) h -> p c h", p=P)[:, csl, base:base + P]
    if pr_src is None:
        def pr_src(os_, csl):
            return wprTd.rearrange("(c p) o -> p c o", p=P)[:, csl, os_ * P:(os_ + 1) * P]
    if True:
        if shared is None:
            x1T = E(nc.sbuf_tensor("x1Ts", [P, DCH, TPC], F32))
            xnT = E(nc.sbuf_tensor("xnTs", [P, DCH, TPC], BF16))
        else:
            x1T = shared["xT"]
            xnT = shared["xnT"]
        hbuf = E(nc.sbuf_tensor("hbufs", [P, NH64, 512], BF16))
        wg = E(nc.sbuf_tensor("wgs", [P, 2, DCH, P], BF16))
        wu = E(nc.sbuf_tensor("wus", [P, 2, DCH, P], BF16))
        wp = E(nc.sbuf_tensor("wps", [P, 2, NH64 // 2, P], BF16))
        sg = E(nc.sbuf_tensor("sgs", [P, 512], F32))
        ht = E(nc.sbuf_tensor("hts", [P, 512], F32))
        sq = E(nc.sbuf_tensor("sqs", [P, 512], BF16))
        rb = E(nc.sbuf_tensor("rbs", [P, 512], F32))
        rrow = E(nc.sbuf_tensor("rrows", [1, 512], F32))
        epsr = E(nc.sbuf_tensor("epsrs", [1, 1], F32))
        ones_col = E(nc.sbuf_tensor("ones_cols", [P, 1], F32))
        ones_colb = E(nc.sbuf_tensor("ones_colbs", [P, 1], BF16))
        ones_row = E(nc.sbuf_tensor("ones_rows", [1, P], F32))
        gps = E(nc.psum_tensor("gps", [P, 512], F32))
        ups = E(nc.psum_tensor("ups", [P, 512], F32))
        yps0 = E(nc.psum_tensor("yps0", [P, 512], F32))
        yps1 = E(nc.psum_tensor("yps1", [P, 512], F32))
        ssq = E(nc.psum_tensor("ssq", [1, 512], F32))
        rbps = E(nc.psum_tensor("rbps", [P, 512], F32))

        def walk(em, S):
            yps = (yps0, yps1)
            if shared is None:
                em.do("gpsimd", lambda g: g.dma_start(
                    out=x1T[:], in_=dram["x1T"].rearrange("(c p) n -> p c n", p=P)),
                    inc=S["ld"], k=16)
                em.do("gpsimd", lambda g: g.dma_start(
                    out=xnT[:], in_=dram["xnT"].rearrange("(c p) n -> p c n", p=P)),
                    inc=S["ld"], k=16)
            LD = S["ld"].n
            em.do("vector", lambda v: v.memset(ones_col[:], 1.0))
            em.do("vector", lambda v: v.memset(ones_colb[:], 1.0))
            em.do("vector", lambda v: v.memset(ones_row[:], 1.0))
            em.do("vector", lambda v: v.memset(epsr[:], RMS_EPS), inc=S["consts"])

            for th in range(2):
                cols = slice(th * 512, (th + 1) * 512)
                for i in range(NH64):
                    gi = th * NH64 + i
                    par = gi % 2
                    for cpart, csl in ((0, slice(0, 8)), (1, slice(8, DCH))):
                        em.do("gpsimd", lambda g, i=i, par=par, csl=csl: g.dma_start(
                            out=wg[:, par, csl, :], in_=gu_src(i, False, csl)),
                            waits=([(S["gmm"], max(0, S["gmm"].n - 1))]
                                   if cpart == 0 else []),
                            inc=S["wl%d" % par], k=16)
                        em.do("gpsimd", lambda g, i=i, par=par, csl=csl: g.dma_start(
                            out=wu[:, par, csl, :], in_=gu_src(i, True, csl)),
                            inc=S["wl%d" % par], k=16)
                    def gmm(t, par=par, cols=cols):
                        last = None
                        for c in range(DCH):
                            last = t.matmul(gps[:], wg[:, par, c, :], xnT[:, c, cols],
                                            start=(c == 0), stop=(c == DCH - 1),
                                            skip_group_check=True)
                        return last
                    em.do("tensor", gmm,
                          waits=[(S["wl%d" % par], 64 * (gi // 2 + 1)), (S["ld"], LD),
                                 (S["sild"], max(0, gi))],
                          inc=S["gmm"])
                    def umm(t, par=par, cols=cols):
                        last = None
                        for c in range(DCH):
                            last = t.matmul(ups[:], wu[:, par, c, :], xnT[:, c, cols],
                                            start=(c == 0), stop=(c == DCH - 1),
                                            skip_group_check=True)
                        return last
                    em.do("tensor", umm,
                          waits=[(S["hmd"], max(0, gi))],
                          inc=S["gmm"])
                    em.do("scalar", lambda s: s.activation(sg[:], gps[:], AF.Silu),
                          waits=[(S["gmm"], S["gmm"].n - 1), (S["hmd"], max(0, gi))],
                          inc=S["sild"])
                    em.do("vector", lambda v: v.tensor_mul(ht[:], sg[:], ups[:]),
                          waits=[(S["sild"], S["sild"].n), (S["gmm"], S["gmm"].n),
                                 (S["sqd"], max(0, gi))],
                          inc=S["hmd"])
                    em.do("scalar", lambda s: s.activation(sq[:], ht[:], AF.Square),
                          waits=[(S["hmd"], S["hmd"].n),
                                 (S["sqmm"], S["sqmm"].n)],
                          inc=S["sqd"])
                    em.do("tensor", lambda t, i=i, th=th: t.matmul(
                        ssq[:], ones_colb[:], sq[:],
                        start=(i == 0), stop=(i == NH64 - 1), skip_group_check=True),
                        waits=[(S["sqd"], S["sqd"].n), (S["consts"], 1)] +
                              ([(S["rstd"], th)] if i == 0 else []),
                        inc=S["sqmm"])
                    em.do("vector", lambda v, i=i, th=th: v.tensor_copy(hbuf[:, i, :], ht[:]),
                          waits=[(S["ymm"], 32 * th)] if i == 0 else [],
                          inc=S["hcp"])
                em.do("scalar", lambda s: s.activation(
                    rrow[:], ssq[:], AF.Ln, bias=epsr[:], scale=1.0 / H),
                    waits=[(S["sqmm"], S["sqmm"].n)])
                em.do("scalar", lambda s: s.activation(
                    rrow[:], rrow[:], AF.Exp, scale=-0.5), inc=S["rstd"])
                em.do("tensor", lambda t: t.matmul(
                    rbps[:], ones_row[:], rrow[:], start=True, stop=True,
                    skip_group_check=True),
                    waits=[(S["rstd"], th + 1), (S["rbd"], th)], inc=S["rbcmm"])
                em.do("vector", lambda v: v.tensor_copy(rb[:], rbps[:]),
                      waits=[(S["rbcmm"], th + 1), (S["hnd"], 64 * th)], inc=S["rbd"])
                for i in range(NH64):
                    em.do("vector", lambda v, i=i: v.tensor_mul(
                        hbuf[:, i, :], hbuf[:, i, :], rb[:]),
                        waits=[(S["rbd"], th + 1)], inc=S["hnd"])
                for os_ in range(DCH):
                    oi = th * DCH + os_
                    ps = yps[oi % 2]
                    for half in range(2):
                        for cpart, csl in ((0, slice(0, 16)), (1, slice(16, 32))):
                            gsl = slice(half * 32 + csl.start, half * 32 + csl.stop)
                            em.do("gpsimd", lambda g, os_=os_, half=half,
                                  csl=csl, gsl=gsl: g.dma_start(
                                      out=wp[:, half, csl, :], in_=pr_src(os_, gsl)),
                                  waits=([(S["ymm"], max(0, 2 * oi + half - 1))]
                                         if cpart == 0 else []),
                                  inc=S["wpl"], k=16)
                    for half in range(2):
                        def ymm(t, ps=ps, half=half):
                            last = None
                            for c in range(32):
                                last = t.matmul(ps[:], wp[:, half, c, :],
                                                hbuf[:, half * 32 + c, :],
                                                start=(half == 0 and c == 0),
                                                stop=(half == 1 and c == 31),
                                                skip_group_check=True)
                            return last
                        em.do("tensor", ymm,
                              waits=[(S["wpl"], 64 * oi + 32 * (half + 1)),
                                     (S["hnd"], 64 * (th + 1))] +
                                    ([(S["resd"], max(0, oi - 1))] if half == 0 else []),
                              inc=S["ymm"])
                    em.do("vector", lambda v, os_=os_, cols=cols, ps=ps: v.tensor_add(
                        x1T[:, os_, cols], x1T[:, os_, cols], ps[:]),
                        waits=[(S["ymm"], S["ymm"].n)], inc=S["resd"])
            if "x2T" in dram:
                em.do("gpsimd", lambda g: g.dma_start(
                    out=dram["x2T"].rearrange("(c p) n -> p c n", p=P), in_=x1T[:]),
                    waits=[(S["resd"], 32)], inc=S["xout"], k=16)

        def done(S):
            return [(S["xout"], S["xout"].n), (S["resd"], 32),
                    (S["hcp"], S["hcp"].n), (S["hnd"], S["hnd"].n)]
        return walk, done


def build_l0ffn():
    nc = bass.Bass("TRN2", target_bir_lowering=False, debug=True,
                   detect_race_conditions=False)
    dram = {
        "x1T": nc.dram_tensor("x1T", [D, TPC], F32, kind="ExternalInput").ap(),
        "xnT": nc.dram_tensor("xnT", [D, TPC], BF16, kind="ExternalInput").ap(),
        "wguT": nc.dram_tensor("wguT", [D, 2 * H], BF16, kind="ExternalInput").ap(),
        "wprT": nc.dram_tensor("wprT", [H, D], BF16, kind="ExternalInput").ap(),
        "x2T": nc.dram_tensor("x2T", [D, TPC], F32, kind="ExternalOutput").ap(),
    }
    with ExitStack() as _ctx:
        walk, done = sec_l0ffn(nc, _ctx.enter_context, dram)
        block = _ctx.enter_context(nc.Block())
        sems = {n: Sem(nc.alloc_semaphore(n)) for n in FFN0_SEMS}

        def top(em):
            walk(em, sems)
            em.wait("gpsimd", done(sems))

        run_walk(nc, block, sems, top)
    return nc



DC_KV = 4
DR = 64


L1PRE_SEMS = LN_SEMS + ["ld", "consts", "wl0", "wl1", "pmm", "evac", "sqd",
                        "nmm", "ncp", "rmm", "roped", "trd", "tcp", "nadd", "out"]


def sec_l1pre(nc, E, dram, shared=None):
    upto = "F"
    wqd = dram["wq"]
    wkvd = dram["wkv"]
    wqrd = dram["wqr"]
    wkrd = dram["wkr"]
    cosd = dram["cos64"]
    sind = dram["sin64"]
    identd = dram["ident"]
    qTd = dram["qT"]
    ckvTd = dram["ckvT"]
    qrTd = dram["qrT"]
    krTd = dram["krT"]
    qn2d = dram["qn2"]
    if True:
        if shared is None:
            xT = E(nc.sbuf_tensor("xTs", [P, DCH, TPC], F32))
            xnT = E(nc.sbuf_tensor("xnTs", [P, DCH, TPC], BF16))
        else:
            xT = shared["xT"]
            xnT = shared["xnT"]
        qT = E(nc.sbuf_tensor("qTs", [P, DCH, TPC], BF16))
        ckvT = E(nc.sbuf_tensor("ckvTs", [P, DC_KV, TPC], BF16))
        qrT = E(nc.sbuf_tensor("qrTs", [DR, TPC], BF16))
        krT = E(nc.sbuf_tensor("krTs", [DR, TPC], BF16))
        wblk = E(nc.sbuf_tensor("wblks", [P, 2, DCH, P], BF16))
        wqr = E(nc.sbuf_tensor("wqrs", [P, DCH, DR], BF16))
        wkr = E(nc.sbuf_tensor("wkrs", [P, DC_KV, DR], BF16))
        cosb = E(nc.sbuf_tensor("cosbs", [P, TS, DR], F32))
        sinb = E(nc.sbuf_tensor("sinbs", [P, TS, DR], F32))
        ident = E(nc.sbuf_tensor("idents", [P, P], BF16))
        sqs = E(nc.sbuf_tensor("sqss", [P, 512], F32))
        shuf = E(nc.sbuf_tensor("shufs", [P, DR], F32))
        tmpA = E(nc.sbuf_tensor("tmpAs", [P, DR], F32))
        tmpB = E(nc.sbuf_tensor("tmpBs", [P, DR], F32))
        rtm = E(nc.sbuf_tensor("rtms", [P, 2, P], BF16))
        qn2row = E(nc.sbuf_tensor("qn2rows", [1, 2, 512], F32))
        qrn2 = E(nc.sbuf_tensor("qrn2s", [1, 2, 512], F32))
        lnaux = mk_ln_aux(nc, E)
        pps0 = E(nc.psum_tensor("pps0", [P, 512], F32))
        pps1 = E(nc.psum_tensor("pps1", [P, 512], F32))
        tps = E(nc.psum_tensor("tps", [P, P], BF16))
        xps = E(nc.psum_tensor("xps", [P, 512], F32))

        def walk(em, S):
            sub = upto
            lvl = "A12BCDEF".index(upto) - 2 if upto in "A12BCDEF" else 5
            pps = (pps0, pps1)
            nps = xps
            rps = xps
            if shared is None:
                em.do("gpsimd", lambda g: g.dma_start(
                    out=xT[:], in_=dram["x2T"].rearrange("(c p) n -> p c n", p=P)),
                    inc=S["ld"], k=16)
            if sub not in "46":
                em.do("gpsimd", lambda g: g.dma_start(
                    out=wqr[:], in_=wqrd.rearrange("(c p) n -> p c n", p=P)), inc=S["ld"], k=16)
                em.do("gpsimd", lambda g: g.dma_start(
                    out=wkr[:], in_=wkrd.rearrange("(c p) n -> p c n", p=P)), inc=S["ld"], k=16)
                em.do("gpsimd", lambda g: g.dma_start(
                    out=cosb[:], in_=cosd.rearrange("(s p) n -> p s n", p=P)), inc=S["ld"], k=16)
                em.do("gpsimd", lambda g: g.dma_start(
                    out=sinb[:], in_=sind.rearrange("(s p) n -> p s n", p=P)), inc=S["ld"], k=16)
                em.do("gpsimd", lambda g: g.dma_start(out=ident[:], in_=identd), inc=S["ld"], k=16)
            LDA = S["ld"].n
            em.do("vector", lambda v: v.memset(lnaux["ones_col"][:], 1.0))
            em.do("vector", lambda v: v.memset(lnaux["ones_colb"][:], 1.0))
            em.do("vector", lambda v: v.memset(lnaux["ones_row"][:], 1.0))
            em.do("vector", lambda v: v.memset(rtm[:], 0.0))
            em.do("vector", lambda v: v.memset(lnaux["eps_b"][:], 1e-5), inc=S["consts"])
            emit_ln(em, S, xT, xnT, lnaux,
                    pre_waits=[(S["ld"], LDA), (S["consts"], 1)])

            jobs = [("kv", i) for i in range(DC_KV)] if (lvl >= 1 or sub in "12") else []
            for j, (kind, i) in enumerate(jobs):
                par = j % 2
                wsrc = wqd if kind == "q" else wkvd
                for cpart, csl in ((0, slice(0, 8)), (1, slice(8, DCH))):
                    em.do("gpsimd", lambda g, i=i, par=par, wsrc=wsrc, csl=csl: g.dma_start(
                        out=wblk[:, par, csl, :],
                        in_=wsrc.rearrange("(c p) f -> p c f", p=P)[
                            :, csl, i * P:(i + 1) * P]),
                        waits=([(S["pmm"], max(0, 2 * (j - 1)))] if cpart == 0 else []),
                        inc=S["wl%d" % par], k=16)
                for th in range(2):
                    if sub == "1":
                        continue
                    cols = slice(th * 512, (th + 1) * 512)
                    jj = 2 * j + th
                    ps = pps[jj % 2]
                    def pmm(t, par=par, cols=cols, ps=ps):
                        last = None
                        for c in range(DCH):
                            last = t.matmul(ps[:], wblk[:, par, c, :], xnT[:, c, cols],
                                            start=(c == 0), stop=(c == DCH - 1),
                                            skip_group_check=True)
                        return last
                    em.do("tensor", pmm,
                          waits=[(S["wl%d" % par], 32 * (j // 2 + 1)), (S["ln_nrm"], 32),
                                 (S["evac"], max(0, jj - 1))],
                          inc=S["pmm"])
                    dst = qT if kind == "q" else ckvT
                    if sub != "2":
                        em.do("vector", lambda v, dst=dst, i=i, cols=cols, ps=ps: v.tensor_copy(
                            dst[:, i, cols], ps[:]),
                            waits=[(S["pmm"], S["pmm"].n)], inc=S["evac"])
                    if kind == "q":
                        em.do("scalar", lambda s, ps=ps: s.activation(sqs[:], ps[:], AF.Square),
                              waits=[(S["pmm"], S["pmm"].n),
                                     (S["nmm"], S["nmm"].n)],
                              inc=S["sqd"])
                        if sub != "9":
                            em.do("tensor", lambda t: t.matmul(
                                nps[0:1, :], lnaux["ones_col"][:], sqs[:],
                                start=True, stop=True, skip_group_check=True),
                                waits=[(S["sqd"], S["sqd"].n), (S["ncp"], S["ncp"].n)],
                                inc=S["nmm"])
                        em.do("vector", lambda v, i=i, cols=cols: v.tensor_copy(
                            qn2r[:, i, cols], nps[:]),
                            waits=[(S["nmm"], S["nmm"].n)], inc=S["ncp"])
            for kind in (("qr", "kr") if lvl >= 2 else ()):
                for ts in range(TS):
                    tsl = slice(ts * P, (ts + 1) * P)
                    def rmm(t, kind=kind, tsl=tsl):
                        last = None
                        if kind == "qr":
                            for c in range(DCH):
                                last = t.matmul(rps[:, 0:DR], xnT[:, c, tsl], wqr[:, c, :],
                                                start=(c == 0), stop=(c == DCH - 1),
                                                skip_group_check=True)
                        else:
                            for c in range(DC_KV):
                                last = t.matmul(rps[:, 0:DR], ckvT[:, c, tsl], wkr[:, c, :],
                                                start=(c == 0), stop=(c == DC_KV - 1),
                                                skip_group_check=True)
                        return last
                    ri = (0 if kind == "qr" else TS) + ts
                    em.do("tensor", rmm,
                          waits=[(S["ld"], LDA), (S["ln_nrm"], 32),
                                 (S["evac"], 8 if kind == "kr" else 0),
                                 (S["roped"], max(0, ri))],
                          inc=S["rmm"])
                    em.do("vector", lambda v: v.tensor_copy(
                        shuf[:, 0:32], rps[:, 32:64]),
                        waits=[(S["rmm"], S["rmm"].n), (S["trd"], max(0, ri - 1))])
                    em.do("vector", lambda v: v.tensor_copy(
                        shuf[:, 32:64], rps[:, 0:32]))
                    em.do("vector", lambda v, ts=ts: v.tensor_mul(
                        tmpA[:, 0:DR], rps[:, 0:DR], cosb[:, ts, :]))
                    em.do("vector", lambda v, ts=ts: v.tensor_mul(
                        tmpB[:, 0:DR], shuf[:, 0:DR], sinb[:, ts, :]))
                    em.do("vector", lambda v, ri=ri: v.tensor_add(
                        rtm[:, ri % 2, 0:DR], tmpA[:, 0:DR], tmpB[:, 0:DR]),
                        inc=S["roped"])
                    if lvl >= 3:
                        em.do("tensor", lambda t, ri=ri: t.transpose(
                            tps[:], rtm[:, ri % 2, :], ident[:]),
                            waits=[(S["roped"], S["roped"].n), (S["tcp"], max(0, ri))],
                            inc=S["trd"])
                        dstT = qrT if kind == "qr" else krT
                        em.do("vector", lambda v, dstT=dstT, tsl=tsl: v.tensor_copy(
                            dstT[0:DR, tsl], tps[0:DR, :]),
                            waits=[(S["trd"], S["trd"].n)], inc=S["tcp"])
                    else:
                        em.do("vector", lambda v, ri=ri: v.tensor_copy(
                            rtm[:, ri % 2, 0:1], rtm[:, ri % 2, 0:1]), inc=S["tcp"])
            for th in (range(2) if lvl >= 4 else ()):
                cols = slice(th * 512, (th + 1) * 512)
                em.do("scalar", lambda s, th=th, cols=cols: s.activation(
                    sqs[0:DR, 0:512], qrT[0:DR, cols], AF.Square),
                    waits=[(S["tcp"], 8), (S["nmm"], S["nmm"].n)], inc=S["sqd"])
                em.do("tensor", lambda t: t.matmul(
                    nps[0:1, :], lnaux["ones_col"][0:DR, :], sqs[0:DR, 0:512],
                    start=True, stop=True, skip_group_check=True),
                    waits=[(S["sqd"], S["sqd"].n), (S["ncp"], S["ncp"].n),
                           (S["roped"], 16)],
                    inc=S["nmm"])
                em.do("vector", lambda v, th=th: v.tensor_copy(
                    qrn2[:, th, :], nps[0:1, :]),
                    waits=[(S["nmm"], S["nmm"].n)], inc=S["ncp"])
            for i in (range(DCH) if lvl >= 5 else []):
                j = DC_KV + 8 + i
                par = i % 2
                for cpart, csl in ((0, slice(0, 8)), (1, slice(8, DCH))):
                    em.do("gpsimd", lambda g, i=i, par=par, csl=csl: g.dma_start(
                        out=wblk[:, par, csl, :],
                        in_=wqd.rearrange("(c p) f -> p c f", p=P)[
                            :, csl, i * P:(i + 1) * P]),
                        waits=([(S["pmm"], max(0, 2 * (DC_KV + i - 1)))]
                               if cpart == 0 else []),
                        inc=S["wl%d" % par], k=16)
                for th in range(2):
                    cols = slice(th * 512, (th + 1) * 512)
                    jj = 2 * (DC_KV + i) + th
                    ps = pps[jj % 2]
                    def pmm(t, par=par, cols=cols, ps=ps):
                        last = None
                        for c in range(DCH):
                            last = t.matmul(ps[:], wblk[:, par, c, :], xnT[:, c, cols],
                                            start=(c == 0), stop=(c == DCH - 1),
                                            skip_group_check=True)
                        return last
                    em.do("tensor", pmm,
                          waits=[(S["wl%d" % par], 32 * ((DC_KV + i) // 2 + 1)),
                                 (S["ln_nrm"], 32),
                                 (S["evac"], max(0, jj - 1))] +
                                ([(S["sqd"], max(0, S["sqd"].n - 1))] if sub != "8" else []),
                          inc=S["pmm"])
                    em.do("vector", lambda v, i=i, cols=cols, ps=ps: v.tensor_copy(
                        qT[:, i, cols], ps[:]),
                        waits=[(S["pmm"], S["pmm"].n)], inc=S["evac"])
                    if sub not in "8":
                        em.do("scalar", lambda s, i=i, cols=cols: s.activation(
                            sqs[:], qT[:, i, cols], AF.Square),
                              waits=[(S["evac"], S["evac"].n), (S["nmm"], S["nmm"].n)],
                              inc=S["sqd"])
                        if sub != "9":
                            em.do("tensor", lambda t: t.matmul(
                                nps[0:1, :], lnaux["ones_col"][:], sqs[:],
                                start=True, stop=True, skip_group_check=True),
                                waits=[(S["sqd"], S["sqd"].n), (S["ncp"], S["ncp"].n)],
                                inc=S["nmm"])
                        if sub != "9":
                            em.do("vector", lambda v, th=th: v.tensor_copy(
                                qn2row[:, th, :], nps[0:1, :]),
                                waits=[(S["nmm"], S["nmm"].n), (S["out"], S["out"].n)])
                            em.do("vector", lambda v, th=th: v.tensor_add(
                                qn2row[:, th, :], qn2row[:, th, :], qrn2[:, th, :]),
                                inc=S["ncp"])
                    if sub not in "789":
                        em.do("gpsimd", lambda g, i=i, th=th, cols=cols: g.dma_start(
                            out=qn2d[i:i + 1, cols], in_=qn2row[:, th, :]),
                            waits=[(S["ncp"], S["ncp"].n)], inc=S["out"], k=16)
            em.do("gpsimd", lambda g: g.dma_start(
                out=qTd.rearrange("(c p) n -> p c n", p=P), in_=qT[:]),
                waits=[(S["evac"], S["evac"].n)], inc=S["out"], k=16)
            em.do("gpsimd", lambda g: g.dma_start(
                out=ckvTd.rearrange("(c p) n -> p c n", p=P), in_=ckvT[:]),
                waits=[(S["evac"], S["evac"].n)], inc=S["out"], k=16)
            if sub not in "45":
                em.do("gpsimd", lambda g: g.dma_start(out=qrTd, in_=qrT[0:DR, :]),
                      waits=[(S["tcp"], S["tcp"].n)], inc=S["out"], k=16)
                em.do("gpsimd", lambda g: g.dma_start(out=krTd, in_=krT[0:DR, :]),
                      waits=[(S["tcp"], S["tcp"].n)], inc=S["out"], k=16)

        def done(S):
            return [(S["out"], S["out"].n)]
        return walk, done


def build_l1pre(upto="F"):
    nc = bass.Bass("TRN2", target_bir_lowering=False, debug=True,
                   detect_race_conditions=False)
    dram = {
        "x2T": nc.dram_tensor("x2T", [D, TPC], F32, kind="ExternalInput").ap(),
        "wq": nc.dram_tensor("wq", [D, D], F32, kind="ExternalInput").ap(),
        "wkv": nc.dram_tensor("wkv", [D, 512], F32, kind="ExternalInput").ap(),
        "wqr": nc.dram_tensor("wqr", [D, DR], F32, kind="ExternalInput").ap(),
        "wkr": nc.dram_tensor("wkr", [512, DR], F32, kind="ExternalInput").ap(),
        "cos64": nc.dram_tensor("cos64", [TPC, DR], F32, kind="ExternalInput").ap(),
        "sin64": nc.dram_tensor("sin64", [TPC, DR], F32, kind="ExternalInput").ap(),
        "ident": nc.dram_tensor("ident", [P, P], BF16, kind="ExternalInput").ap(),
        "qT": nc.dram_tensor("qT", [D, TPC], BF16, kind="ExternalOutput").ap(),
        "ckvT": nc.dram_tensor("ckvT", [512, TPC], BF16, kind="ExternalOutput").ap(),
        "qrT": nc.dram_tensor("qrT", [DR, TPC], BF16, kind="ExternalOutput").ap(),
        "krT": nc.dram_tensor("krT", [DR, TPC], BF16, kind="ExternalOutput").ap(),
        "qn2": nc.dram_tensor("qn2", [NH, TPC], F32, kind="ExternalOutput").ap(),
    }
    with ExitStack() as _ctx:
        walk, done = sec_l1pre(nc, _ctx.enter_context, dram)
        block = _ctx.enter_context(nc.Block())
        sems = {n: Sem(nc.alloc_semaphore(n)) for n in L1PRE_SEMS}

        def top(em):
            walk(em, sems)
            em.wait("gpsimd", done(sems))

        run_walk(nc, block, sems, top)
    return nc




ATTN1_SEMS = ATTN_SEMS + ["ld", "cks0", "cks1", "sout", "kvmm", "kvcp", "vcp",
                          "nsq", "nmm", "nred", "prek"]


def sec_l1attn(nc, E, dram, qn2_rows=None, ckv_src=None):
    if ckv_src is None:
        def ckv_src(st, csl):
            tsl = slice(st * 512, (st + 1) * 512)
            return dram["ckvT"].rearrange("(c p) t -> p c t", p=P)[:, csl, tsl]
    if True:
        tl = mk_attn_tiles(nc, E, L1=True)
        attnout = E(nc.sbuf_tensor("l1_attnout", [P, HPC, T], BF16))
        tl["attnout"] = attnout
        tl["qrT"] = E(nc.sbuf_tensor("l1_qrT", [DR, T], BF16))
        tl["krT"] = E(nc.sbuf_tensor("l1_krT", [DR, T], BF16))
        ckvs = E(nc.sbuf_tensor("l1_ckvs", [P, 2, DC_KV, 512], BF16))
        wkup = E(nc.sbuf_tensor("l1_wkup", [P, DC_KV, HPC * HD], BF16))
        wvup = E(nc.sbuf_tensor("l1_wvup", [P, DC_KV, HPC * HD], BF16))
        kmax = E(nc.sbuf_tensor("l1_kmax", [1, 1], F32))
        kmaxkr = E(nc.sbuf_tensor("l1_kmaxkr", [1, 1], F32))
        ones_colf = E(nc.sbuf_tensor("l1_onescf", [P, 1], F32))

        def walk(em, S):
            sps = tl["sps"]
            avps = tl["avps"]
            nrps = tl["lps"]
            sqn = tl["lbc"]
            def dma_in(out, in_):
                em.do("gpsimd", lambda g, out=out, in_=in_: g.dma_start(out=out, in_=in_),
                      inc=S["ld"], k=16)
            if isinstance(dram["qTh_src"], list):
                for s_, ap in enumerate(dram["qTh_src"]):
                    dma_in(tl["qTh"][:, :, s_ * TPC:(s_ + 1) * TPC], ap)
            else:
                dma_in(tl["qTh"][:], dram["qTh_src"])
            if isinstance(dram["qrT_src"], list):
                for s_, ap in enumerate(dram["qrT_src"]):
                    dma_in(tl["qrT"][0:DR, s_ * TPC:(s_ + 1) * TPC], ap)
                for s_, ap in enumerate(dram["krT_src"]):
                    dma_in(tl["krT"][0:DR, s_ * TPC:(s_ + 1) * TPC], ap)
            else:
                dma_in(tl["qrT"][0:DR, :], dram["qrT_src"])
                dma_in(tl["krT"][0:DR, :], dram["krT_src"])
            if qn2_rows is None:
                dma_in(tl["qn2"][:], dram["qn2_src"])
            else:
                for h_ in range(HPC):
                    for qt_ in range(16):
                        r = h_ * 16 + qt_
                        dma_in(tl["qn2"][r:r + 1, :], qn2_rows(h_, qt_))
            dma_in(tl["masks"][:], dram["masks"])
            dma_in(tl["eye32"][:], dram["eye32"])
            dma_in(wkup[:], dram["wkup"].rearrange("(c p) f -> p c f", p=P))
            dma_in(wvup[:], dram["wvup"].rearrange("(c p) f -> p c f", p=P))
            LDALL = S["ld"].n
            em.do("vector", lambda v: v.memset(tl["ones_colb"][:], 1.0))
            em.do("vector", lambda v: v.memset(tl["ones_row"][:], 1.0))
            em.do("vector", lambda v: v.memset(tl["ones_r32"][:], 1.0))
            em.do("vector", lambda v: v.memset(ones_colf[:], 1.0))
            em.do("vector", lambda v: v.memset(kmaxkr[:], 0.0))
            em.do("vector", lambda v: v.memset(tl["maxk2"][:], 0.0), inc=S["prek"])

            for st in range(16):
                tsl = slice(st * 512, (st + 1) * 512)
                spar = st % 2
                for cpart, csl in ((0, slice(0, 2)), (1, slice(2, DC_KV))):
                    em.do("gpsimd", lambda g, st=st, spar=spar, csl=csl: g.dma_start(
                        out=ckvs[:, spar, csl, :], in_=ckv_src(st, csl)),
                        waits=([(S["kvmm"], max(0, 6 * (st - 1)))] if cpart == 0 else []),
                        inc=S["cks%d" % spar], k=16)
                CKSN = S["cks%d" % spar].n
                for hs in range(HPC):
                    ji = st * HPC + hs
                    ps = sps[ji % 2]
                    def kmm(t, hs=hs, spar=spar, ps=ps):
                        last = None
                        for c in range(DC_KV):
                            last = t.matmul(ps[:], wkup[:, c, hs * HD:(hs + 1) * HD],
                                            ckvs[:, spar, c, :],
                                            start=(c == 0), stop=(c == DC_KV - 1),
                                            skip_group_check=True)
                        return last
                    em.do("tensor", kmm,
                          waits=[(S["ld"], LDALL), (S["cks%d" % spar], CKSN),
                                 (S["kvcp"], max(0, ji - 1)),
                                 (S["nsq"], max(0, ji - 1))],
                          inc=S["kvmm"])
                    em.do("vector", lambda v, hs=hs, tsl=tsl, ps=ps: v.tensor_copy(
                        tl["kT"][:, hs, tsl], ps[:]),
                        waits=[(S["kvmm"], S["kvmm"].n)], inc=S["kvcp"])
                    em.do("scalar", lambda s, hs=hs, tsl=tsl: s.activation(
                        sqn[:, 0, :], tl["kT"][:, hs, tsl], AF.Square),
                        waits=[(S["kvcp"], S["kvcp"].n), (S["nmm"], S["nmm"].n)],
                        inc=S["nsq"])
                    em.do("tensor", lambda t: t.matmul(
                        nrps[0:1, 0:512], ones_colf[:], sqn[:, 0, :],
                        start=True, stop=True, skip_group_check=True),
                        waits=[(S["nsq"], S["nsq"].n), (S["nred"], S["nred"].n),
                               (S["prek"], 1)],
                        inc=S["nmm"])
                    em.do("vector", lambda v: v.reduce_max(
                        kmax[:], nrps[0:1, 0:512], axis=AX.X),
                        waits=[(S["nmm"], S["nmm"].n)])
                    em.do("vector", lambda v: v.tensor_max(
                        tl["maxk2"][:], tl["maxk2"][:], kmax[:]), inc=S["nred"])
                for sub in range(4):
                    b = st * 4 + sub
                    bsl = slice(sub * P, (sub + 1) * P)
                    ps = avps[b % 2]
                    def vmm(t, spar=spar, bsl=bsl, ps=ps):
                        last = None
                        for c in range(DC_KV):
                            last = t.matmul(ps[:, 0:HPC * HD], ckvs[:, spar, c, bsl],
                                            wvup[:, c, :],
                                            start=(c == 0), stop=(c == DC_KV - 1),
                                            skip_group_check=True)
                        return last
                    em.do("tensor", vmm,
                          waits=[(S["vcp"], max(0, b - 1))],
                          inc=S["kvmm"])
                    em.do("vector", lambda v, b=b, ps=ps: v.tensor_copy(
                        tl["vt"][:, b, :], ps[:, 0:HPC * HD]),
                        waits=[(S["kvmm"], S["kvmm"].n)], inc=S["vcp"])
            for st in range(16):
                tsl = slice(st * 512, (st + 1) * 512)
                em.do("scalar", lambda s, tsl=tsl: s.activation(
                    sqn[0:DR, 0, :], tl["krT"][0:DR, tsl], AF.Square),
                    waits=[(S["ld"], LDALL), (S["nmm"], S["nmm"].n)], inc=S["nsq"])
                em.do("tensor", lambda t: t.matmul(
                    nrps[0:1, 0:512], ones_colf[0:DR, :], sqn[0:DR, 0, :],
                    start=True, stop=True, skip_group_check=True),
                    waits=[(S["nsq"], S["nsq"].n), (S["nred"], S["nred"].n)],
                    inc=S["nmm"])
                em.do("vector", lambda v: v.reduce_max(
                    kmax[:], nrps[0:1, 0:512], axis=AX.X),
                    waits=[(S["nmm"], S["nmm"].n)])
                em.do("vector", lambda v: v.tensor_max(
                    kmaxkr[:], kmaxkr[:], kmax[:]), inc=S["nred"])
            em.do("vector", lambda v: v.tensor_add(
                tl["maxk2"][:], tl["maxk2"][:], kmaxkr[:]),
                waits=[(S["nred"], 48)], inc=S["prek"])

            kv_ready = [(S["ld"], LDALL), (S["kvcp"], 32), (S["vcp"], 64),
                        (S["prek"], 2)]
            emit_attn_core(em, S, tl, L1=True, kv_ready_waits=kv_ready)
            if isinstance(dram["attnout_dst"], list):
                for s_, ap in enumerate(dram["attnout_dst"]):
                    em.do("gpsimd", lambda g, s_=s_, ap=ap: g.dma_start(
                        out=ap, in_=attnout[:, :, s_ * TPC:(s_ + 1) * TPC]),
                        waits=[(S["at_evacd"], HPC * 16)], inc=S["sout"], k=16)
            else:
                em.do("gpsimd", lambda g: g.dma_start(
                    out=dram["attnout_dst"], in_=attnout[:]),
                    waits=[(S["at_evacd"], HPC * 16)], inc=S["sout"], k=16)

        def done(S):
            return [(S["sout"], S["sout"].n)]
        return walk, done


def build_l1attn():
    nc = bass.Bass("TRN2", target_bir_lowering=False, debug=True,
                   detect_race_conditions=False)
    qThd = nc.dram_tensor("qTh", [HPC * HD, T], BF16, kind="ExternalInput").ap()
    qrTd = nc.dram_tensor("qrT", [DR, T], BF16, kind="ExternalInput").ap()
    ckvTd = nc.dram_tensor("ckvT", [512, T], BF16, kind="ExternalInput").ap()
    krTd = nc.dram_tensor("krT", [DR, T], BF16, kind="ExternalInput").ap()
    wkupd = nc.dram_tensor("wkup", [512, HPC * HD], F32, kind="ExternalInput").ap()
    wvupd = nc.dram_tensor("wvup", [512, HPC * HD], F32, kind="ExternalInput").ap()
    qn2d = nc.dram_tensor("qn2", [32, 512], F32, kind="ExternalInput").ap()
    maskd = nc.dram_tensor("masks", [P, 4, 512], F32, kind="ExternalInput").ap()
    eyed = nc.dram_tensor("eye32", [32, 32], F32, kind="ExternalInput").ap()
    outd = nc.dram_tensor("attnout", [HPC * HD, T], BF16, kind="ExternalOutput").ap()
    dram = {
        "qTh_src": qThd.rearrange("(h p) t -> p h t", p=P),
        "qrT_src": qrTd,
        "krT_src": krTd,
        "ckvT": ckvTd,
        "wkup": wkupd,
        "wvup": wvupd,
        "qn2_src": qn2d,
        "masks": maskd,
        "eye32": eyed,
        "attnout_dst": outd.rearrange("(h p) t -> p h t", p=P),
    }
    with ExitStack() as _ctx:
        walk, done = sec_l1attn(nc, _ctx.enter_context, dram)
        block = _ctx.enter_context(nc.Block())
        sems = {n: Sem(nc.alloc_semaphore(n)) for n in ATTN1_SEMS}

        def top(em):
            walk(em, sems)
            em.wait("gpsimd", done(sems))

        run_walk(nc, block, sems, top)
    return nc



KI = 8192
I_ = 1024
NE = 128


MOE_SEMS = LN_SEMS + ["ld", "consts", "vbar", "rmm", "rsf", "nmxd", "topd",
                      "trmm", "trcp", "amm", "acp", "mxd", "esd", "wl0", "wl1",
                      "gumm", "sild", "basd", "mxacc", "mtd", "mtcp", "sgl0",
                      "sgl1", "sgmm", "ssil", "shd", "wdl", "ymm", "resd", "xout"]


def sec_l1moe(nc, E, dram, shared=None, debug_taps=False, upto=7, skip_ln=False):
    xTd = dram.get("x3T")
    wrtd = dram["wrt"]
    wgated = dram["wgate"]
    wupd = dram["wup"]
    wdownd = dram["wdown"]
    amp0d = dram["amp0"]
    amp1d = dram["amp1"]
    esd = dram["es"]
    wsgd = dram["wsg"]
    wsud = dram["wsu"]
    wsdd = dram["wsd"]
    identd = dram["ident"]
    x4Td = dram["x4T"]
    if debug_taps:
        alphad = dram["alphaD"]
        probsd = dram["probsD"]
        m8d = dram["m8D"]
        mskd = dram["mskD"]
        wtbd = dram["wtbD"]
        mixsd = dram["mixsD"]
        ppd = dram["ppD"]
        amp0o = dram["amp0D"]
        mixedd = dram["mixedD"]
        mixTd = dram["mixTD"]

    if True:
        if shared is None:
            xT = E(nc.sbuf_tensor("m_xT", [P, DCH, TPC], F32))
            xnT = E(nc.sbuf_tensor("m_xnT", [P, DCH, TPC], BF16))
        else:
            xT = shared["xT"]
            xnT = shared["xnT"]
        mixed = E(nc.sbuf_tensor("m_mixed", [P, TS, I_], F32))
        mixT = E(nc.sbuf_tensor("m_mixT", [P, 8, TPC], BF16))
        alpha = E(nc.sbuf_tensor("m_alpha", [P, TS, 8], F32))
        wblk = E(nc.sbuf_tensor("m_wblk", [P, 2, DCH, 256], BF16))
        wblku = E(nc.sbuf_tensor("m_wblku", [P, 2, DCH, 256], BF16))
        wdwn = E(nc.sbuf_tensor("m_wdwn", [P, 2, 8, P], BF16))
        wsdb = E(nc.sbuf_tensor("m_wsdb", [P, 2, DCH, P], BF16))
        amp0 = E(nc.sbuf_tensor("m_amp0", [P, 8], F32))
        amp1 = E(nc.sbuf_tensor("m_amp1", [P, 8], F32))
        es = E(nc.sbuf_tensor("m_es", [1, NE], F32))
        esb = E(nc.sbuf_tensor("m_esb", [P, NE], F32))
        ident = E(nc.sbuf_tensor("m_ident", [P, P], BF16))
        probs = E(nc.sbuf_tensor("m_probs", [P, NE], F32))
        m8 = E(nc.sbuf_tensor("m_m8", [P, 8], F32))
        msk = E(nc.sbuf_tensor("m_msk", [P, NE], F32))
        nmax = E(nc.sbuf_tensor("m_nmax", [P, 1], F32))
        ssum = E(nc.sbuf_tensor("m_ssum", [P, 1], F32))
        stw = E(nc.sbuf_tensor("m_stw", [P, 1], F32))
        pp = E(nc.sbuf_tensor("m_pp", [P, 8], F32))
        mixSb = E(nc.sbuf_tensor("m_mixSb", [P, 8], BF16))
        wb16 = E(nc.sbuf_tensor("m_wb16", [P, NE], BF16))
        wtb = E(nc.sbuf_tensor("m_wtb", [P, NE], BF16))
        lnaux = mk_ln_aux(nc, E)
        rps = E(nc.psum_tensor("m_rps", [P, NE], F32))
        tps = E(nc.psum_tensor("m_tps", [P, P], BF16))
        gps = E(nc.psum_tensor("m_gps", [P, 512], F32))
        ups = E(nc.psum_tensor("m_ups", [P, 512], F32))

        def walk(em, S):
            lvl = upto
            yps = (lnaux["bcm"], lnaux["bcr"])
            bas = lnaux["tmpA"]
            mtb = wb16
            shT = mixed.ap().bitcast(BF16).rearrange("p a (b n) -> p (a b) n", b=2)
            wrt = wblk.ap().rearrange("p a c f -> p (a c) f")[:, 0:DCH, 0:NE] if False else None
            def dma_in(out, in_):
                em.do("gpsimd", lambda g, out=out, in_=in_: g.dma_start(out=out, in_=in_),
                      inc=S["ld"], k=16)
            if shared is None:
                dma_in(xT[:], xTd.rearrange("(c p) n -> p c n", p=P))
            dma_in(wblk[:, 0, :, 0:NE], wrtd.rearrange("(c p) n -> p c n", p=P))
            dma_in(amp0[0:NE, :], amp0d)
            dma_in(amp1[0:NE, :], amp1d)
            dma_in(es[:], esd)
            dma_in(ident[:], identd)
            LDA = S["ld"].n
            em.do("vector", lambda v: v.memset(lnaux["ones_col"][:], 1.0))
            em.do("vector", lambda v: v.memset(lnaux["ones_colb"][:], 1.0))
            em.do("vector", lambda v: v.memset(lnaux["ones_row"][:], 1.0))
            em.do("vector", lambda v: v.memset(lnaux["eps_b"][:], 1e-5), inc=S["consts"])
            for k in range(TS):
                em.do("vector", lambda v, k=k: v.memset(mixed[:, k, :], 0.0),
                      inc=(S["consts"] if k == TS - 1 else None))
            if not skip_ln:
                emit_ln(em, S, xT, xnT, lnaux,
                        pre_waits=[(S["ld"], LDA), (S["consts"], 2)])

            VB = S["vbar"]
            def vb(fn, waits=None, inc=None):
                em.do("vector", fn, waits=(waits or []) + [(VB, VB.n)],
                      inc=(inc if inc is not None else VB))
            vb(lambda v: v.tensor_mul(pp[0:NE, :], amp0[0:NE, :], amp0[0:NE, :]),
               waits=[(S["ld"], LDA)])
            vb(lambda v: v.tensor_mul(m8[0:NE, :], amp1[0:NE, :], amp1[0:NE, :]))
            vb(lambda v: v.tensor_add(pp[0:NE, :], pp[0:NE, :], m8[0:NE, :]))
            vb(lambda v: v.reduce_sum(ssum[0:NE, :], pp[0:NE, :], axis=AX.X))
            vb(lambda v: v.tensor_single_scalar(
                ssum[0:NE, :], ssum[0:NE, :], 1e-8, ALU.add))
            vb(lambda v: v.reciprocal(stw[0:NE, :], ssum[0:NE, :]))
            vb(lambda v: v.scalar_tensor_tensor(
                mixSb[0:NE, :], pp[0:NE, :], stw[0:NE, :], pp[0:NE, :],
                ALU.mult, ALU.bypass), inc=S["mxd"])
            em.do("tensor", lambda t: t.matmul(
                rps[:], lnaux["ones_row"][:], es[:],
                start=True, stop=True, skip_group_check=True),
                waits=[(S["ld"], LDA), (S["consts"], 2)], inc=S["rmm"])
            em.do("vector", lambda v: v.tensor_copy(esb[:], rps[:]),
                  waits=[(S["rmm"], 1)], inc=S["esd"])

            for ts in (range(TS) if lvl >= 2 else []):
                tsl = slice(ts * P, (ts + 1) * P)
                def rtmm(t, tsl=tsl):
                    last = None
                    for c in range(DCH):
                        last = t.matmul(rps[:], xnT[:, c, tsl], wblk[:, 0, c, 0:NE],
                                        start=(c == 0), stop=(c == DCH - 1),
                                        skip_group_check=True)
                    return last
                em.do("tensor", rtmm,
                      waits=[(S["ld"], LDA), (S["ln_nrm"], 32), (S["esd"], 1),
                             (S["rsf"], max(0, ts)), (S["acp"], max(0, ts))],
                      inc=S["rmm"])
                RN = S["rmm"].n
                vb(lambda v: v.tensor_copy(probs[:], rps[:]),
                   waits=[(S["rmm"], RN), (S["topd"], S["topd"].n)])
                vb(lambda v: v.reduce_max(nmax[:], probs[:], axis=AX.X))
                vb(lambda v: v.tensor_single_scalar(
                    nmax[:], nmax[:], -1.0, ALU.mult), inc=S["nmxd"])
                em.do("scalar", lambda s: s.activation(
                    probs[:], probs[:], AF.Exp, bias=nmax[:], accum_out=ssum[:]),
                    waits=[(S["rsf"], S["rsf"].n), (S["nmxd"], S["nmxd"].n)],
                    inc=S["rsf"])
                vb(lambda v: v.reciprocal(stw[:], ssum[:]),
                   waits=[(S["rsf"], S["rsf"].n)])
                vb(lambda v: v.scalar_tensor_tensor(
                    probs[:], probs[:], stw[:], probs[:], ALU.mult, ALU.bypass))
                vb(lambda v: v.max(out=m8[:], in_=probs[:]))
                vb(lambda v: v.scalar_tensor_tensor(
                    msk[:], probs[:], m8[:, 3:4], probs[:], ALU.is_ge, ALU.bypass))
                vb(lambda v: v.reduce_sum(ssum[:], m8[:, 0:4], axis=AX.X))
                vb(lambda v: v.tensor_single_scalar(
                    ssum[:], ssum[:], 1e-8, ALU.add))
                vb(lambda v: v.reciprocal(stw[:], ssum[:]))
                vb(lambda v: v.tensor_mul(msk[:], msk[:], probs[:]))
                vb(lambda v: v.scalar_tensor_tensor(
                    msk[:], msk[:], stw[:], msk[:], ALU.mult, ALU.bypass))
                vb(lambda v: v.tensor_mul(msk[:], msk[:], esb[:]))
                vb(lambda v: v.tensor_copy(wb16[:], msk[:]),
                   inc=S["topd"])
                if lvl < 3:
                    em.do("vector", lambda v, ts=ts: v.memset(alpha[:, ts, :], 0.125),
                          inc=S["acp"])
                    continue
                em.do("tensor", lambda t: t.transpose(tps[:], wb16[:], ident[:]),
                      waits=[(S["topd"], S["topd"].n), (S["trcp"], max(0, ts)),
                             (S["ld"], LDA)],
                      inc=S["trmm"])
                em.do("vector", lambda v: v.tensor_copy(wtb[:], tps[:]),
                      waits=[(S["trmm"], S["trmm"].n)], inc=S["trcp"])
                em.do("tensor", lambda t: t.matmul(
                    rps[:, 0:8], wtb[:], mixSb[0:NE, :],
                    start=True, stop=True, skip_group_check=True),
                    waits=[(S["trcp"], S["trcp"].n), (S["mxd"], 1)],
                    inc=S["amm"])
                em.do("vector", lambda v, ts=ts: v.tensor_copy(alpha[:, ts, :], rps[:, 0:8]),
                      waits=[(S["amm"], S["amm"].n)], inc=S["acp"])
                if debug_taps and ts == 0:
                    em.do("gpsimd", lambda g: g.dma_start(out=probsd, in_=probs[:]),
                          waits=[(S["acp"], 1)], inc=S["xout"], k=16)
                    em.do("gpsimd", lambda g: g.dma_start(out=m8d, in_=m8[:]),
                          waits=[(S["acp"], 1)], inc=S["xout"], k=16)
                    em.do("gpsimd", lambda g: g.dma_start(out=mskd, in_=msk[:]),
                          waits=[(S["acp"], 1)], inc=S["xout"], k=16)
                    em.do("gpsimd", lambda g: g.dma_start(out=wtbd, in_=wtb[:]),
                          waits=[(S["acp"], 1)], inc=S["xout"], k=16)
                    em.do("gpsimd", lambda g: g.dma_start(out=mixsd, in_=mixSb[0:P, :]),
                          waits=[(S["acp"], 1)], inc=S["xout"], k=16)
                    em.do("gpsimd", lambda g: g.dma_start(out=ppd, in_=pp[0:P, :]),
                          waits=[(S["acp"], 1)], inc=S["xout"], k=16)
                    em.do("gpsimd", lambda g: g.dma_start(out=amp0o, in_=amp0[0:P, :]),
                          waits=[(S["acp"], 1)], inc=S["xout"], k=16)

            for hs in (range(32) if lvl >= 4 else []):
                par = hs % 2
                k = hs // 4
                io = (hs % 4) * 256
                for cpart, csl in ((0, slice(0, 8)), (1, slice(8, DCH))):
                    em.do("gpsimd", lambda g, hs=hs, par=par, cpart=cpart, csl=csl: g.dma_start(
                        out=wblk[:, par, csl, :],
                        in_=wgated.rearrange("(c p) h -> p c h", p=P)[
                            :, csl, hs * 256:(hs + 1) * 256]),
                        waits=([(S["gumm"], max(0, 2 * TS * (hs - 1)))] if cpart == 0 else []) +
                              ([(S["acp"], TS)] if hs == 0 and cpart == 0 else []),
                        inc=S["wl0"], k=16)
                    em.do("gpsimd", lambda g, hs=hs, par=par, cpart=cpart, csl=csl: g.dma_start(
                        out=wblku[:, par, csl, :],
                        in_=wupd.rearrange("(c p) h -> p c h", p=P)[
                            :, csl, hs * 256:(hs + 1) * 256]),
                        inc=S["wl0"], k=16)
                for ts in range(TS):
                    tsl = slice(ts * P, (ts + 1) * P)
                    def gmm(t, par=par, tsl=tsl):
                        last = None
                        for c in range(DCH):
                            last = t.matmul(gps[:, 0:256], xnT[:, c, tsl], wblk[:, par, c, :],
                                            start=(c == 0), stop=(c == DCH - 1),
                                            skip_group_check=True)
                        return last
                    em.do("tensor", gmm,
                          waits=[(S["wl0"], 64 * (hs + 1)),
                                 (S["ln_nrm"], 32),
                                 (S["sild"], max(0, S["sild"].n))],
                          inc=S["gumm"])
                    def umm(t, par=par, tsl=tsl):
                        last = None
                        for c in range(DCH):
                            last = t.matmul(ups[:, 0:256], xnT[:, c, tsl], wblku[:, par, c, :],
                                            start=(c == 0), stop=(c == DCH - 1),
                                            skip_group_check=True)
                        return last
                    em.do("tensor", umm,
                          waits=[(S["basd"], max(0, S["basd"].n))],
                          inc=S["gumm"])
                    em.do("scalar", lambda s: s.activation(lnaux["sq"][:, 0, 0:256], gps[:, 0:256], AF.Silu),
                          waits=[(S["gumm"], S["gumm"].n - 1),
                                 (S["basd"], S["basd"].n)],
                          inc=S["sild"])
                    em.do("vector", lambda v: v.tensor_mul(bas[:, 0:256], lnaux["sq"][:, 0, 0:256], ups[:, 0:256]),
                          waits=[(S["sild"], S["sild"].n), (S["gumm"], S["gumm"].n)],
                          inc=S["basd"])
                    em.do("vector", lambda v, ts=ts, k=k, io=io: v.scalar_tensor_tensor(
                        mixed[:, ts, io:io + 256], bas[:, 0:256], alpha[:, ts, k:k + 1],
                        mixed[:, ts, io:io + 256], ALU.mult, ALU.add),
                        waits=[(S["acp"], TS)], inc=S["mxacc"])

            for ts in (range(TS) if lvl >= 5 else []):
                for ic in range(8):
                    mi = ts * 8 + ic
                    em.do("vector", lambda v, ts=ts, ic=ic: v.tensor_copy(
                        mtb[:, 0:P], mixed[:, ts, ic * P:(ic + 1) * P]),
                        waits=[(S["mxacc"], 256)], inc=S["mtd"])
                    em.do("tensor", lambda t: t.transpose(tps[:], mtb[:, 0:P], ident[:]),
                          waits=[(S["mtd"], S["mtd"].n), (S["mtcp"], max(0, mi))],
                          inc=S["trmm"])
                    em.do("vector", lambda v, ts=ts, ic=ic: v.tensor_copy(
                        mixT[:, ic, ts * P:(ts + 1) * P], tps[:]),
                        waits=[(S["trmm"], S["trmm"].n)], inc=S["mtcp"])

            for fs in (range(DCH) if lvl >= 6 else []):
                par = fs % 2
                for cpart, csl in ((0, slice(0, 8)), (1, slice(8, DCH))):
                    em.do("gpsimd", lambda g, fs=fs, par=par, csl=csl: g.dma_start(
                        out=wblk[:, par, csl, 0:P],
                        in_=wsgd.rearrange("(c p) h -> p c h", p=P)[
                            :, csl, fs * P:(fs + 1) * P]),
                        waits=(([(S["sgmm"], max(0, 4 * (fs - 1)))] if cpart == 0 else []) +
                               ([(S["gumm"], 512)] if fs <= 1 and cpart == 0 else [])),
                        inc=S["sgl0"], k=16)
                    em.do("gpsimd", lambda g, fs=fs, par=par, csl=csl: g.dma_start(
                        out=wblku[:, par, csl, 0:P],
                        in_=wsud.rearrange("(c p) h -> p c h", p=P)[
                            :, csl, fs * P:(fs + 1) * P]),
                        inc=S["sgl0"], k=16)
                for th in range(2):
                    cols = slice(th * 512, (th + 1) * 512)
                    def sgmm(t, par=par, cols=cols):
                        last = None
                        for c in range(DCH):
                            last = t.matmul(gps[:], wblk[:, par, c, 0:P], xnT[:, c, cols],
                                            start=(c == 0), stop=(c == DCH - 1),
                                            skip_group_check=True)
                        return last
                    em.do("tensor", sgmm,
                          waits=[(S["sgl0"], 64 * (fs + 1)),
                                 (S["gumm"], 512), (S["sild"], 256),
                                 (S["mtcp"], 64),
                                 (S["ssil"], max(0, S["ssil"].n))],
                          inc=S["sgmm"])
                    def summ(t, par=par, cols=cols):
                        last = None
                        for c in range(DCH):
                            last = t.matmul(ups[:], wblku[:, par, c, 0:P], xnT[:, c, cols],
                                            start=(c == 0), stop=(c == DCH - 1),
                                            skip_group_check=True)
                        return last
                    em.do("tensor", summ,
                          waits=[(S["shd"], max(0, S["shd"].n)), (S["basd"], 256)],
                          inc=S["sgmm"])
                    em.do("scalar", lambda s: s.activation(lnaux["sq"][:, 0, :], gps[:], AF.Silu),
                          waits=[(S["sgmm"], S["sgmm"].n - 1), (S["shd"], S["shd"].n),
                                 (S["basd"], 256)],
                          inc=S["ssil"])
                    em.do("vector", lambda v, fs=fs, cols=cols: v.tensor_mul(
                        shT[:, fs, cols], lnaux["sq"][:, 0, :], ups[:]),
                        waits=[(S["ssil"], S["ssil"].n), (S["sgmm"], S["sgmm"].n)],
                        inc=S["shd"])

            for os_ in (range(DCH) if lvl >= 7 else []):
                par = os_ % 2
                for cpart, csl in ((0, slice(0, 8)), (1, slice(8, DCH))):
                    em.do("gpsimd", lambda g, os_=os_, par=par, csl=csl: g.dma_start(
                        out=wsdb[:, par, csl, :],
                        in_=wsdd.rearrange("(c p) o -> p c o", p=P)[
                            :, csl, os_ * P:(os_ + 1) * P]),
                        waits=([(S["ymm"], max(0, 2 * (os_ - 1)))] if cpart == 0 else []),
                        inc=S["wdl"], k=16)
                em.do("gpsimd", lambda g, os_=os_, par=par: g.dma_start(
                    out=wdwn[:, par, :, :],
                    in_=wdownd.rearrange("(c p) o -> p c o", p=P)[:, :, os_ * P:(os_ + 1) * P]),
                    inc=S["wdl"], k=16)
                for th in range(2):
                    cols = slice(th * 512, (th + 1) * 512)
                    oi = os_ * 2 + th
                    ps = yps[oi % 2]
                    def ymm(t, os_=os_, cols=cols, ps=ps, par=par):
                        last = None
                        for c in range(8):
                            last = t.matmul(ps[:], wdwn[:, par, c, :],
                                            mixT[:, c, cols],
                                            start=(c == 0), stop=False,
                                            skip_group_check=True)
                        for c in range(DCH):
                            last = t.matmul(ps[:], wsdb[:, par, c, :], shT[:, c, cols],
                                            start=False, stop=(c == DCH - 1),
                                            skip_group_check=True)
                        return last
                    em.do("tensor", ymm,
                          waits=[(S["wdl"], 48 * (os_ + 1)), (S["ld"], LDA),
                                 (S["mtcp"], 64), (S["shd"], 32),
                                 (S["resd"], max(0, oi - 1))],
                          inc=S["ymm"])
                    em.do("vector", lambda v, os_=os_, cols=cols, ps=ps: v.tensor_add(
                        xT[:, os_, cols], xT[:, os_, cols], ps[:]),
                        waits=[(S["ymm"], S["ymm"].n)], inc=S["resd"])
            em.do("gpsimd", lambda g: g.dma_start(
                out=x4Td.rearrange("(c p) n -> p c n", p=P), in_=xT[:]),
                waits=[(S["resd"], 32 if lvl >= 7 else 0),
                       (S["ln_nrm"], 32)], inc=S["xout"], k=16)
            if debug_taps:
                em.do("gpsimd", lambda g: g.dma_start(out=alphad, in_=alpha[:]),
                      waits=[(S["acp"], 8)], inc=S["xout"], k=16)
                em.do("gpsimd", lambda g: g.dma_start(out=mixedd, in_=mixed[:]),
                      waits=[(S["mxacc"], 256), (S["sgmm"], 0 if lvl < 6 else 64)],
                      inc=S["xout"], k=16)
                em.do("gpsimd", lambda g: g.dma_start(out=mixTd, in_=mixT[:]),
                      waits=[(S["mtcp"], 64)], inc=S["xout"], k=16)

        def done(S):
            return [(S["xout"], S["xout"].n)]
        return walk, done


def build_l1moe(upto=7, debug_taps=False):
    nc = bass.Bass("TRN2", target_bir_lowering=False, debug=True,
                   detect_race_conditions=False)
    dram = {
        "x3T": nc.dram_tensor("x3T", [D, TPC], F32, kind="ExternalInput").ap(),
        "wrt": nc.dram_tensor("wrt", [D, NE], F32, kind="ExternalInput").ap(),
        "wgate": nc.dram_tensor("wgate", [D, KI], F32, kind="ExternalInput").ap(),
        "wup": nc.dram_tensor("wup", [D, KI], F32, kind="ExternalInput").ap(),
        "wdown": nc.dram_tensor("wdown", [I_, D], F32, kind="ExternalInput").ap(),
        "amp0": nc.dram_tensor("amp0", [NE, 8], F32, kind="ExternalInput").ap(),
        "amp1": nc.dram_tensor("amp1", [NE, 8], F32, kind="ExternalInput").ap(),
        "es": nc.dram_tensor("es", [1, NE], F32, kind="ExternalInput").ap(),
        "wsg": nc.dram_tensor("wsg", [D, D], F32, kind="ExternalInput").ap(),
        "wsu": nc.dram_tensor("wsu", [D, D], F32, kind="ExternalInput").ap(),
        "wsd": nc.dram_tensor("wsd", [D, D], F32, kind="ExternalInput").ap(),
        "ident": nc.dram_tensor("ident", [P, P], BF16, kind="ExternalInput").ap(),
        "x4T": nc.dram_tensor("x4T", [D, TPC], F32, kind="ExternalOutput").ap(),
    }
    if debug_taps:
        dram["alphaD"] = nc.dram_tensor("alphaD", [P, TS, 8], F32, kind="ExternalOutput").ap()
        dram["probsD"] = nc.dram_tensor("probsD", [P, NE], F32, kind="ExternalOutput").ap()
        dram["m8D"] = nc.dram_tensor("m8D", [P, 8], F32, kind="ExternalOutput").ap()
        dram["mskD"] = nc.dram_tensor("mskD", [P, NE], F32, kind="ExternalOutput").ap()
        dram["wtbD"] = nc.dram_tensor("wtbD", [P, NE], BF16, kind="ExternalOutput").ap()
        dram["mixsD"] = nc.dram_tensor("mixsD", [P, 8], BF16, kind="ExternalOutput").ap()
        dram["ppD"] = nc.dram_tensor("ppD", [P, 8], F32, kind="ExternalOutput").ap()
        dram["amp0D"] = nc.dram_tensor("amp0D", [P, 8], F32, kind="ExternalOutput").ap()
        dram["mixedD"] = nc.dram_tensor("mixedD", [P, TS, I_], F32, kind="ExternalOutput").ap()
        dram["mixTD"] = nc.dram_tensor("mixTD", [P, 8, TPC], BF16, kind="ExternalOutput").ap()
    with ExitStack() as _ctx:
        walk, done = sec_l1moe(nc, _ctx.enter_context, dram,
                               debug_taps=debug_taps, upto=upto)
        block = _ctx.enter_context(nc.Block())
        sems = {n: Sem(nc.alloc_semaphore(n)) for n in MOE_SEMS}

        def top(em):
            walk(em, sems)
            em.wait("gpsimd", done(sems))

        run_walk(nc, block, sems, top)
    return nc




def _mksems(nc, prefix, names):
    return {n: Sem(nc.alloc_semaphore(prefix + n)) for n in names}


class NcPfx:
    def __init__(self, nc, pfx):
        self._nc = nc
        self._pfx = pfx

    def sbuf_tensor(self, name, *a, **k):
        return self._nc.sbuf_tensor(self._pfx + name, *a, **k)

    def psum_tensor(self, name, *a, **k):
        return self._nc.psum_tensor(self._pfx + name, *a, **k)

    def __getattr__(self, n):
        return getattr(self._nc, n)


def build_g1():
    nc = bass.Bass("TRN2", target_bir_lowering=False, debug=True,
                   detect_race_conditions=False)
    dram = {
        "xT": nc.dram_tensor("xT", [D, TPC], F32, kind="ExternalInput").ap(),
        "wq": nc.dram_tensor("wq", [D, D], F32, kind="ExternalInput").ap(),
        "wkv": nc.dram_tensor("wkv", [D, 256], F32, kind="ExternalInput").ap(),
        "cos4": nc.dram_tensor("cos4", [TPC, 512], F32, kind="ExternalInput").ap(),
        "sin4": nc.dram_tensor("sin4", [TPC, 512], F32, kind="ExternalInput").ap(),
        "qrope": nc.dram_tensor("qrope", [TPC, D], BF16, kind="ExternalOutput").ap(),
        "kt": nc.dram_tensor("kt", [TPC, 128], BF16, kind="ExternalOutput").ap(),
        "vt": nc.dram_tensor("vt", [TPC, 128], BF16, kind="ExternalOutput").ap(),
        "qn2": nc.dram_tensor("qn2", [P, 8, 16], F32, kind="ExternalOutput").ap(),
        "kn2": nc.dram_tensor("kn2", [P, 8], F32, kind="ExternalOutput").ap(),
        "wgu": nc.dram_tensor("wgu", [D, D], F32, kind="ExternalInput").ap(),
        "wpr": nc.dram_tensor("wpr", [256, 4 * D], F32, kind="ExternalInput").ap(),
        "bcselT": nc.dram_tensor("bcselT", [P, 2], BF16, kind="ExternalInput").ap(),
        "bcself": nc.dram_tensor("bcself", [2, P], F32, kind="ExternalInput").ap(),
        "wguT": nc.dram_tensor("wguT", [D, D], BF16, kind="ExternalOutput").ap(),
        "wprT": nc.dram_tensor("wprT", [4 * D, 256], BF16, kind="ExternalOutput").ap(),
    }
    with ExitStack() as top:
        with ExitStack() as s1:
            walkA, doneA = sec_l0pre(NcPfx(nc, "a0"), s1.enter_context, dram)
        with ExitStack() as s2:
            walkB, doneB = sec_quant(NcPfx(nc, "b0"), s2.enter_context, dram)
        block = top.enter_context(nc.Block())
        semsA = _mksems(nc, "a_", L0PRE_SEMS)
        semsB = _mksems(nc, "b_", QUANT_SEMS)
        allsems = dict(semsA)
        allsems.update({"b_" + k: v for k, v in semsB.items()})

        def mega(em):
            walkA(em, semsA)
            em.set_pre(doneA(semsA))
            walkB(em, semsB)
            em.wait("gpsimd", doneA(semsA) + doneB(semsB))

        run_walk(nc, block, allsems, mega)
    return nc


def build_g3():
    nc = bass.Bass("TRN2", target_bir_lowering=False, debug=True,
                   detect_race_conditions=False)
    dram = {
        "xT": nc.dram_tensor("xT", [D, TPC], F32, kind="ExternalInput").ap(),
        "aoT": nc.dram_tensor("aoT", [D, TPC], BF16, kind="ExternalInput").ap(),
        "wout": nc.dram_tensor("wout", [D, D], F32, kind="ExternalInput").ap(),
        "wguT": nc.dram_tensor("wguT", [D, 2 * H], BF16, kind="ExternalInput").ap(),
        "wprT": nc.dram_tensor("wprT", [H, D], BF16, kind="ExternalInput").ap(),
        "wq": nc.dram_tensor("wq", [D, D], F32, kind="ExternalInput").ap(),
        "wkv": nc.dram_tensor("wkv", [D, 512], F32, kind="ExternalInput").ap(),
        "wqr": nc.dram_tensor("wqr", [D, DR], F32, kind="ExternalInput").ap(),
        "wkr": nc.dram_tensor("wkr", [512, DR], F32, kind="ExternalInput").ap(),
        "cos64": nc.dram_tensor("cos64", [TPC, DR], F32, kind="ExternalInput").ap(),
        "sin64": nc.dram_tensor("sin64", [TPC, DR], F32, kind="ExternalInput").ap(),
        "ident": nc.dram_tensor("ident", [P, P], BF16, kind="ExternalInput").ap(),
        "x2T": nc.dram_tensor("x2T", [D, TPC], F32, kind="ExternalOutput").ap(),
        "qT": nc.dram_tensor("qT", [D, TPC], BF16, kind="ExternalOutput").ap(),
        "ckvT": nc.dram_tensor("ckvT", [512, TPC], BF16, kind="ExternalOutput").ap(),
        "qrT": nc.dram_tensor("qrT", [DR, TPC], BF16, kind="ExternalOutput").ap(),
        "krT": nc.dram_tensor("krT", [DR, TPC], BF16, kind="ExternalOutput").ap(),
        "qn2": nc.dram_tensor("qn2", [16, TPC], F32, kind="ExternalOutput").ap(),
    }
    with ExitStack() as top:
        E = top.enter_context
        shared = {"xT": E(nc.sbuf_tensor("sh_xT", [P, DCH, TPC], F32)),
                  "xnT": E(nc.sbuf_tensor("sh_xnT", [P, DCH, TPC], BF16))}
        ffn_dram = {"wguT": dram["wguT"], "wprT": dram["wprT"], "x2T": dram["x2T"]}
        with ExitStack() as s1:
            walkA, doneA = sec_post(NcPfx(nc, "a0"), s1.enter_context, dram, shared=shared)
        with ExitStack() as s2:
            walkB, doneB = sec_l0ffn(NcPfx(nc, "b0"), s2.enter_context, ffn_dram, shared=shared)
        with ExitStack() as s3:
            walkC, doneC = sec_l1pre(NcPfx(nc, "c0"), s3.enter_context, dram, shared=shared)
        block = top.enter_context(nc.Block())
        semsA = _mksems(nc, "a_", POST_SEMS)
        semsB = _mksems(nc, "b_", FFN0_SEMS)
        semsC = _mksems(nc, "c_", L1PRE_SEMS)
        allsems = {}
        for pfx, sd in (("a_", semsA), ("b_", semsB), ("c_", semsC)):
            allsems.update({pfx + k: v for k, v in sd.items()})

        def mega(em):
            walkA(em, semsA)
            em.set_pre(doneA(semsA))
            walkB(em, semsB)
            em.set_pre(doneB(semsB))
            walkC(em, semsC)
            em.wait("gpsimd", doneB(semsB) + doneC(semsC))

        run_walk(nc, block, allsems, mega)
    return nc


def build_g5():
    nc = bass.Bass("TRN2", target_bir_lowering=False, debug=True,
                   detect_race_conditions=False)
    dram = {
        "xT": nc.dram_tensor("xT", [D, TPC], F32, kind="ExternalInput").ap(),
        "aoT": nc.dram_tensor("aoT", [D, TPC], BF16, kind="ExternalInput").ap(),
        "wout": nc.dram_tensor("wout", [D, D], F32, kind="ExternalInput").ap(),
        "wrt": nc.dram_tensor("wrt", [D, NE], F32, kind="ExternalInput").ap(),
        "wgate": nc.dram_tensor("wgate", [D, KI], F32, kind="ExternalInput").ap(),
        "wup": nc.dram_tensor("wup", [D, KI], F32, kind="ExternalInput").ap(),
        "wdown": nc.dram_tensor("wdown", [I_, D], F32, kind="ExternalInput").ap(),
        "amp0": nc.dram_tensor("amp0", [NE, 8], F32, kind="ExternalInput").ap(),
        "amp1": nc.dram_tensor("amp1", [NE, 8], F32, kind="ExternalInput").ap(),
        "es": nc.dram_tensor("es", [1, NE], F32, kind="ExternalInput").ap(),
        "wsg": nc.dram_tensor("wsg", [D, D], F32, kind="ExternalInput").ap(),
        "wsu": nc.dram_tensor("wsu", [D, D], F32, kind="ExternalInput").ap(),
        "wsd": nc.dram_tensor("wsd", [D, D], F32, kind="ExternalInput").ap(),
        "ident": nc.dram_tensor("ident", [P, P], BF16, kind="ExternalInput").ap(),
        "x4T": nc.dram_tensor("x4T", [D, TPC], F32, kind="ExternalOutput").ap(),
    }
    with ExitStack() as top:
        E = top.enter_context
        shared = {"xT": E(nc.sbuf_tensor("sh_xT", [P, DCH, TPC], F32)),
                  "xnT": E(nc.sbuf_tensor("sh_xnT", [P, DCH, TPC], BF16))}
        with ExitStack() as s1:
            walkA, doneA = sec_post(NcPfx(nc, "a0"), s1.enter_context, dram, shared=shared)
        with ExitStack() as s2:
            walkB, doneB = sec_l1moe(NcPfx(nc, "b0"), s2.enter_context, dram,
                                     shared=shared, skip_ln=True)
        block = top.enter_context(nc.Block())
        semsA = _mksems(nc, "a_", POST_SEMS)
        semsB = _mksems(nc, "b_", MOE_SEMS)
        semsB["ln_nrm"] = semsA["ln_nrm"]
        allsems = {"a_" + k: v for k, v in semsA.items()}
        allsems.update({"b_" + k: v for k, v in semsB.items() if k != "ln_nrm"})

        def mega(em):
            walkA(em, semsA)
            em.set_pre(doneA(semsA))
            walkB(em, semsB)
            em.wait("gpsimd", doneB(semsB))

        run_walk(nc, block, allsems, mega)
    return nc


ALU = mybir.AluOpType


def build_mega():
    nc = bass.Bass("TRN2", target_bir_lowering=False, debug=True,
                   detect_race_conditions=False)
    dI, dO = "ExternalInput", "ExternalOutput"

    def din(name, shape, dt=F32):
        return nc.dram_tensor(name, shape, dt, kind=dI).ap()

    def dint(name, shape, dt=BF16):
        return nc.dram_tensor(name, shape, dt).ap()

    ext = {
        "xT": din("xT", [D, TPC]), "wq0": din("wq0", [D, D]),
        "wkv0": din("wkv0", [D, 256]),
        "cos4": din("cos4", [TPC, 512]), "sin4": din("sin4", [TPC, 512]),
        "wgu": din("wgu", [D, D]), "wpr": din("wpr", [256, 4 * D]),
        "bcselT": din("bcselT", [P, 2], BF16), "bcself": din("bcself", [2, P]),
        "wout0": din("wout0", [D, D]),
        "wq1": din("wq1", [D, D]), "wkv1": din("wkv1", [D, 512]),
        "wqr": din("wqr", [D, DR]), "wkr": din("wkr", [512, DR]),
        "cos64": din("cos64", [TPC, DR]), "sin64": din("sin64", [TPC, DR]),
        "ident": din("ident", [P, P], BF16),
        "masks": din("masks", [P, 4, 512]), "eye32": din("eye32", [32, 32]),
        "wkup": din("wkup", [512, HPC * HD]), "wvup": din("wvup", [512, HPC * HD]),
        "wo1": din("wo1", [D, D]),
        "wrt": din("wrt", [D, NE]), "wgate": din("wgate", [D, KI]),
        "wup": din("wup", [D, KI]), "wdown": din("wdown", [I_, D]),
        "amp0": din("amp0", [NE, 8]), "amp1": din("amp1", [NE, 8]),
        "es": din("es", [1, NE]),
        "wsg": din("wsg", [D, D]), "wsu": din("wsu", [D, D]),
        "wsd": din("wsd", [D, D]),
    }
    x4Td = nc.dram_tensor("x4T", [D, TPC], F32, kind=dO).ap()

    qTd0 = dint("i_qT0", [D, TPC])
    ktd = dint("i_kt", [HD, TPC])
    vtd = dint("i_vt", [TPC, HD])
    qn2d0 = dint("i_qn20", [NH, TPC], F32)
    kn2d0 = dint("i_kn20", [TPC, 1], F32)
    wguTs = dint("i_wguTs", [D, D])
    wprTs = dint("i_wprTs", [4 * D, 256])
    qTa = dint("i_qTa", [D, TPC])
    kta = dint("i_kta", [NCORES * HD, TPC])
    vta = dint("i_vta", [T, HD])
    qn2a = dint("i_qn2a", [NH, TPC], F32)
    kn2a = dint("i_kn2a", [T, 1], F32)
    wguTg = dint("i_wguTg", [NCORES * D, D])
    wprTg = dint("i_wprTg", [NCORES * 4 * D, 256])
    aod = dint("i_aod", [D, TPC])
    aoa = dint("i_aoa", [D, TPC])
    x2Td = dint("i_x2T", [D, TPC], F32)
    qTd1 = dint("i_qT1", [D, TPC])
    ckvTd = dint("i_ckvT", [512, TPC])
    qrTd = dint("i_qrT", [DR, TPC])
    krTd = dint("i_krT", [DR, TPC])
    qn2d1 = dint("i_qn21", [NH, TPC], F32)
    ckvg = dint("i_ckvg", [NCORES * 512, TPC])
    qrg = dint("i_qrg", [NCORES * DR, TPC])
    krg = dint("i_krg", [NCORES * DR, TPC])
    qTa1 = dint("i_qTa1", [D, TPC])
    qn2a1 = dint("i_qn2a1", [NH, TPC], F32)
    ao1d = dint("i_ao1d", [D, TPC])
    ao1a = dint("i_ao1a", [D, TPC])

    dram_pre = {"xT": ext["xT"], "wq": ext["wq0"], "wkv": ext["wkv0"],
                "cos4": ext["cos4"], "sin4": ext["sin4"], "ident": ext["ident"],
                "qrope": qTd0, "kt": ktd, "vt": vtd, "qn2": qn2d0, "kn2": kn2d0}
    dram_q = {"wgu": ext["wgu"], "wpr": ext["wpr"], "bcselT": ext["bcselT"],
              "bcself": ext["bcself"], "wguT": wguTs, "wprT": wprTs}
    qTa4 = qTa.rearrange("(s hh p) t -> s p hh t", p=P, hh=HPC)
    kta4 = kta.rearrange("(s p) t -> s p t", p=P)
    kn2a4 = kn2a.rearrange("(s t) o -> s o t", s=NCORES)
    dram_at0 = {
        "qTh_src": [qTa4[s] for s in range(NCORES)],
        "kT_src": [kta4[s] for s in range(NCORES)],
        "vt_src": vta.rearrange("(b p) f -> p b f", p=P),
        "kn2_src": [kn2a4[s] for s in range(NCORES)],
        "masks": ext["masks"], "eye32": ext["eye32"],
        "attnout_dst": [aod.rearrange("(s h p) t -> s p h t", p=P, h=HPC)[s]
                        for s in range(NCORES)],
    }

    def qn2_rows0(h, qt):
        r = 2 * (qt // 2) + h
        c0 = (qt % 2) * 512
        return qn2a[r:r + 1, c0:c0 + 512]

    dram_post0 = {"xT": ext["xT"], "aoT": aoa, "wout": ext["wout0"]}
    dram_ffn = {"x2T": x2Td}

    def gu_src(i, up, csl):
        j0 = (H if up else 0) + i * P
        s = j0 // D
        cc0 = j0 % D
        return wguTg.rearrange("(s c p) h -> p s c h", p=P, c=DCH)[:, s, csl, cc0:cc0 + P]

    def pr_src(os_, gsl):
        s = os_ // 2
        cc0 = (os_ % 2) * P
        return wprTg.rearrange("(s c p) o -> p s c o", p=P, c=64)[:, s, gsl, cc0:cc0 + P]

    dram_l1pre = {"wq": ext["wq1"], "wkv": ext["wkv1"], "wqr": ext["wqr"],
                  "wkr": ext["wkr"], "cos64": ext["cos64"], "sin64": ext["sin64"],
                  "ident": ext["ident"], "qT": qTd1, "ckvT": ckvTd,
                  "qrT": qrTd, "krT": krTd, "qn2": qn2d1}
    qTa14 = qTa1.rearrange("(s hh p) t -> s p hh t", p=P, hh=HPC)
    qrg4 = qrg.rearrange("(s r) t -> s r t", r=DR)
    krg4 = krg.rearrange("(s r) t -> s r t", r=DR)
    dram_at1 = {
        "qTh_src": [qTa14[s] for s in range(NCORES)],
        "qrT_src": [qrg4[s] for s in range(NCORES)],
        "krT_src": [krg4[s] for s in range(NCORES)],
        "wkup": ext["wkup"], "wvup": ext["wvup"],
        "masks": ext["masks"], "eye32": ext["eye32"],
        "attnout_dst": [ao1d.rearrange("(s h p) t -> s p h t", p=P, h=HPC)[s]
                        for s in range(NCORES)],
    }

    def ckv_src(st, csl):
        return ckvg.rearrange("(s c p) t -> p s c t", p=P, c=DC_KV)[
            :, st // 2, csl, (st % 2) * 512:(st % 2) * 512 + 512]

    def qn2_rows1(h, qt):
        r = 2 * (qt // 2) + h
        c0 = (qt % 2) * 512
        return qn2a1[r:r + 1, c0:c0 + 512]

    dram_post1 = {"xT": x2Td, "aoT": ao1a, "wout": ext["wo1"]}
    dram_moe = {"wrt": ext["wrt"], "wgate": ext["wgate"], "wup": ext["wup"],
                "wdown": ext["wdown"], "amp0": ext["amp0"], "amp1": ext["amp1"],
                "es": ext["es"], "wsg": ext["wsg"], "wsu": ext["wsu"],
                "wsd": ext["wsd"], "ident": ext["ident"], "x4T": x4Td}

    with ExitStack() as top:
        with ExitStack() as s1:
            w_pre, d_pre = sec_l0pre(NcPfx(nc, "p0"), s1.enter_context,
                                     dram_pre, fused=True)
        with ExitStack() as s2:
            w_q, d_q = sec_quant(NcPfx(nc, "q0"), s2.enter_context, dram_q)
        with ExitStack() as s3:
            w_a0, d_a0 = sec_attn0(NcPfx(nc, "t0"), s3.enter_context, dram_at0,
                                   qn2_rows=qn2_rows0)
        with ExitStack() as s4:
            E4 = s4.enter_context
            shared1 = {"xT": E4(nc.sbuf_tensor("g3_xT", [P, DCH, TPC], F32)),
                       "xnT": E4(nc.sbuf_tensor("g3_xnT", [P, DCH, TPC], BF16))}
            with ExitStack() as s4a:
                w_p0, d_p0 = sec_post(NcPfx(nc, "u0"), s4a.enter_context,
                                      dram_post0, shared=shared1)
            with ExitStack() as s4b:
                w_f, d_f = sec_l0ffn(NcPfx(nc, "f0"), s4b.enter_context,
                                     dram_ffn, shared=shared1,
                                     gu_src=gu_src, pr_src=pr_src)
            with ExitStack() as s4c:
                w_l1p, d_l1p = sec_l1pre(NcPfx(nc, "v0"), s4c.enter_context,
                                         dram_l1pre, shared=shared1)
        with ExitStack() as s5:
            w_a1, d_a1 = sec_l1attn(NcPfx(nc, "w0"), s5.enter_context, dram_at1,
                                    qn2_rows=qn2_rows1, ckv_src=ckv_src)
        with ExitStack() as s6:
            E6 = s6.enter_context
            shared2 = {"xT": E6(nc.sbuf_tensor("g5_xT", [P, DCH, TPC], F32)),
                       "xnT": E6(nc.sbuf_tensor("g5_xnT", [P, DCH, TPC], BF16))}
            with ExitStack() as s6a:
                w_p1, d_p1 = sec_post(NcPfx(nc, "y0"), s6a.enter_context,
                                      dram_post1, shared=shared2)
            with ExitStack() as s6b:
                w_moe, d_moe = sec_l1moe(NcPfx(nc, "z0"), s6b.enter_context,
                                         dram_moe, shared=shared2, skip_ln=True)

        block = top.enter_context(nc.Block())
        pool = {"dma": [], "cmp": []}

        def pool_get(names, dma_names):
            used = {"dma": 0, "cmp": 0}
            out = {}
            for n in names:
                t = "dma" if n in dma_names else "cmp"
                i = used[t]
                used[t] += 1
                if i >= len(pool[t]):
                    pool[t].append(Sem(nc.alloc_semaphore("%s%02d" % (t, i))))
                out[n] = pool[t][i]
            return out

        DMA = {
            "p": {"ld", "wql0", "wql1", "qout0", "qout1", "kvout0", "kvout1", "nout"},
            "q": {"csel", "cl0", "cl1", "cst0", "cst1", "tl0", "tl1", "qout0", "qout1"},
            "t": {"ld", "sout"},
            "u": {"ld", "wl0", "wl1", "xout", "nout"},
            "f": {"ld", "wl0", "wl1", "wpl", "xout"},
            "v": {"ld", "wl0", "wl1", "out"},
            "w": {"ld", "cks0", "cks1", "sout"},
            "y": {"ld", "wl0", "wl1", "xout", "nout"},
            "z": {"ld", "wl0", "wl1", "sgl0", "sgl1", "wdl", "xout"},
        }
        NAMES = {"p": L0PRE_SEMS, "q": QUANT_SEMS, "t": ATTN0_SEMS,
                 "u": POST_SEMS, "f": FFN0_SEMS, "v": L1PRE_SEMS,
                 "w": ATTN1_SEMS, "y": POST_SEMS, "z": MOE_SEMS}
        phys = {pfx: pool_get(NAMES[pfx], DMA[pfx]) for pfx in
                ("p", "q", "t", "u", "f", "v", "w", "y", "z")}
        cc1 = Sem(nc.alloc_semaphore("cc1"))
        cc2 = Sem(nc.alloc_semaphore("cc2"))
        cc3 = Sem(nc.alloc_semaphore("cc3"))
        cc4 = Sem(nc.alloc_semaphore("cc4"))
        sems = {"cc1": cc1, "cc2": cc2, "cc3": cc3, "cc4": cc4}
        for i, s in enumerate(pool["dma"]):
            sems["dma%02d" % i] = s
        for i, s in enumerate(pool["cmp"]):
            sems["cmp%02d" % i] = s

        def views(pfx):
            return {n: SemView(s) for n, s in phys[pfx].items()}

        import concourse.aot_env as _aot_env

        def _cc(g, src, dst, kind):
            with _aot_env.allow_ici_all_to_all():
                return g.collective_compute(kind, ALU.bypass,
                                            replica_groups=[list(range(NCORES))],
                                            ins=[src], outs=[dst])

        def coll(em, src, dst, kind, waits, inc):
            em.do("gpsimd", lambda g, src=src, dst=dst, kind=kind:
                  _cc(g, src, dst, kind),
                  waits=waits, inc=inc, k=1)

        def mega(em):
            sP = views("p")
            w_pre(em, sP)
            em.set_pre(d_pre(sP))
            sQ = views("q")
            w_q(em, sQ)
            pre_done = d_pre(sP)
            q_done = d_q(sQ)
            coll(em, ktd, kta, "AllGather", pre_done, cc1)
            coll(em, vtd, vta, "AllGather", [], cc1)
            coll(em, kn2d0, kn2a, "AllGather", [], cc1)
            coll(em, qTd0, qTa, "AllToAll", [], cc1)
            coll(em, qn2d0, qn2a, "AllToAll", [], cc1)
            coll(em, wguTs, wguTg, "AllGather", q_done, cc1)
            coll(em, wprTs, wprTg, "AllGather", [], cc1)
            em.set_pre(pre_done + q_done + [(cc1, 5)])
            sA0 = views("t")
            w_a0(em, sA0)
            coll(em, aod, aoa, "AllToAll", d_a0(sA0), cc2)
            em.set_pre(d_a0(sA0) + [(cc2, 1)])
            sU0 = views("u")
            w_p0(em, sU0)
            em.set_pre(d_p0(sU0) + [(cc1, 7)])
            sF = views("f")
            w_f(em, sF)
            em.set_pre(d_f(sF))
            sV = views("v")
            w_l1p(em, sV)
            l1p_done = d_l1p(sV)
            coll(em, ckvTd, ckvg, "AllGather", l1p_done, cc3)
            coll(em, qrTd, qrg, "AllGather", [], cc3)
            coll(em, krTd, krg, "AllGather", [], cc3)
            coll(em, qTd1, qTa1, "AllToAll", [], cc3)
            coll(em, qn2d1, qn2a1, "AllToAll", [], cc3)
            em.set_pre(d_p0(sU0) + d_f(sF) + l1p_done + [(cc3, 5)])
            sW = views("w")
            w_a1(em, sW)
            coll(em, ao1d, ao1a, "AllToAll", d_a1(sW), cc4)
            em.set_pre(d_a1(sW) + [(cc4, 1)])
            sY = views("y")
            w_p1(em, sY)
            em.set_pre(d_p1(sY))
            sZ = views("z")
            sZ["ln_nrm"] = sY["ln_nrm"]
            w_moe(em, sZ)
            em.wait("gpsimd", d_moe(sZ))

        run_walk(nc, block, sems, mega)
    return nc


_BF = ml_dtypes.bfloat16
_CORES = list(range(NCORES))


def _mk_masks():
    masks = np.zeros((P, 4, 512), dtype=np.float32)
    for j in range(4):
        for p in range(P):
            masks[p, j][np.arange(512) < (128 * j + p)] = NEG
    return masks


def kernel(**inputs):
    f32 = np.float32
    x = np.ascontiguousarray(np.asarray(inputs["L0_attn_x"], dtype=f32)[0])
    masks = _mk_masks()
    eye32 = np.eye(32, dtype=f32)
    ident = np.eye(P, dtype=_BF)
    cos4, sin4 = host_rope_tables(T, 128, 4)
    cos64, sin64 = host_rope_tables(T, 64, 1)
    bcselT = np.zeros((P, 2), dtype=_BF); bcselT[0:64, 0] = 1; bcselT[64:128, 1] = 1
    bcself = np.zeros((2, P), dtype=f32); bcself[0, 0:64] = 1; bcself[1, 64:128] = 1
    wgu = inputs["L0_ffn_w_gate_up"]
    wpr = inputs["L0_ffn_w_proj"]
    amps = inputs["L1_ffn_amplitudes"]

    nc = build_mega()
    im = []
    for c in range(NCORES):
        sl = slice(c * TPC, (c + 1) * TPC)
        hs = slice(2 * c * HD, (2 * c + 2) * HD)
        im.append({
            "xT": np.ascontiguousarray(x[sl].T),
            "wq0": inputs["L0_attn_Wq"], "wkv0": inputs["L0_attn_Wkv"],
            "cos4": np.ascontiguousarray(cos4[sl]),
            "sin4": np.ascontiguousarray(sin4[sl]),
            "wgu": wgu[c * D:(c + 1) * D], "wpr": wpr[c * 256:(c + 1) * 256],
            "bcselT": bcselT, "bcself": bcself,
            "wout0": inputs["L0_attn_Wout"],
            "wq1": inputs["L1_attn_Wq"], "wkv1": inputs["L1_attn_Wkv"],
            "wqr": inputs["L1_attn_Wqr"], "wkr": inputs["L1_attn_Wkr"],
            "cos64": np.ascontiguousarray(cos64[sl]),
            "sin64": np.ascontiguousarray(sin64[sl]),
            "ident": ident, "masks": masks, "eye32": eye32,
            "wkup": np.ascontiguousarray(inputs["L1_attn_Wkup"][:, hs]),
            "wvup": np.ascontiguousarray(inputs["L1_attn_Wvup"][:, hs]),
            "wo1": inputs["L1_attn_Wo"],
            "wrt": inputs["L1_ffn_w_router"], "wgate": inputs["L1_ffn_w_gate"],
            "wup": inputs["L1_ffn_w_up"], "wdown": inputs["L1_ffn_w_down"],
            "amp0": np.ascontiguousarray(amps[:, :, 0]),
            "amp1": np.ascontiguousarray(amps[:, :, 1]),
            "es": np.ascontiguousarray(inputs["L1_ffn_expert_scale"].reshape(1, NE)),
            "wsg": inputs["L1_ffn_shared_w_gate"],
            "wsu": inputs["L1_ffn_shared_w_up"],
            "wsd": inputs["L1_ffn_shared_w_down"],
        })
    r = run_bass_kernel_spmd(nc, im, core_ids=_CORES).results

    out = np.empty((T, D), dtype=f32)
    for c in range(NCORES):
        out[c * TPC:(c + 1) * TPC] = np.asarray(r[c]["x4T"]).reshape(D, TPC).T
    return out.reshape(1, T, D)

